# version 11
# baseline (speedup 1.0000x reference)
import numpy as np
import ml_dtypes
import concourse.bass as bass
import concourse.mybir as mybir
from concourse.bass_utils import run_bass_kernel_spmd
from contextlib import ExitStack

BF16 = mybir.dt.bfloat16
F32 = mybir.dt.float32
AF = mybir.ActivationFunctionType
ALU = mybir.AluOpType
AX = mybir.AxisListType
BND = ml_dtypes.bfloat16

T, D, NH, HD, CH = 4096, 1024, 16, 64, 64
NT = T // 128
NSEG = 8
TPS = 4
DHID = 4096
P2_HEADS = None
P2_FENCE_HEADS = False
P2_DBG = None
P2_SEC = {}
P2_TSEC = 99
LH_LIM = None

def _tile_ocp(w):
    din, dout = w.shape
    dc, oc = din // 128, dout // 128
    return np.ascontiguousarray(
        w.reshape(dc, 128, oc, 128).transpose(2, 1, 0, 3).reshape(oc, 128, dc * 128))

def host_prepare(inputs):
    g = lambda n: np.ascontiguousarray(np.asarray(inputs[n], dtype=np.float32))
    cm = {}
    cm["w1t"] = _tile_ocp(g("L0_ffn_w1_w").T)
    cm["w2t"] = _tile_ocp(g("L0_ffn_w2_w").T)
    cm["h1t"] = _tile_ocp(g("L0_ffn_h1_w").T)
    cm["h2t"] = _tile_ocp(g("L0_ffn_h2_w").T)
    cm["w3t"] = _tile_ocp(g("L0_ffn_w3_w").T)
    cm["wot"] = _tile_ocp(g("L0_attn_Wo"))
    for nm, k in (("s1", "L0_ffn_w1_s"), ("s2", "L0_ffn_w2_s"),
                  ("sh1", "L0_ffn_h1_s"), ("sh2", "L0_ffn_h2_s"),
                  ("s3", "L0_ffn_w3_s"), ("nw", "L0_attn_norm_w")):
        v = g(k)
        cm[nm] = np.ascontiguousarray(v.reshape(-1, 128).T)
    qkvg = np.concatenate([g("L0_attn_Wq"), g("L0_attn_Wk"), g("L0_attn_Wv"),
                           g("L0_attn_Wgamma")], axis=1)
    cm["qkvg"] = np.ascontiguousarray(qkvg.reshape(8, 128, 3088))
    f1og1 = np.concatenate([g("L0_attn_Wf1"), g("L0_attn_Wog1")], axis=1)
    cm["f1og1"] = np.ascontiguousarray(f1og1.reshape(8, 128, 128))
    cm["wf2"] = g("L0_attn_Wf2")
    wog2 = g("L0_attn_Wog2")
    cm["wog2t"] = np.ascontiguousarray(wog2.reshape(64, 8, 128).transpose(1, 0, 2))
    cm["ident_b"] = np.eye(128).astype(BND)
    cm["ones_b"] = np.ones((128, 128), dtype=BND)
    cm["ones_f"] = np.ones((1, 128), dtype=np.float32)
    cm["actc"] = np.tile(np.array([[1e-5, 1.0]], np.float32), (128, 1))
    su = np.triu(np.ones((CH, CH), np.float32), 1)
    iu = np.triu(np.ones((CH, CH), np.float32), 0)
    sl = np.tril(np.ones((CH, CH), np.float32), -1)
    cm["maskct"] = np.ascontiguousarray(
        np.concatenate([np.concatenate([su, iu], 1)] * 2, 0)).astype(BND)
    cm["maskc"] = np.ascontiguousarray(np.concatenate([sl, sl], 0)).astype(BND)
    cm["lcum"] = np.ascontiguousarray(np.concatenate([iu] * 2, 0)).astype(BND)
    cm["lcum2"] = np.ascontiguousarray(
        np.concatenate([iu + np.eye(CH, dtype=np.float32)] * 2, 0)).astype(BND)
    i64d = np.concatenate([np.eye(64, dtype=np.float32)] * 2, 0)
    cm["i64d"] = i64d.astype(BND)
    xs = [np.ascontiguousarray(np.asarray(inputs["L0_attn_x"][b], np.float32).T)
          for b in range(8)]
    return cm, xs


class KB:
    ENGS = ("tensor", "vector", "scalar", "gpsimd", "sync")

    def __init__(self, nc, stack):
        self.nc = nc
        self.stack = stack
        self.streams = {e: [] for e in self.ENGS}
        self.sems = {}
        self.cnt = {}
        for s in ("T", "V", "A", "G", "F", "DS", "DT", "DG", "DH"):
            self.sems[s] = stack.enter_context(nc.semaphore("sem_" + s))
            self.cnt[s] = 0
        self.tens = {}
        self.waited = {s: set() for s in self.DMA_SEMS}

    def sb(self, name, shape, dtype, stack=None, side=None):
        kw = {"side": side} if side else {}
        t = (stack or self.stack).enter_context(self.nc.sbuf_tensor(name, shape, dtype, **kw))
        self.tens[name] = t
        return t

    def ps(self, name, shape=(128, 512), dtype=F32, stack=None):
        t = (stack or self.stack).enter_context(self.nc.psum_tensor(name, list(shape), dtype))
        self.tens[name] = t
        return t

    def fence(self):
        for eng in ("tensor", "vector", "scalar", "gpsimd"):
            self.emit(eng, lambda e: e.nop(), inc="F")
        snap = dict(self.cnt)
        waits = [(s, snap[s]) for s in ("F", "DS", "DT", "DG", "DH")]
        for eng in self.ENGS:
            self.wait(eng, waits)

    SELF_SEM = {"tensor": "T", "vector": "V", "scalar": "A", "gpsimd": "G"}
    DMA_SEMS = ("DS", "DT", "DG", "DH")

    def emit(self, eng, make, waits=(), inc=None, chain=False):
        v = None
        if eng in ("vector", "scalar", "gpsimd"):
            chain = True
        if chain:
            s = self.SELF_SEM[eng]
            waits = list(waits) + [(s, self.cnt[s])]
            if inc is None:
                inc = s
        if inc is not None:
            k = 16 if inc in self.DMA_SEMS else 1
            self.cnt[inc] += k
            v = self.cnt[inc]
        waits = [(s, w) for (s, w) in waits if w > 0]
        sems = self.sems
        kk = (16 if inc in self.DMA_SEMS else 1) if inc else 0
        def f(e, make=make, inc=inc, k=kk):
            for s, w in f.waits:
                e.wait_ge(sems[s], w)
            ins = make(e)
            if inc is not None:
                ins.then_inc(sems[inc], k)
        f.waits = waits
        f.inc_sem = inc
        f.inc_k = kk
        f.dma_sem = inc if inc in self.DMA_SEMS else None
        f.dma_count = v
        self.streams[eng].append(f)
        return v

    def wait(self, eng, waits):
        waits = [(s, w) for (s, w) in waits if w > 0]
        if not waits:
            return
        sems = self.sems
        def f(e):
            for s, w in f.waits:
                e.wait_ge(sems[s], w)
        f.waits = waits
        f.inc_sem = None
        f.inc_k = 0
        f.dma_sem = None
        f.dma_count = None
        self.streams[eng].append(f)

    def finalize(self):
        waited = {s: set() for s in self.DMA_SEMS}
        succ = {s: {} for s in self.DMA_SEMS}
        for eng in self.ENGS:
            for f in self.streams[eng]:
                for s, w in f.waits:
                    if s in self.DMA_SEMS:
                        waited[s].add(w)
                if f.dma_sem is not None:
                    succ[f.dma_sem][f.dma_count] = f
        for s, vals in waited.items():
            for v in vals:
                nf = succ[s].get(v + 16)
                if nf is not None and not any(
                        ss == s and ww >= v for ss, ww in nf.waits):
                    nf.waits = list(nf.waits) + [(s, v)]
        def run(e, fs):
            for f in fs:
                f(e)
        with self.nc.Block() as block:
            @block.tensor
            def _(e):
                run(e, self.streams["tensor"])
            @block.vector
            def _(e):
                run(e, self.streams["vector"])
            @block.scalar
            def _(e):
                run(e, self.streams["scalar"])
            @block.gpsimd
            def _(e):
                run(e, self.streams["gpsimd"])
            @block.sync
            def _(e):
                run(e, self.streams["sync"])


def declare_io(nc, expose=()):
    d = {}
    def di(name, shape, dtype, kind="ExternalInput"):
        if kind == "Internal" and name in expose:
            kind = "ExternalOutput"
        if kind == "Internal" and ("in:" + name) in expose:
            kind = "ExternalInput"
        d[name] = nc.dram_tensor(name, list(shape), dtype, kind=kind).ap()
    di("xT", (D, T), F32)
    di("qkvg", (8, 128, 3088), F32)
    di("f1og1", (8, 128, 128), F32)
    di("wf2", (64, D), F32)
    di("wog2t", (8, 64, 128), F32)
    di("w1t", (32, 128, 1024), F32)
    di("w2t", (32, 128, 1024), F32)
    di("h1t", (32, 128, 4096), F32)
    di("h2t", (32, 128, 4096), F32)
    di("w3t", (8, 128, 4096), F32)
    di("wot", (8, 128, 1024), F32)
    for nm, n in (("s1", 32), ("s2", 32), ("sh1", 32), ("sh2", 32), ("s3", 8), ("nw", 8)):
        di(nm, (128, n), F32)
    di("ident_b", (128, 128), BF16)
    di("ones_b", (128, 128), BF16)
    di("ones_f", (1, 128), F32)
    di("actc", (128, 2), F32)
    di("maskct", (128, 128), BF16)
    di("maskc", (128, 64), BF16)
    di("lcum", (128, 64), BF16)
    di("lcum2", (128, 64), BF16)
    di("i64d", (128, 64), BF16)
    di("y", (8, 128, T), F32, kind="ExternalOutput")
    di("h_dram", (32, 128, T), BF16, kind="Internal")
    di("g_dram", (32, 128, T), BF16, kind="Internal")
    di("o_dram", (8, 128, T), BF16, kind="Internal")
    if "dbg" in expose:
        di("dbg", (128, 4096), BF16, kind="ExternalOutput")
    if "xn" in expose:
        di("xn_dump", (128, 8 * T), BF16, kind="ExternalOutput")
    return d


def phase0_xn(kb, io, pstk, xn_stack=None):
    nc = kb.nc
    xn = kb.sb("xn_fm", [128, 8 * T], BF16, stack=xn_stack, side="right")
    ones = kb.sb("c_ones", [128, 128], BF16)
    onesf = kb.sb("c_onesf", [1, 128], F32)
    actc = kb.sb("c_actc", [128, 2], F32)
    identb = kb.sb("c_identb", [128, 128], BF16)
    maskct = kb.sb("c_maskct", [128, 128], BF16)
    maskc = kb.sb("c_maskc", [128, 64], BF16)
    lcum = kb.sb("c_lcum", [128, 64], BF16)
    lcum2 = kb.sb("c_lcum2", [128, 64], BF16)
    i64d = kb.sb("c_i64d", [128, 64], BF16)
    nwc = kb.sb("c_nw", [128, 8], F32)
    xf = kb.sb("p0_xf", [128, 2 * 8 * 512], F32, stack=pstk)
    xb = kb.sb("p0_xb", [128, 8 * 512], BF16, stack=pstk)
    x2b = kb.sb("p0_x2b", [128, 8 * 512], BF16, stack=pstk)
    st = kb.sb("p0_stats", [2, 6 * 512], F32, stack=pstk)
    psS = kb.ps("ps0_s", (2, 512), stack=pstk)
    psS2 = kb.ps("ps0_s2", (2, 512), stack=pstk)
    psR = kb.ps("ps0_r", (128, 512), stack=pstk)
    psM = kb.ps("ps0_m", (128, 512), stack=pstk)

    for nm, dst in (("ones_b", ones), ("ones_f", onesf), ("actc", actc), ("ident_b", identb),
                    ("maskct", maskct), ("maskc", maskc), ("lcum", lcum),
                    ("lcum2", lcum2), ("i64d", i64d), ("nw", nwc)):
        kb.emit("scalar", lambda e, dst=dst, nm=nm: e.dma_start(out=dst[:], in_=io[nm][:]), inc="DS")
    c_done = kb.cnt["DS"]

    ev_prev = 0
    tsum_prev = 0
    for tt in range(8):
        t0 = tt * 512
        pb = (tt % 2) * 8 * 512
        for dc in range(8):
            kb.emit("scalar", lambda e, dc=dc, t0=t0, pb=pb: e.dma_start(
                out=xf[:, pb + dc * 512: pb + (dc + 1) * 512],
                in_=io["xT"][dc * 128:(dc + 1) * 128, t0:t0 + 512]), inc="DS")
        ld = kb.cnt["DS"]
        for dc in range(8):
            vv = kb.emit("vector", lambda e, dc=dc, pb=pb: e.tensor_copy(
                xb[:, dc * 512:(dc + 1) * 512], xf[:, pb + dc * 512: pb + dc * 512 + 512]),
                waits=([("DS", ld), ("T", tsum_prev)] if dc == 0 else []), inc="V")
            kb.emit("scalar", lambda e, dc=dc: e.square(
                x2b[:, dc * 512:(dc + 1) * 512], xb[:, dc * 512:(dc + 1) * 512]),
                waits=[("V", vv)] + ([("T", tsum_prev)] if dc == 0 else []), inc="A")
        vcast = kb.cnt["V"]; acast = kb.cnt["A"]
        for dc in range(8):
            kb.emit("tensor", lambda e, dc=dc, first=(dc == 0): e.matmul(
                psS[0:1, :], ones[:, 0:1], xb[:, dc * 512:(dc + 1) * 512],
                start=first, stop=(dc == 7)),
                waits=([("V", vcast)] if dc == 0 else []))
            kb.emit("tensor", lambda e, dc=dc, first=(dc == 0): e.matmul(
                psS2[0:1, :], ones[:, 0:1], x2b[:, dc * 512:(dc + 1) * 512],
                start=first, stop=(dc == 7)),
                waits=([("A", acast)] if dc == 0 else []), inc=("T" if dc == 7 else None))
        tsum = kb.cnt["T"]
        tsum_prev = tsum
        MU, S2N, VAR, SQ, RSTD, NMR = (st[0:1, i * 512:(i + 1) * 512] for i in range(6))
        kb.emit("vector", lambda e: e.tensor_scalar(MU, psS[0:1, :], 1.0 / D, None, op0=ALU.mult),
                waits=[("T", tsum)], chain=True)
        kb.emit("vector", lambda e: e.tensor_scalar(S2N, psS2[0:1, :], 1.0 / D, None, op0=ALU.mult),
                chain=True)
        kb.emit("vector", lambda e: e.tensor_tensor(VAR, MU, MU, op=ALU.mult), chain=True)
        kb.emit("vector", lambda e: e.tensor_tensor(VAR, S2N, VAR, op=ALU.subtract), chain=True)
        vvar = kb.cnt["V"]
        kb.emit("scalar", lambda e: e.activation(SQ, VAR, AF.Sqrt, bias=actc[0:1, 0:1], scale=1.0),
                waits=[("V", vvar)], inc="A")
        asq = kb.cnt["A"]
        kb.emit("vector", lambda e: e.reciprocal(RSTD, SQ), waits=[("A", asq)], chain=True)
        vrstd = kb.cnt["V"]
        kb.emit("vector", lambda e: e.scalar_tensor_tensor(NMR, MU, -1.0, RSTD, op0=ALU.mult, op1=ALU.mult),
                chain=True)
        vnmr = kb.cnt["V"]
        kb.emit("tensor", lambda e: e.matmul(psR[:, :], onesf[0:1, :], RSTD, start=True, stop=True),
                waits=[("V", vrstd), ("V", ev_prev)])
        kb.emit("tensor", lambda e: e.matmul(psM[:, :], onesf[0:1, :], NMR, start=True, stop=True),
                waits=[("V", vnmr)], inc="T")
        trep = kb.cnt["T"]
        for dc in range(8):
            kb.emit("vector", lambda e, dc=dc, pb=pb: e.tensor_tensor(
                xb[:, dc * 512:(dc + 1) * 512],
                xf[:, pb + dc * 512: pb + dc * 512 + 512], psR[:, :], op=ALU.mult),
                waits=[("T", trep)] if dc == 0 else [], chain=True)
            kb.emit("vector", lambda e, dc=dc, t0=t0: e.tensor_tensor(
                xn[:, dc * T + t0: dc * T + t0 + 512],
                xb[:, dc * 512:(dc + 1) * 512], psM[:, :], op=ALU.add), chain=True)
        ev_prev = kb.cnt["V"]
    kb.wait("tensor", [("DS", c_done)])
    return dict(xn=xn, ones=ones, onesf=onesf, identb=identb, maskct=maskct,
                maskc=maskc, lcum=lcum, lcum2=lcum2, i64d=i64d, nwc=nwc,
                xn_ready=kb.cnt["V"])


def phase_l1(kb, io, env, pstk):
    xn = env["xn"]
    s1c = kb.sb("l1_s1", [128, 32], F32, stack=pstk)
    s2c = kb.sb("l1_s2", [128, 32], F32, stack=pstk)
    wbb = kb.sb("l1_wbb", [128, 2 * 2 * 1024], BF16, stack=pstk)
    hsb = kb.sb("l1_h", [128, 2 * 512], BF16, stack=pstk)
    tsl = kb.sb("l1_tsil", [128, 2 * 512], BF16, stack=pstk)
    psA = [kb.ps("psl1_a%d" % i, stack=pstk) for i in range(2)]
    psB = [kb.ps("psl1_b%d" % i, stack=pstk) for i in range(2)]
    kb.emit("scalar", lambda e: e.dma_start(out=s1c[:], in_=io["s1"][:]), inc="DS")
    kb.emit("scalar", lambda e: e.dma_start(out=s2c[:], in_=io["s2"][:]), inc="DS")
    sc_ds = kb.cnt["DS"]

    xn_ready = env["xn_ready"]
    ldq = []
    mm_after_panel = {0: 0, 1: 0}
    def load_panel(oc):
        b = (oc % 2)
        sm = "DG" if b == 0 else "DH"
        kb.emit("gpsimd", lambda e, oc=oc, b=b: e.dma_start(
            out=wbb[:, b * 2048: b * 2048 + 1024], in_=io["w1t"][oc]),
            waits=[("T", mm_after_panel[b])], inc=sm)
        v = kb.emit("gpsimd", lambda e, oc=oc, b=b: e.dma_start(
            out=wbb[:, b * 2048 + 1024: b * 2048 + 2048], in_=io["w2t"][oc]), inc=sm)
        ldq.append((sm, v))

    load_panel(0)
    load_panel(1)
    ev_tail = {0: 0, 1: 0}
    evA_tail = {0: 0, 1: 0}
    st_tail = {0: 0, 1: 0}
    for oc in range(32):
        wb = oc % 2
        for tt in range(8):
            b = tt % 2
            w0 = []
            if tt == 0:
                w0 = [ldq[oc], ("V", xn_ready)]
            for dc in range(8):
                kb.emit("tensor", lambda e, dc=dc, b=b, wb=wb, t0=tt * 512, first=(dc == 0): e.matmul(
                    psA[b][:, :], wbb[:, wb * 2048 + dc * 128: wb * 2048 + dc * 128 + 128],
                    xn[:, dc * T + t0: dc * T + t0 + 512], start=first, stop=(dc == 7)),
                    waits=(w0 if dc == 0 else []) + ([("A", evA_tail[b]), ("V", ev_tail[b])] if dc == 0 else []))
            for dc in range(8):
                kb.emit("tensor", lambda e, dc=dc, b=b, wb=wb, t0=tt * 512, first=(dc == 0): e.matmul(
                    psB[b][:, :], wbb[:, wb * 2048 + 1024 + dc * 128: wb * 2048 + 1024 + dc * 128 + 128],
                    xn[:, dc * T + t0: dc * T + t0 + 512], start=first, stop=(dc == 7)),
                    inc=("T" if dc == 7 else None))
            tmm = kb.cnt["T"]
            kb.emit("scalar", lambda e, b=b, oc=oc: e.activation(
                tsl[:, b * 512:(b + 1) * 512], psA[b][:, :], AF.Silu,
                scale=s1c[:, oc:oc + 1]),
                waits=[("T", tmm), ("V", ev_tail[b])]
                + ([("DS", sc_ds)] if oc == 0 and tt == 0 else []), inc="A")
            evA_tail[b] = kb.cnt["A"]
            kb.emit("vector", lambda e, b=b, oc=oc: e.scalar_tensor_tensor(
                hsb[:, b * 512:(b + 1) * 512], psB[b][:, :], s2c[:, oc:oc + 1],
                tsl[:, b * 512:(b + 1) * 512], op0=ALU.mult, op1=ALU.mult),
                waits=[("A", evA_tail[b]), ("DS", st_tail[b])]
                + ([("DS", sc_ds)] if oc == 0 and tt == 0 else []), inc="V")
            ev_tail[b] = kb.cnt["V"]
            kb.emit("scalar", lambda e, b=b, oc=oc, t0=tt * 512: e.dma_start(
                out=io["h_dram"][oc, :, t0:t0 + 512], in_=hsb[:, b * 512:(b + 1) * 512]),
                waits=[("V", ev_tail[b])], inc="DS")
            st_tail[b] = kb.cnt["DS"]
        mm_after_panel[wb] = kb.cnt["T"]
        if oc + 2 < 32:
            load_panel(oc + 2)
    env["l1_done_ds"] = kb.cnt["DS"]
    env["l1_done_T"] = kb.cnt["T"]


def phase_p2(kb, io, env, pstk, nseg=NSEG):
    xn = env["xn"]; identb = env["identb"]
    maskct = env["maskct"]; maskc = env["maskc"]
    lcum = env["lcum"]; lcum2 = env["lcum2"]; i64d = env["i64d"]
    L = lambda n, sh, dt: kb.sb(n, sh, dt, stack=pstk)
    qkvgw = L("p2_qkvgw", [128, 8 * 3088], BF16)
    f1og1w = L("p2_f1og1", [128, 8 * 128], BF16)
    wf2w = L("p2_wf2", [64, 1024], BF16)
    wog2w = L("p2_wog2", [64, 8 * 128], BF16)
    pack = L("p2_pack", [128, 2 * 4096], BF16)
    fmp = L("p2_fmp", [128, 2 * 4096], BF16)
    vtm2 = L("p2_vtm2", [128, 2 * 2048], BF16)
    ogp = L("p2_ogp", [128, 8 * 512], BF16)
    osg = L("p2_osg", [128, 8 * 512], BF16)
    z1f = L("p2_z1", [64, 512], BF16)
    og1f = L("p2_og1", [64, 512], BF16)
    states = L("p2_S", [128, 512], BF16)
    ctpt = L("p2_ctpt", [128, 128], BF16)
    cun = L("p2_cun", [128, 64], BF16)
    powsb = L("p2_pow", [128, 2 * 128], BF16)
    xw = L("p2_xw", [128, 2 * 128], BF16)
    ctptb = L("p2_ctptb", [128, 128], BF16)
    cunb = L("p2_cunb", [128, 64], BF16)
    powsbb = L("p2_powb", [128, 2 * 128], BF16)
    xwb = L("p2_xwb", [128, 2 * 128], BF16)
    sumb = L("p2_sumb", [128, 256], BF16)
    qtmp = L("p2_qtmp", [128, 1024], BF16)
    ktmp = L("p2_ktmp", [128, 1024], BF16)
    knk = L("p2_knk", [128, 1024], F32)
    kn = L("p2_kn", [128, 64], F32)
    emF = L("p2_emF", [128, 1024], BF16)
    eagF = L("p2_eagF", [128, 1024], BF16)
    gtmp = L("p2_gt", [128, 16], F32)
    gkb = L("p2_gkb", [128, 1024], BF16)
    scr1 = L("p2_scr1", [128, 512], BF16)
    scr2 = L("p2_scr2", [128, 512], BF16)
    pg4 = kb.ps("p2_pg4", (128, 1024), stack=pstk)
    pg4b = kb.ps("p2_pg4b", (128, 1024), stack=pstk)
    px2 = kb.ps("p2_px2", (128, 1024), stack=pstk)
    ps4 = kb.ps("p2_ps4", (128, 1024), stack=pstk)
    l1_s1 = L("p2l1_s1", [128, 32], F32)
    l1_s2 = L("p2l1_s2", [128, 32], F32)
    l1_wbb = L("p2l1_wbb", [128, 2 * 1024], BF16)
    l1_hsb = L("p2l1_h", [128, 512], BF16)
    l1_tsl = L("p2l1_t", [128, 512], BF16)

    for dc in range(8):
        kb.emit("gpsimd", lambda e, dc=dc: e.dma_start(
            out=qkvgw[:, dc * 3088:(dc + 1) * 3088], in_=io["qkvg"][dc]), inc="DG")
    kb.emit("gpsimd", lambda e: e.dma_start(
        out=f1og1w[:].rearrange("p (a b) -> p a b", b=128),
        in_=io["f1og1"].rearrange("a p b -> p a b")), inc="DG")
    kb.emit("gpsimd", lambda e: e.dma_start(out=wf2w[:], in_=io["wf2"][:]), inc="DG")
    kb.emit("gpsimd", lambda e: e.dma_start(
        out=wog2w[:].rearrange("p (a b) -> p a b", b=128),
        in_=io["wog2t"].rearrange("a p b -> p a b")), inc="DG")
    wready = kb.cnt["DG"]
    kb.emit("vector", lambda e: e.memset(states[:], 0.0))
    kb.emit("vector", lambda e: e.memset(ctpt[:], 0.0))
    if P2_HEADS is not None:
        kb.emit("vector", lambda e: e.memset(osg[:], 0.0))
    kb.emit("vector", lambda e: e.memset(cun[:], 0.0), inc="V")
    kb.wait("tensor", [("DG", wready), ("V", kb.cnt["V"])])
    kb.wait("scalar", [("DG", wready)])
    kb.wait("gpsimd", [("V", kb.cnt["V"])])

    fmp4 = fmp.rearrange("p (tb ti jc r) -> p tb ti jc r", tb=2, ti=4, jc=8, r=128)

    fr = dict(psG=[], psH=[], bA=[], bB=[], bC=[], bF=[],
              qtmp=[], ktmp=[], knk=[], gtmp=[], gkb=[], emF=[], eagF=[], scr=[],
              vtm=[[], []], pack=[[], []], fmpc=[[], []], og=[], osg=[], chain=[],
              ct=[], pow=[], xwg=[], sumbg=[])
    LN = [dict(bank=pg4, xoff=0, ct=ctpt, cu=cun, pw=powsb, xw=xw),
          dict(bank=pg4b, xoff=512, ct=ctptb, cu=cunb, pw=powsbb, xw=xwb)]
    frl = [dict(bA=[], bX=[], pow=[], sumev=[], psS=[], psO=[], chain=[],
                vmask=0, vpow=0, axw=0, tg=0, tx=0, tsq=0, taj=0, tch=0, gadd=0)
           for _ in range(2)]

    def W(*gs):
        out = []
        for g in gs:
            out += g
        return out

    kb.emit("scalar", lambda e: e.dma_start(out=l1_s1[:], in_=io["s1"][:]), inc="DS")
    kb.emit("scalar", lambda e: e.dma_start(out=l1_s2[:], in_=io["s2"][:]), inc="DS")
    l1_sc_ds = kb.cnt["DS"]
    l1_ldq = []
    l1_state = {"mm_after": 0, "st": 0, "ts": 0}

    def l1_load_panel(oc):
        sm = "DG" if oc % 2 == 0 else "DH"
        kb.emit("gpsimd", lambda e, oc=oc: e.dma_start(
            out=l1_wbb[:, 0:1024], in_=io["w1t"][oc]),
            waits=[("T", l1_state["mm_after"])], inc=sm)
        v = kb.emit("gpsimd", lambda e, oc=oc: e.dma_start(
            out=l1_wbb[:, 1024:2048], in_=io["w2t"][oc]), inc=sm)
        l1_ldq.append((sm, v))

    l1_load_panel(0)
    l1_u = [0]

    def l1_emit_unit():
        u = l1_u[0]
        if u >= 256:
            return
        l1_u[0] = u + 1
        oc, tt = u // 8, u % 8
        t0 = tt * 512
        w1 = ([l1_ldq[oc]] if tt == 0 else []) + W(fr["psG"])
        for dc in range(8):
            kb.emit("tensor", lambda e, dc=dc, t0=t0, first=(dc == 0): e.matmul(
                ps4[:, 0:512], l1_wbb[:, dc * 128: dc * 128 + 128],
                xn[:, dc * T + t0: dc * T + t0 + 512], start=first, stop=(dc == 7)),
                waits=(w1 if dc == 0 else []), inc=("T" if dc == 7 else None))
        t1 = kb.cnt["T"]
        for dc in range(8):
            kb.emit("tensor", lambda e, dc=dc, t0=t0, first=(dc == 0): e.matmul(
                ps4[:, 512:1024], l1_wbb[:, 1024 + dc * 128: 1024 + dc * 128 + 128],
                xn[:, dc * T + t0: dc * T + t0 + 512], start=first, stop=(dc == 7)),
                waits=(W(fr["psH"]) if dc == 0 else []), inc=("T" if dc == 7 else None))
        t2 = kb.cnt["T"]
        kb.emit("scalar", lambda e, oc=oc: e.activation(
            l1_tsl[:], ps4[:, 0:512], AF.Silu, scale=l1_s1[:, oc:oc + 1]),
            waits=[("T", t1), ("V", l1_state["ts"])]
            + ([("DS", l1_sc_ds)] if u == 0 else []), inc="A")
        a1 = kb.cnt["A"]
        fr["psG"] = [("A", a1)]
        kb.emit("vector", lambda e, oc=oc: e.scalar_tensor_tensor(
            l1_hsb[:], ps4[:, 512:1024], l1_s2[:, oc:oc + 1],
            l1_tsl[:], op0=ALU.mult, op1=ALU.mult),
            waits=[("T", t2), ("A", a1), ("DS", l1_state["st"])]
            + ([("DS", l1_sc_ds)] if u == 0 else []), inc="V")
        v1 = kb.cnt["V"]
        fr["psH"] = [("V", v1)]
        l1_state["ts"] = v1
        kb.emit("scalar", lambda e, oc=oc, t0=t0: e.dma_start(
            out=io["h_dram"][oc, :, t0:t0 + 512], in_=l1_hsb[:]),
            waits=[("V", v1)], inc="DS")
        l1_state["st"] = kb.cnt["DS"]
        if tt == 7:
            l1_state["mm_after"] = kb.cnt["T"]
            if oc + 1 < 32:
                l1_load_panel(oc + 1)

    def projgrp(tk, lo, hi, bank, bguard, waits0=()):
        t0 = tk * 128
        for dc in range(8):
            w = list(waits0) + W(bguard) if dc == 0 else []
            kb.emit("tensor", lambda e, dc=dc, t0=t0, lo=lo, hi=hi, bank=bank,
                    first=(dc == 0): e.matmul(
                bank[:, 0:hi - lo], xn[:, dc * T + t0:dc * T + t0 + 128],
                qkvgw[:, dc * 3088 + lo:dc * 3088 + hi],
                start=first, stop=(dc == 7)),
                waits=w, inc=("T" if dc == 7 else None))
        return kb.cnt["T"]

    bG = lambda: ps4[:, 0:512]
    bH = lambda: ps4[:, 512:1024]
    bA = lambda: pg4[:, 0:512]
    bB = lambda: pg4[:, 512:1024]
    bC = lambda: px2[:, 0:512]
    bF = lambda: px2[:, 512:1024]

    for s in range(nseg):
        s0 = s * 512
        for dc in range(8):
            kb.emit("tensor", lambda e, dc=dc, s0=s0, first=(dc == 0): e.matmul(
                bG()[:, 0:512], f1og1w[:, dc * 128:(dc + 1) * 128],
                xn[:, dc * T + s0:dc * T + s0 + 512], start=first, stop=(dc == 7)),
                waits=W(fr["psG"]) if dc == 0 else [], inc=("T" if dc == 7 else None))
        tz = kb.cnt["T"]
        kb.emit("scalar", lambda e: e.copy(z1f[:], bG()[0:64, 0:512]), waits=[("T", tz)])
        kb.emit("scalar", lambda e: e.copy(og1f[:], bG()[64:128, 0:512]), inc="A")
        fr["psG"] = [("A", kb.cnt["A"])]
        az = kb.cnt["A"]
        for oc in range(8):
            w0 = ([("A", az)] + W(fr["psH"])) if oc == 0 else [("A", kb.cnt["A"])]
            kb.emit("tensor", lambda e, oc=oc: e.matmul(
                bH()[:, 0:512], wog2w[:, oc * 128:(oc + 1) * 128], og1f[:],
                start=True, stop=True), waits=w0, inc="T")
            tog = kb.cnt["T"]
            kb.emit("scalar", lambda e, oc=oc, tog=tog: e.activation(
                ogp[:, oc * 512:(oc + 1) * 512], bH()[:, 0:512], AF.Sigmoid),
                waits=[("T", tog)] + W(fr["og"]), inc="A")
        fr["psH"] = [("A", kb.cnt["A"])]

        for tk4 in range(TPS):
            tk = s * TPS + tk4
            tb = tk % 2
            t0 = tk * 128
            if P2_TSEC < 1:
                continue
            tq = projgrp(tk, 0, 512, bG(), fr["psG"])
            kb.emit("scalar", lambda e, tq=tq: e.activation(
                qtmp[:, 0:512], bG()[:, 0:512], AF.Silu), waits=[("T", tq)] + W(fr["qtmp"]), inc="A")
            fr["psG"] = [("A", kb.cnt["A"])]
            tq = projgrp(tk, 512, 1024, bH(), fr["psH"])
            kb.emit("scalar", lambda e, tq=tq: e.activation(
                qtmp[:, 512:1024], bH()[:, 0:512], AF.Silu), waits=[("T", tq)], inc="A")
            fr["psH"] = [("A", kb.cnt["A"])]
            tq = projgrp(tk, 1024, 1536, bG(), fr["psG"])
            kb.emit("scalar", lambda e, tq=tq: e.activation(
                ktmp[:, 0:512], bG()[:, 0:512], AF.Silu), waits=[("T", tq)] + W(fr["ktmp"]), inc="A")
            fr["psG"] = [("A", kb.cnt["A"])]
            tq = projgrp(tk, 1536, 2048, bH(), fr["psH"])
            kb.emit("scalar", lambda e, tq=tq: e.activation(
                ktmp[:, 512:1024], bH()[:, 0:512], AF.Silu), waits=[("T", tq)], inc="A")
            fr["psH"] = [("A", kb.cnt["A"])]
            kready = kb.cnt["A"]
            for half in range(2):
                tq = projgrp(tk, 2048 + half * 512, 2560 + half * 512,
                             bG() if half == 0 else bH(),
                             fr["psG"] if half == 0 else fr["psH"])
                base = half * 1024
                kb.emit("vector", lambda e, tq=tq, base=base, half=half, tb=tb: e.tensor_copy(
                    vtm2[:, tb * 2048:(tb + 1) * 2048].rearrange(
                        "p (q c) -> p q c", c=64)[:, (base // 64 + 1)::4, :][:, 0:4, :],
                    (bG() if half == 0 else bH())[:, 0:512].rearrange(
                        "p (a b) -> p a b", b=64)[:, 0::2, :]),
                    waits=[("T", tq)] + (W(fr["vtm"][tb]) if half == 0 else []), inc="V")
                kb.emit("vector", lambda e, base=base, half=half, tb=tb: e.tensor_copy(
                    vtm2[:, tb * 2048:(tb + 1) * 2048].rearrange(
                        "p (q c) -> p q c", c=64)[:, (base // 64 + 2)::4, :][:, 0:4, :],
                    (bG() if half == 0 else bH())[:, 0:512].rearrange(
                        "p (a b) -> p a b", b=64)[:, 1::2, :]), inc="V")
                if half == 0:
                    fr["psG"] = [("V", kb.cnt["V"])]
                else:
                    fr["psH"] = [("V", kb.cnt["V"])]
            tq = projgrp(tk, 3072, 3088, bG(), fr["psG"])
            kb.emit("scalar", lambda e, tq=tq: e.activation(
                gtmp[:, 0:16], bG()[:, 0:16], AF.Sigmoid),
                waits=[("T", tq)] + W(fr["gtmp"]), inc="A")
            fr["psG"] = [("A", kb.cnt["A"])]
            if P2_TSEC < 2:
                continue
            for hf in range(2):
                bank, bn = (bH(), "psH") if hf == 0 else (bG(), "psG")
                kb.emit("tensor", lambda e, hf=hf, tk4=tk4, bank=bank: e.matmul(
                    bank[:, 0:512], z1f[:, tk4 * 128:(tk4 + 1) * 128],
                    wf2w[:, hf * 512:(hf + 1) * 512], start=True, stop=True),
                    waits=W(fr[bn], fr["scr"], fr["gkb"]), inc="T")
                tzz = kb.cnt["T"]
                kb.emit("scalar", lambda e, bank=bank: e.activation(scr1[:], bank[:, 0:512], AF.Abs),
                        waits=[("T", tzz)])
                aabs = kb.cnt["A"]
                kb.emit("scalar", lambda e: e.activation(scr1[:], scr1[:], AF.Exp, scale=-1.0))
                kb.emit("scalar", lambda e: e.activation(
                    scr1[:], scr1[:], AF.Ln, bias=kb.tens["c_actc"][:, 1:2]), inc="A")
                aln = kb.cnt["A"]
                kb.emit("vector", lambda e, bank=bank: e.tensor_scalar(
                    scr2[:], bank[:, 0:512], 0.0, None, op0=ALU.min),
                    waits=[("T", tzz), ("A", aabs)])
                kb.emit("vector", lambda e, hf=hf: e.tensor_tensor(
                    gkb[:, hf * 512:(hf + 1) * 512], scr2[:], scr1[:], op=ALU.subtract),
                    waits=[("A", aln)], inc="V")
                fr[bn] = [("V", kb.cnt["V"])]
                fr["scr"] = [("V", kb.cnt["V"])]
            vgk = kb.cnt["V"]
            if P2_TSEC < 3:
                continue
            for hf in range(2):
                for ch in (0, 64):
                    kb.emit("tensor", lambda e, ch=ch, hf=hf: e.matmul(
                        bG()[ch:ch + 64, 0:512], lcum[ch:ch + 64, :],
                        gkb[ch:ch + 64, hf * 512:(hf + 1) * 512],
                        start=True, stop=True, tile_position=(ch, ch), skip_group_check=True),
                        waits=([("V", vgk)] + W(fr["psG"], fr["emF"], fr["pack"][tb])
                               if ch == 0 else []))
                    kb.emit("tensor", lambda e, ch=ch, hf=hf: e.matmul(
                        bH()[ch:ch + 64, 0:512], lcum2[ch:ch + 64, :],
                        gkb[ch:ch + 64, hf * 512:(hf + 1) * 512],
                        start=True, stop=True, tile_position=(ch, ch), skip_group_check=True),
                        waits=(W(fr["psH"], fr["eagF"]) if ch == 0 else []),
                        inc=("T" if ch == 64 else None))
                tcs = kb.cnt["T"]
                kb.emit("scalar", lambda e, hf=hf, tb=tb, tcs=tcs: e.activation(
                    pack[:, tb * 4096 + 3 * 1024 + hf * 512: tb * 4096 + 3 * 1024 + hf * 512 + 512],
                    bG()[:, 0:512], AF.Exp), waits=[("T", tcs)], inc="A")
                kb.emit("scalar", lambda e, hf=hf: e.activation(
                    emF[:, hf * 512:(hf + 1) * 512], bG()[:, 0:512], AF.Exp, scale=-1.0), inc="A")
                fr["psG"] = [("A", kb.cnt["A"])]
                fr["emF"] = [("A", kb.cnt["A"])]
                kb.emit("scalar", lambda e, hf=hf: e.activation(
                    eagF[:, hf * 512:(hf + 1) * 512], bH()[:, 0:512], AF.Exp), inc="A")
                fr["psH"] = [("A", kb.cnt["A"])]
                fr["eagF"] = [("A", kb.cnt["A"])]
            aexp = kb.cnt["A"]
            if P2_TSEC < 4:
                continue
            kb.emit("scalar", lambda e: e.square(knk[:], ktmp[:]),
                    waits=[("A", kready)] + W(fr["knk"]), inc="A")
            fr["ktmp"] = [("A", kb.cnt["A"])]
            asq = kb.cnt["A"]
            kb.emit("vector", lambda e: e.reduce_sum(
                kn[:, 0:16], knk.rearrange("p (h j) -> p h j", h=16)[:, :, :], axis=AX.X),
                waits=[("A", asq)], inc="V")
            vkn = kb.cnt["V"]
            kb.emit("scalar", lambda e: e.activation(kn[:, 16:32], kn[:, 0:16], AF.Sqrt),
                    waits=[("V", vkn)], inc="A")
            akn = kb.cnt["A"]
            kb.emit("vector", lambda e: e.tensor_scalar(
                kn[:, 32:48], kn[:, 16:32], 1e-12, None, op0=ALU.max), waits=[("A", akn)])
            kb.emit("vector", lambda e: e.reciprocal(kn[:, 48:64], kn[:, 32:48]))
            kb.emit("vector", lambda e: e.tensor_tensor(
                knk.rearrange("p (h j) -> p h j", h=16),
                ktmp.rearrange("p (h j) -> p h j", h=16),
                kn[:, 48:64].unsqueeze(-1).broadcast_to((128, 16, 64)), op=ALU.mult), inc="V")
            fr["ktmp"] = fr["ktmp"] + [("V", kb.cnt["V"])]
            kb.emit("vector", lambda e, tb=tb: e.tensor_tensor(
                pack[:, tb * 4096:tb * 4096 + 1024], knk[:], emF[:], op=ALU.mult),
                waits=[("A", aexp)], inc="V")
            fr["emF"] = [("V", kb.cnt["V"])]
            kb.emit("vector", lambda e: e.scalar_tensor_tensor(
                eagF.rearrange("p (h j) -> p h j", h=16),
                eagF.rearrange("p (h j) -> p h j", h=16), -1.0,
                gtmp[:, 0:16].unsqueeze(-1).broadcast_to((128, 16, 64)),
                op0=ALU.mult, op1=ALU.mult))
            fr["gtmp"] = [("V", kb.cnt["V"])]
            kb.emit("vector", lambda e, tb=tb: e.tensor_tensor(
                pack[:, tb * 4096 + 1024:tb * 4096 + 2048], knk[:], eagF[:], op=ALU.mult), inc="V")
            fr["eagF"] = [("V", kb.cnt["V"])]
            fr["knk"] = [("V", kb.cnt["V"])]
            kb.emit("vector", lambda e, tb=tb: e.tensor_copy(
                vtm2[:, tb * 2048:(tb + 1) * 2048].rearrange("p (q c) -> p q c", c=64)[:, 0::4, :],
                pack[:, tb * 4096 + 1024:tb * 4096 + 2048].rearrange(
                    "p (a b) -> p a b", b=64)[:, 0::2, :]), inc="V")
            kb.emit("vector", lambda e, tb=tb: e.tensor_copy(
                vtm2[:, tb * 2048:(tb + 1) * 2048].rearrange("p (q c) -> p q c", c=64)[:, 3::4, :],
                pack[:, tb * 4096 + 1024:tb * 4096 + 2048].rearrange(
                    "p (a b) -> p a b", b=64)[:, 1::2, :]), inc="V")
            kb.emit("vector", lambda e, tb=tb: e.tensor_tensor(
                pack[:, tb * 4096 + 2 * 1024:tb * 4096 + 3 * 1024], qtmp[:],
                pack[:, tb * 4096 + 3 * 1024:tb * 4096 + 4 * 1024], op=ALU.mult), inc="V")
            fr["qtmp"] = [("V", kb.cnt["V"])]
            vpack = kb.cnt["V"]
            apack = kb.cnt["A"]
            if P2_TSEC < 5:
                continue
            kb.emit("sync", lambda e, tb=tb: e.dma_start_transpose(
                fmp[:, tb * 4096:(tb + 1) * 4096].rearrange("p (m r) -> p m r", r=128),
                pack[:, tb * 4096:(tb + 1) * 4096]),
                waits=[("V", vpack), ("A", apack)] + W(fr["fmpc"][tb]), inc="DT")
            dtf = kb.cnt["DT"]
            kb.wait("tensor", [("DT", dtf)])
            hl = list(range(16)) if P2_HEADS is None else list(P2_HEADS)
            pairs = [tuple(hl[i:i + 2]) for i in range(0, len(hl), 2)]
            for l in (0, 1):
                frl[l]["psS"] = frl[l]["psS"] + fr["psG"]
                frl[l]["psO"] = frl[l]["psO"] + fr["psH"]
            for pair in pairs:
                lanes = [(h % 2, h) for h in pair]
                for l, h in lanes:
                    fl = frl[l]; B = LN[l]["bank"]
                    jr = (h % 2) * 64; jc = h // 2
                    kt0 = tb * 4096 + jc * 128
                    at0 = tb * 4096 + 1024 + jc * 128
                    for cpar in range(2):
                        r0 = cpar * 64
                        kb.emit("tensor", lambda e, jr=jr, jc=jc, kt0=kt0, r0=r0, tb=tb, B=B: e.matmul(
                            B[r0:r0 + 64, 0:128],
                            fmp[jr:jr + 64, kt0 + r0:kt0 + r0 + 64],
                            fmp4[jr:jr + 64, tb, 1:3, jc, r0:r0 + 64],
                            start=True, stop=True, tile_position=(jr, r0), skip_group_check=True),
                            waits=(W(fl["bA"], fl["psS"], fl["psO"]) if cpar == 0 else []))
                        kb.emit("tensor", lambda e, jr=jr, kt0=kt0, at0=at0, r0=r0, B=B: e.matmul(
                            B[r0:r0 + 64, 512:576],
                            fmp[jr:jr + 64, at0 + r0:at0 + r0 + 64],
                            fmp[jr:jr + 64, kt0 + r0:kt0 + r0 + 64],
                            start=True, stop=True, tile_position=(jr, r0), skip_group_check=True),
                            inc=("T" if cpar == 1 else None))
                    fl["tg"] = kb.cnt["T"]
                for l, h in lanes:
                    fl = frl[l]; B = LN[l]["bank"]
                    kb.emit("vector", lambda e, B=B, ct=LN[l]["ct"]: e.tensor_tensor(
                        ct[:], B[:, 0:128], maskct[:], op=ALU.mult),
                        waits=[("T", fl["tg"])], inc="V")
                    kb.emit("vector", lambda e, B=B, cu=LN[l]["cu"]: e.tensor_tensor(
                        cu[:], B[:, 512:576], maskc[:], op=ALU.mult), inc="V")
                    fl["vmask"] = kb.cnt["V"]
                    fl["pow"] = [("V", fl["vmask"])]
                for l, h in lanes:
                    fl = frl[l]; xo = LN[l]["xoff"]
                    for cpar in range(2):
                        r0 = cpar * 64
                        kb.emit("tensor", lambda e, h=h, r0=r0, xo=xo, tb=tb: e.matmul(
                            px2[r0:r0 + 64, xo:xo + 128], identb[r0:r0 + 64, r0:r0 + 64],
                            vtm2[r0:r0 + 64, tb * 2048 + h * 128:tb * 2048 + h * 128 + 128],
                            start=True, stop=True, tile_position=(r0, r0), skip_group_check=True),
                            waits=(W(fl["bX"]) if cpar == 0 else []))
                        kb.emit("tensor", lambda e, h=h, r0=r0, xo=xo, tb=tb, ct=LN[l]["ct"]: e.matmul(
                            px2[r0:r0 + 64, xo:xo + 128], ct[r0:r0 + 64, 0:64],
                            vtm2[r0:r0 + 64, tb * 2048 + h * 128:tb * 2048 + h * 128 + 128],
                            start=False, stop=True, tile_position=(r0, r0), skip_group_check=True),
                            waits=([("V", fl["vmask"])] if cpar == 0 else []),
                            inc=("T" if cpar == 1 else None))
                    fl["tx"] = kb.cnt["T"]
                for l, h in lanes:
                    fl = frl[l]; xo = LN[l]["xoff"]
                    kb.emit("scalar", lambda e, xo=xo, xwl=LN[l]["xw"]: e.copy(
                        xwl[:, 128:256], px2[:, xo:xo + 128]),
                        waits=[("T", fl["tx"])], inc="A")
                    fl["axw"] = kb.cnt["A"]
                for j in range(1, 6):
                    for l, h in lanes:
                        fl = frl[l]; B = LN[l]["bank"]; pw = LN[l]["pw"]
                        for cpar in range(2):
                            r0 = cpar * 64
                            if j == 1:
                                ctp = LN[l]["ct"][r0:r0 + 64, 0:64]
                                cpp = LN[l]["cu"][r0:r0 + 64, 0:64]
                            else:
                                pbq = ((j - 1) % 2) * 128
                                ctp = pw[r0:r0 + 64, pbq:pbq + 64]
                                cpp = pw[r0:r0 + 64, pbq + 64:pbq + 128]
                            kb.emit("tensor", lambda e, ctp=ctp, cpp=cpp, r0=r0, B=B: e.matmul(
                                B[r0:r0 + 64, 0:64], cpp, ctp,
                                start=True, stop=True, tile_position=(r0, r0), skip_group_check=True),
                                waits=(W(fl["pow"]) if cpar == 0 else []))
                            kb.emit("tensor", lambda e, ctp=ctp, cpp=cpp, r0=r0, B=B: e.matmul(
                                B[r0:r0 + 64, 512:576], ctp, cpp,
                                start=True, stop=True, tile_position=(r0, r0), skip_group_check=True),
                                inc=("T" if cpar == 1 else None))
                        fl["tsq"] = kb.cnt["T"]
                    for l, h in lanes:
                        fl = frl[l]; B4 = LN[l]["bank"]; pw = LN[l]["pw"]
                        pb2 = (j % 2) * 128
                        kb.emit("vector", lambda e, pb2=pb2, B4=B4, pw=pw, tsq=fl["tsq"]: e.tensor_copy(
                            pw[:, pb2:pb2 + 128].rearrange("p (a b) -> p a b", b=64),
                            B4.rearrange("p (a b) -> p a b", b=64)[:, 0::8, :]),
                            waits=[("T", fl["tsq"])], inc="V")
                        fl["vpow"] = kb.cnt["V"]
                        fl["pow"] = [("V", fl["vpow"])]
                    for l, h in lanes:
                        fl = frl[l]; pw = LN[l]["pw"]; xwl = LN[l]["xw"]; xo = LN[l]["xoff"]
                        pb2 = (j % 2) * 128
                        for cpar in range(2):
                            r0 = cpar * 64
                            lh = pw[r0:r0 + 64, pb2:pb2 + 64]
                            rh = xwl[r0:r0 + 64, (j % 2) * 128:(j % 2) * 128 + 128]
                            kb.emit("tensor", lambda e, lh=lh, rh=rh, r0=r0, xo=xo: e.matmul(
                                px2[r0:r0 + 64, xo:xo + 128], lh, rh,
                                start=False, stop=True, tile_position=(r0, r0), skip_group_check=True),
                                waits=([("V", fl["vpow"]), ("A", fl["axw"])] if cpar == 0 else []),
                                inc=("T" if cpar == 1 else None))
                        fl["taj"] = kb.cnt["T"]
                    for l, h in lanes:
                        fl = frl[l]; xwl = LN[l]["xw"]; xo = LN[l]["xoff"]
                        kb.emit("scalar", lambda e, j=j, xo=xo, xwl=xwl, t=fl["taj"]: e.copy(
                            xwl[:, ((j + 1) % 2) * 128:((j + 1) % 2) * 128 + 128],
                            px2[:, xo:xo + 128]),
                            waits=[("T", fl["taj"])], inc="A")
                        fl["axw"] = kb.cnt["A"]
                for cpar in range(2):
                    r0 = cpar * 64
                    for l, h in lanes:
                        fl = frl[l]; B = LN[l]["bank"]; xwl = LN[l]["xw"]
                        jr = (h % 2) * 64; jc = h // 2
                        jo = jr ^ 64
                        wa_off = 0 if h % 2 == 0 else 64
                        wv_off = 64 if h % 2 == 0 else 0
                        wv = xwl[r0:r0 + 64, wv_off:wv_off + 64]
                        wa = xwl[r0:r0 + 64, wa_off:wa_off + 64]
                        ktm = pack[r0:r0 + 64, tb * 4096 + h * 64:tb * 4096 + h * 64 + 64]
                        ptc = LN[l]["ct"][r0:r0 + 64, 64:128]
                        w0 = (([("V", fl["vpow"]), ("A", fl["axw"])] if cpar == 0
                               else W(fl["psS"], fl["psO"])) + W(fl["sumev"]))
                        kb.emit("tensor", lambda e, r0=r0, jr=jr, wv=wv, ptc=ptc, B=B: e.matmul(
                            B[jr:jr + 64, 0:64], wv, ptc,
                            start=True, stop=True, tile_position=(r0, jr), skip_group_check=True),
                            waits=w0)
                        kb.emit("tensor", lambda e, r0=r0, jr=jr, wa=wa, ptc=ptc, B=B: e.matmul(
                            B[jr:jr + 64, 512:576], wa, ptc,
                            start=True, stop=True, tile_position=(r0, jr), skip_group_check=True))
                        kb.emit("tensor", lambda e, r0=r0, jo=jo, ktm=ktm, wv=wv, B=B: e.matmul(
                            B[jo:jo + 64, 0:64], ktm, wv,
                            start=True, stop=True, tile_position=(r0, jo), skip_group_check=True))
                        kb.emit("tensor", lambda e, r0=r0, jo=jo, ktm=ktm, wa=wa, B=B: e.matmul(
                            B[jo:jo + 64, 512:576], wa, ktm,
                            start=True, stop=True, tile_position=(r0, jo), skip_group_check=True),
                            inc="T")
                        ts1 = kb.cnt["T"]
                        kb.emit("scalar", lambda e, jr=jr, ts1=ts1, B=B: e.copy(
                            sumb[jr:jr + 64, 0:64], B[jr:jr + 64, 0:64]),
                            waits=[("T", ts1)])
                        kb.emit("scalar", lambda e, jr=jr, B=B: e.copy(
                            sumb[jr:jr + 64, 64:128], B[jr:jr + 64, 512:576]))
                        kb.emit("scalar", lambda e, jr=jr, jo=jo, B=B: e.copy(
                            sumb[jr:jr + 64, 128:192], B[jo:jo + 64, 0:64]))
                        kb.emit("scalar", lambda e, jr=jr, jo=jo, B=B: e.copy(
                            sumb[jr:jr + 64, 192:256], B[jo:jo + 64, 512:576]), inc="A")
                        as2 = kb.cnt["A"]
                        fl["sumev"] = [("A", as2)]
                        fl["bA"] = [("A", as2)]
                    for l, h in lanes:
                        fl = frl[l]; B = LN[l]["bank"]
                        jr = (h % 2) * 64; jc = h // 2
                        qbT = fmp[jr:jr + 64, tb * 4096 + 2 * 1024 + jc * 128 + r0:
                                  tb * 4096 + 2 * 1024 + jc * 128 + r0 + 64]
                        kb.emit("tensor", lambda e, jr=jr, jc=jc, B=B: e.matmul(
                            B[jr:jr + 64, 0:64], identb[jr:jr + 64, jr:jr + 64],
                            states[jr:jr + 64, jc * 64:jc * 64 + 64],
                            start=True, stop=False, tile_position=(jr, jr), skip_group_check=True),
                            waits=W(fl["sumev"], fl["psS"], fl["chain"]))
                        kb.emit("tensor", lambda e, jr=jr, jc=jc, B=B: e.matmul(
                            B[jr:jr + 64, 0:64], sumb[jr:jr + 64, 192:256],
                            states[jr:jr + 64, jc * 64:jc * 64 + 64],
                            start=False, stop=False, tile_position=(jr, jr), skip_group_check=True))
                        kb.emit("tensor", lambda e, jr=jr, B=B: e.matmul(
                            B[jr:jr + 64, 0:64], identb[jr:jr + 64, jr:jr + 64],
                            sumb[jr:jr + 64, 128:192],
                            start=False, stop=True, tile_position=(jr, jr), skip_group_check=True))
                        kb.emit("tensor", lambda e, jr=jr, jc=jc, B=B: e.matmul(
                            B[jr:jr + 64, 512:576], states[jr:jr + 64, jc * 64:jc * 64 + 64],
                            sumb[jr:jr + 64, 64:128],
                            start=True, stop=False, tile_position=(jr, jr), skip_group_check=True),
                            waits=W(fl["psO"]))
                        kb.emit("tensor", lambda e, jr=jr, jc=jc, qbT=qbT, B=B: e.matmul(
                            B[jr:jr + 64, 512:576], states[jr:jr + 64, jc * 64:jc * 64 + 64],
                            qbT,
                            start=False, stop=False, tile_position=(jr, jr), skip_group_check=True))
                        kb.emit("tensor", lambda e, jr=jr, B=B: e.matmul(
                            B[jr:jr + 64, 512:576], identb[jr:jr + 64, jr:jr + 64],
                            sumb[jr:jr + 64, 0:64],
                            start=False, stop=True, tile_position=(jr, jr), skip_group_check=True),
                            inc="T")
                        fl["tch"] = kb.cnt["T"]
                    for l, h in lanes:
                        fl = frl[l]; B = LN[l]["bank"]
                        jr = (h % 2) * 64; jc = h // 2
                        exoff = tb * 4096 + 3 * 1024 + jc * 128 + r0 + 63
                        kb.emit("vector", lambda e, jr=jr, jc=jc, exoff=exoff, B=B: e.tensor_tensor(
                            states[jr:jr + 64, jc * 64:jc * 64 + 64], B[jr:jr + 64, 0:64],
                            fmp[jr:jr + 64, exoff:exoff + 1].broadcast_to((64, 64)),
                            op=ALU.mult), waits=[("T", fl["tch"])], inc="V")
                        fl["psS"] = [("V", kb.cnt["V"])]
                        ocol = jc * 512 + tk4 * 128 + r0
                        kb.emit("vector", lambda e, jr=jr, ocol=ocol, B=B: e.tensor_tensor(
                            osg[jr:jr + 64, ocol:ocol + 64], B[jr:jr + 64, 512:576],
                            ogp[jr:jr + 64, ocol:ocol + 64],
                            op=ALU.mult), waits=W(fr["osg"]), inc="V")
                        fl["psO"] = [("V", kb.cnt["V"])]
                        fl["chain"] = [("V", kb.cnt["V"])]
                for l, h in lanes:
                    frl[l]["bX"] = [("A", frl[l]["axw"])]
                if P2_HEADS is None and P2_TSEC >= 5:
                    l1_emit_unit()
            fr["og"] = [("V", kb.cnt["V"])]
            fr["psG"] = [("V", kb.cnt["V"])]
            fr["psH"] = [("V", kb.cnt["V"])]
            fr["vtm"][tb] = [("T", kb.cnt["T"])]
            fr["pack"][tb] = [("T", kb.cnt["T"]), ("DT", kb.cnt["DT"])]
            fr["fmpc"][tb] = [("T", kb.cnt["T"]), ("V", kb.cnt["V"]), ("G", kb.cnt["G"])]
            fr["gkb"] = [("T", kb.cnt["T"])]
        vlast = kb.cnt["V"]
        for jc in range(8):
            kb.emit("scalar", lambda e, jc=jc, s0=s0: e.dma_start(
                out=io["o_dram"][jc, :, s0:s0 + 512], in_=osg[:, jc * 512:(jc + 1) * 512]),
                waits=([("V", vlast)] if jc == 0 else []), inc="DS")
        fr["osg"] = [("DS", kb.cnt["DS"])]
    while l1_u[0] < 256:
        l1_emit_unit()
    env["p2_done"] = True


def phase_hidden(kb, io, in_name, out_name, wt_name, sc_name, pstk):
    tag = wt_name
    L = lambda n, sh, dt: kb.sb(n + tag, sh, dt, stack=pstk)
    hblk = L("lh_hblk", [128, 32 * 2048], BF16)
    wbb = L("lh_wbb", [128, 2 * 4096], BF16)
    gout = L("lh_gout", [128, 2 * 512], BF16)
    scc = L("lh_sc", [128, 32], F32)
    psb = [kb.ps("lh_ps%d%s" % (i, tag), stack=pstk) for i in range(2)]
    kb.emit("scalar", lambda e: e.dma_start(out=scc[:], in_=io[sc_name][:]), inc="DS")
    scc_ds = kb.cnt["DS"]

    steps = [(TT, oc) for TT in range(2) for oc in range(32)]
    if LH_LIM is not None:
        steps = steps[:LH_LIM]
    ldq = {}
    st_tail = {0: 0, 1: 0}
    evA_tail = {0: 0, 1: 0}
    mm_tail = {0: 0, 1: 0}
    hld = {}

    def load_for(i):
        if i >= len(steps) or i in ldq:
            return
        TT, oc = steps[i]
        b = i % 2
        sm = "DG" if b == 0 else "DH"
        v = kb.emit("gpsimd", lambda e, oc=oc, b=b: e.dma_start(
            out=wbb[:, b * 4096:(b + 1) * 4096], in_=io[wt_name][oc]),
            waits=[("T", mm_tail[b])], inc=sm)
        ldq[i] = (sm, v)

    load_for(0)
    for i, (TT, oc) in enumerate(steps):
        b = i % 2
        if oc == 0:
            tguard = kb.cnt["T"]
            for dc in range(32):
                kb.emit("scalar", lambda e, dc=dc, TT=TT: e.dma_start(
                    out=hblk[:, dc * 2048:(dc + 1) * 2048],
                    in_=io[in_name][dc, :, TT * 2048:TT * 2048 + 2048]),
                    waits=([("T", tguard)] if dc == 0 else []), inc="DS")
            hld[TT] = kb.cnt["DS"]
        load_for(i + 1)
        for ts in range(4):
            pb = ts % 2
            w0 = ([ldq[i], ("DS", hld[TT])] if ts == 0 else []) + [("A", evA_tail[pb])]
            for dc in range(32):
                kb.emit("tensor", lambda e, dc=dc, b=b, pb=pb, ts=ts, TT=TT, first=(dc == 0): e.matmul(
                    psb[pb][:, :], wbb[:, b * 4096 + dc * 128:b * 4096 + dc * 128 + 128],
                    hblk[:, dc * 2048 + ts * 512:dc * 2048 + ts * 512 + 512],
                    start=first, stop=(dc == 31)),
                    waits=w0 if dc == 0 else [], inc=("T" if dc == 31 else None))
            tmm = kb.cnt["T"]
            kb.emit("scalar", lambda e, pb=pb, oc=oc, tmm=tmm: e.activation(
                gout[:, pb * 512:(pb + 1) * 512], psb[pb][:, :], AF.Gelu,
                scale=scc[:, oc:oc + 1]),
                waits=[("T", tmm), ("DS", st_tail[pb])]
                + ([("DS", scc_ds)] if i == 0 and ts == 0 else []), inc="A")
            evA_tail[pb] = kb.cnt["A"]
            kb.emit("scalar", lambda e, pb=pb, oc=oc, t0=TT * 2048 + ts * 512: e.dma_start(
                out=io[out_name][oc, :, t0:t0 + 512], in_=gout[:, pb * 512:(pb + 1) * 512]),
                waits=[("A", evA_tail[pb])], inc="DS")
            st_tail[pb] = kb.cnt["DS"]
        mm_tail[b] = kb.cnt["T"]


def phase_final(kb, io, pstk):
    L = lambda n, sh, dt: kb.sb(n, sh, dt, stack=pstk)
    w3b = L("f_w3b", [128, 8 * 4096], BF16)
    wob = L("f_wob", [128, 8 * 1024], BF16)
    g2b = L("f_g2b", [128, 2 * 32 * 512], BF16)
    olod = L("f_olod", [128, 8 * 512], BF16)
    olnb = L("f_oln", [128, 8 * 512], BF16)
    osq = L("f_osq", [128, 512], BF16)
    ysb = L("f_ysb", [128, 2 * 512], F32)
    xfb = L("f_xfb", [128, 2 * 512], F32)
    stt = L("f_stats", [2, 6 * 512], F32)
    s3c = L("f_s3", [128, 8], F32)
    ones = kb.tens["c_ones"]
    onesf = kb.tens["c_onesf"]
    nwc = kb.tens["c_nw"]
    pFs = [kb.ps("f_pF%d" % i, stack=pstk) for i in range(2)]
    pWs = [kb.ps("f_pW%d" % i, stack=pstk) for i in range(2)]
    pS = kb.ps("f_pS", (2, 512), stack=pstk)
    pS2 = kb.ps("f_pS2", (2, 512), stack=pstk)
    pR = kb.ps("f_pR", stack=pstk)
    pM = kb.ps("f_pM", stack=pstk)

    kb.emit("scalar", lambda e: e.dma_start(out=s3c[:], in_=io["s3"][:]), inc="DS")
    for oc in range(8):
        kb.emit("gpsimd", lambda e, oc=oc: e.dma_start(
            out=w3b[:, oc * 4096:(oc + 1) * 4096], in_=io["w3t"][oc]), inc="DG")
        kb.emit("gpsimd", lambda e, oc=oc: e.dma_start(
            out=wob[:, oc * 1024:(oc + 1) * 1024], in_=io["wot"][oc]), inc="DG")
    cast_done = kb.cnt["DG"]
    kb.wait("tensor", [("DG", cast_done)])

    MU, S2N, VAR, SQ, RSTD, NMU = (stt[0:1, i * 512:(i + 1) * 512] for i in range(6))
    ev_prev = 0
    yst_tail = {0: 0, 1: 0}
    aw_tail = {0: 0, 1: 0}
    g2_rd = {0: 0, 1: 0}
    for tt in range(8):
        t0 = tt * 512
        gb = (tt % 2)
        for jc in range(8):
            kb.emit("scalar", lambda e, jc=jc, t0=t0: e.dma_start(
                out=olod[:, jc * 512:(jc + 1) * 512], in_=io["o_dram"][jc, :, t0:t0 + 512]),
                waits=([("V", ev_prev)] if jc == 0 else []), inc="DS")
        old = kb.cnt["DS"]
        gsm = "DH" if gb == 0 else "DG"
        for dc in range(32):
            w = [("T", g2_rd[gb])] if dc == 0 else []
            if dc == 0 and gsm == "DG" and tt == 1:
                w = w + [("DG", cast_done)]
            kb.emit("gpsimd", lambda e, dc=dc, t0=t0, gb=gb: e.dma_start(
                out=g2b[:, gb * 16384 + dc * 512:gb * 16384 + (dc + 1) * 512],
                in_=io["h_dram"][dc, :, t0:t0 + 512]),
                waits=w, inc=gsm)
        g2ld = (gsm, kb.cnt[gsm])
        sq_guard = 0
        for jc in range(8):
            kb.emit("scalar", lambda e, jc=jc: e.square(
                osq[:], olod[:, jc * 512:(jc + 1) * 512]),
                waits=([("DS", old)] if jc == 0 else []) + [("T", sq_guard)], inc="A")
            aq = kb.cnt["A"]
            kb.emit("tensor", lambda e, jc=jc, first=(jc == 0): e.matmul(
                pS[0:1, :], ones[:, 0:1], olod[:, jc * 512:(jc + 1) * 512],
                start=first, stop=(jc == 7)), waits=[("DS", old)] if jc == 0 else [])
            kb.emit("tensor", lambda e, jc=jc, first=(jc == 0), aq=aq: e.matmul(
                pS2[0:1, :], ones[:, 0:1], osq[:],
                start=first, stop=(jc == 7)), waits=[("A", aq)], inc="T")
            sq_guard = kb.cnt["T"]
        tsum = kb.cnt["T"]
        kb.emit("vector", lambda e: e.tensor_scalar(MU, pS[0:1, :], 1.0 / D, None, op0=ALU.mult),
                waits=[("T", tsum)], chain=True)
        kb.emit("vector", lambda e: e.tensor_scalar(S2N, pS2[0:1, :], 1.0 / D, None, op0=ALU.mult),
                chain=True)
        kb.emit("vector", lambda e: e.tensor_tensor(VAR, MU, MU, op=ALU.mult), chain=True)
        kb.emit("vector", lambda e: e.tensor_tensor(VAR, S2N, VAR, op=ALU.subtract), chain=True)
        vvar = kb.cnt["V"]
        kb.emit("scalar", lambda e: e.activation(SQ, VAR, AF.Sqrt, bias=kb.tens["c_actc"][0:1, 0:1], scale=1.0),
                waits=[("V", vvar)], inc="A")
        asq2 = kb.cnt["A"]
        kb.emit("vector", lambda e: e.reciprocal(RSTD, SQ), waits=[("A", asq2)], chain=True)
        kb.emit("vector", lambda e: e.tensor_scalar(NMU, MU, -1.0, None, op0=ALU.mult), chain=True)
        vst = kb.cnt["V"]
        kb.emit("tensor", lambda e: e.matmul(pR[:, :], onesf[0:1, :], RSTD, start=True, stop=True),
                waits=[("V", vst), ("V", ev_prev)])
        kb.emit("tensor", lambda e: e.matmul(pM[:, :], onesf[0:1, :], NMU, start=True, stop=True),
                inc="T")
        trep = kb.cnt["T"]
        for jc in range(8):
            kb.emit("vector", lambda e, jc=jc: e.tensor_tensor(
                olnb[:, jc * 512:(jc + 1) * 512], olod[:, jc * 512:(jc + 1) * 512],
                pM[:, :], op=ALU.add), waits=[("T", trep)] if jc == 0 else [], chain=True)
            kb.emit("vector", lambda e, jc=jc: e.scalar_tensor_tensor(
                olnb[:, jc * 512:(jc + 1) * 512], olnb[:, jc * 512:(jc + 1) * 512],
                nwc[:, jc:jc + 1], pR[:, :], op0=ALU.mult, op1=ALU.mult), chain=True)
        ev_prev = kb.cnt["V"]
        voln = kb.cnt["V"]
        for oc in range(8):
            ob = oc % 2
            kb.emit("scalar", lambda e, oc=oc, t0=t0, ob=ob: e.dma_start(
                out=xfb[:, ob * 512:(ob + 1) * 512], in_=io["xT"][oc * 128:(oc + 1) * 128, t0:t0 + 512]),
                waits=[("V", yst_tail[ob])], inc="DS")
            xld = kb.cnt["DS"]
            for dc in range(32):
                kb.emit("tensor", lambda e, dc=dc, oc=oc, gb=gb, ob=ob, first=(dc == 0): e.matmul(
                    pFs[ob][:, :], w3b[:, oc * 4096 + dc * 128:oc * 4096 + dc * 128 + 128],
                    g2b[:, gb * 16384 + dc * 512:gb * 16384 + dc * 512 + 512],
                    start=first, stop=(dc == 31)),
                    waits=[g2ld, ("V", yst_tail[ob]), ("A", aw_tail[ob])] if dc == 0 else [])
            for jc in range(8):
                kb.emit("tensor", lambda e, jc=jc, oc=oc, ob=ob, first=(jc == 0): e.matmul(
                    pWs[ob][:, :], wob[:, oc * 1024 + jc * 128:oc * 1024 + jc * 128 + 128],
                    olnb[:, jc * 512:(jc + 1) * 512],
                    start=first, stop=(jc == 7)),
                    waits=[("V", voln)] if jc == 0 else [], inc=("T" if jc == 7 else None))
            tmm = kb.cnt["T"]
            kb.emit("scalar", lambda e, oc=oc, ob=ob: e.copy(
                ysb[:, ob * 512:(ob + 1) * 512], pWs[ob][:, :]),
                waits=[("T", tmm), ("DS", xld)], inc="A")
            ayw = kb.cnt["A"]
            aw_tail[ob] = ayw
            kb.emit("vector", lambda e, oc=oc, ob=ob: e.scalar_tensor_tensor(
                ysb[:, ob * 512:(ob + 1) * 512], pFs[ob][:, :], s3c[:, oc:oc + 1],
                ysb[:, ob * 512:(ob + 1) * 512],
                op0=ALU.mult, op1=ALU.add), waits=[("A", ayw), ("DS", xld)], chain=True)
            kb.emit("vector", lambda e, oc=oc, ob=ob: e.tensor_tensor(
                ysb[:, ob * 512:(ob + 1) * 512], ysb[:, ob * 512:(ob + 1) * 512],
                xfb[:, ob * 512:(ob + 1) * 512], op=ALU.add), chain=True)
            yv = kb.cnt["V"]
            yst_tail[ob] = yv
            kb.emit("scalar", lambda e, oc=oc, t0=t0, ob=ob, yv=yv: e.dma_start(
                out=io["y"][oc, :, t0:t0 + 512], in_=ysb[:, ob * 512:(ob + 1) * 512]),
                waits=[("V", yv)], inc="DS")
        g2_rd[gb] = kb.cnt["T"]


def build_program(phases=("p0", "p2", "l2", "l3", "fin"), expose=(), p2_nseg=NSEG):
    nc = bass.Bass("TRN2", target_bir_lowering=False)
    io = declare_io(nc, expose)
    stack = ExitStack()
    with stack:
        kb = KB(nc, stack)
        env = None
        xn_stack = ExitStack()
        with ExitStack() as ps0:
            env = phase0_xn(kb, io, ps0, xn_stack=xn_stack)
        if "xn" in expose:
            v = kb.cnt["V"]
            kb.emit("scalar", lambda e: e.dma_start(out=io["xn_dump"][:], in_=kb.tens["xn_fm"][:]),
                    waits=[("V", v)], inc="DS")
        kb.fence()
        if "l1" in phases:
            with ExitStack() as ps:
                phase_l1(kb, io, env, ps)
            kb.fence()
        if "p2" in phases:
            with ExitStack() as ps:
                phase_p2(kb, io, env, ps, nseg=p2_nseg)
            kb.fence()
        xn_stack.close()
        if "l2" in phases:
            with ExitStack() as ps:
                phase_hidden(kb, io, "h_dram", "g_dram", "h1t", "sh1", ps)
            kb.fence()
        if "l3" in phases:
            with ExitStack() as ps:
                phase_hidden(kb, io, "g_dram", "h_dram", "h2t", "sh2", ps)
            kb.fence()
        if "fin" in phases:
            with ExitStack() as ps:
                phase_final(kb, io, ps)
        kb.wait("tensor", [("DS", kb.cnt["DS"]), ("DT", kb.cnt["DT"]), ("DG", kb.cnt["DG"]),
                           ("V", kb.cnt["V"]), ("A", kb.cnt["A"])])
        kb.finalize()
    return nc


def kernel(**inputs):
    cm, xs = host_prepare(inputs)
    nc = build_program()
    in_maps = []
    for b in range(8):
        m = dict(cm)
        m["xT"] = xs[b]
        in_maps.append(m)
    res = run_bass_kernel_spmd(nc, in_maps, core_ids=list(range(8)))
    outs = []
    for b in range(8):
        yfm = np.asarray(res.results[b]["y"], dtype=np.float32)
        outs.append(yfm.reshape(1024, 4096).T)
    return np.ascontiguousarray(np.stack(outs)).astype(np.float32)


# version 12
# speedup vs baseline: 1.0132x; 1.0132x over previous
import numpy as np
import ml_dtypes
import concourse.bass as bass
import concourse.mybir as mybir
from concourse.bass_utils import run_bass_kernel_spmd
from contextlib import ExitStack

BF16 = mybir.dt.bfloat16
F32 = mybir.dt.float32
AF = mybir.ActivationFunctionType
ALU = mybir.AluOpType
AX = mybir.AxisListType
BND = ml_dtypes.bfloat16

T, D, NH, HD, CH = 4096, 1024, 16, 64, 64
NT = T // 128
NSEG = 8
TPS = 4
DHID = 4096
P2_HEADS = None
P2_FENCE_HEADS = False
P2_DBG = None
P2_SEC = {}
P2_TSEC = 99
LH_LIM = None

def _tile_ocp(w):
    din, dout = w.shape
    dc, oc = din // 128, dout // 128
    return np.ascontiguousarray(
        w.reshape(dc, 128, oc, 128).transpose(2, 1, 0, 3).reshape(oc, 128, dc * 128))

def host_prepare(inputs):
    g = lambda n: np.ascontiguousarray(np.asarray(inputs[n], dtype=np.float32))
    cm = {}
    cm["w1t"] = _tile_ocp(g("L0_ffn_w1_w").T)
    cm["w2t"] = _tile_ocp(g("L0_ffn_w2_w").T)
    cm["h1t"] = _tile_ocp(g("L0_ffn_h1_w").T)
    cm["h2t"] = _tile_ocp(g("L0_ffn_h2_w").T)
    cm["w3t"] = _tile_ocp(g("L0_ffn_w3_w").T)
    cm["wot"] = _tile_ocp(g("L0_attn_Wo"))
    for nm, k in (("s1", "L0_ffn_w1_s"), ("s2", "L0_ffn_w2_s"),
                  ("sh1", "L0_ffn_h1_s"), ("sh2", "L0_ffn_h2_s"),
                  ("s3", "L0_ffn_w3_s"), ("nw", "L0_attn_norm_w")):
        v = g(k)
        cm[nm] = np.ascontiguousarray(v.reshape(-1, 128).T)
    qkvg = np.concatenate([g("L0_attn_Wq"), g("L0_attn_Wk"), g("L0_attn_Wv"),
                           g("L0_attn_Wgamma")], axis=1)
    cm["qkvg"] = np.ascontiguousarray(qkvg.reshape(8, 128, 3088))
    f1og1 = np.concatenate([g("L0_attn_Wf1"), g("L0_attn_Wog1")], axis=1)
    cm["f1og1"] = np.ascontiguousarray(f1og1.reshape(8, 128, 128))
    cm["wf2"] = g("L0_attn_Wf2")
    wog2 = g("L0_attn_Wog2")
    cm["wog2t"] = np.ascontiguousarray(wog2.reshape(64, 8, 128).transpose(1, 0, 2))
    cm["ident_b"] = np.eye(128).astype(BND)
    cm["ones_b"] = np.ones((128, 128), dtype=BND)
    cm["ones_f"] = np.ones((1, 128), dtype=np.float32)
    cm["actc"] = np.tile(np.array([[1e-5, 1.0]], np.float32), (128, 1))
    su = np.triu(np.ones((CH, CH), np.float32), 1)
    iu = np.triu(np.ones((CH, CH), np.float32), 0)
    sl = np.tril(np.ones((CH, CH), np.float32), -1)
    cm["maskct"] = np.ascontiguousarray(
        np.concatenate([np.concatenate([su, iu], 1)] * 2, 0)).astype(BND)
    cm["maskc"] = np.ascontiguousarray(np.concatenate([sl, sl], 0)).astype(BND)
    cm["lcum"] = np.ascontiguousarray(np.concatenate([iu] * 2, 0)).astype(BND)
    cm["lcum2"] = np.ascontiguousarray(
        np.concatenate([iu + np.eye(CH, dtype=np.float32)] * 2, 0)).astype(BND)
    i64d = np.concatenate([np.eye(64, dtype=np.float32)] * 2, 0)
    cm["i64d"] = i64d.astype(BND)
    xs = [np.ascontiguousarray(np.asarray(inputs["L0_attn_x"][b], np.float32).T)
          for b in range(8)]
    return cm, xs


class KB:
    ENGS = ("tensor", "vector", "scalar", "gpsimd", "sync")

    def __init__(self, nc, stack):
        self.nc = nc
        self.stack = stack
        self.streams = {e: [] for e in self.ENGS}
        self.sems = {}
        self.cnt = {}
        for s in ("T", "V", "A", "G", "F", "DS", "DT", "DG", "DH"):
            self.sems[s] = stack.enter_context(nc.semaphore("sem_" + s))
            self.cnt[s] = 0
        self.tens = {}
        self.waited = {s: set() for s in self.DMA_SEMS}

    def sb(self, name, shape, dtype, stack=None, side=None):
        kw = {"side": side} if side else {}
        t = (stack or self.stack).enter_context(self.nc.sbuf_tensor(name, shape, dtype, **kw))
        self.tens[name] = t
        return t

    def ps(self, name, shape=(128, 512), dtype=F32, stack=None):
        t = (stack or self.stack).enter_context(self.nc.psum_tensor(name, list(shape), dtype))
        self.tens[name] = t
        return t

    def fence(self):
        for eng in ("tensor", "vector", "scalar", "gpsimd"):
            self.emit(eng, lambda e: e.nop(), inc="F")
        snap = dict(self.cnt)
        waits = [(s, snap[s]) for s in ("F", "DS", "DT", "DG", "DH")]
        for eng in self.ENGS:
            self.wait(eng, waits)

    SELF_SEM = {"tensor": "T", "vector": "V", "scalar": "A", "gpsimd": "G"}
    DMA_SEMS = ("DS", "DT", "DG", "DH")

    def emit(self, eng, make, waits=(), inc=None, chain=False):
        v = None
        if eng in ("vector", "scalar", "gpsimd"):
            chain = True
        if chain:
            s = self.SELF_SEM[eng]
            waits = list(waits) + [(s, self.cnt[s])]
            if inc is None:
                inc = s
        if inc is not None:
            k = 16 if inc in self.DMA_SEMS else 1
            self.cnt[inc] += k
            v = self.cnt[inc]
        waits = [(s, w) for (s, w) in waits if w > 0]
        sems = self.sems
        kk = (16 if inc in self.DMA_SEMS else 1) if inc else 0
        def f(e, make=make, inc=inc, k=kk):
            for s, w in f.waits:
                e.wait_ge(sems[s], w)
            ins = make(e)
            if inc is not None:
                ins.then_inc(sems[inc], k)
        f.waits = waits
        f.inc_sem = inc
        f.inc_k = kk
        f.dma_sem = inc if inc in self.DMA_SEMS else None
        f.dma_count = v
        self.streams[eng].append(f)
        return v

    def wait(self, eng, waits):
        waits = [(s, w) for (s, w) in waits if w > 0]
        if not waits:
            return
        sems = self.sems
        def f(e):
            for s, w in f.waits:
                e.wait_ge(sems[s], w)
        f.waits = waits
        f.inc_sem = None
        f.inc_k = 0
        f.dma_sem = None
        f.dma_count = None
        self.streams[eng].append(f)

    def finalize(self):
        waited = {s: set() for s in self.DMA_SEMS}
        succ = {s: {} for s in self.DMA_SEMS}
        for eng in self.ENGS:
            for f in self.streams[eng]:
                for s, w in f.waits:
                    if s in self.DMA_SEMS:
                        waited[s].add(w)
                if f.dma_sem is not None:
                    succ[f.dma_sem][f.dma_count] = f
        for s, vals in waited.items():
            for v in vals:
                nf = succ[s].get(v + 16)
                if nf is not None and not any(
                        ss == s and ww >= v for ss, ww in nf.waits):
                    nf.waits = list(nf.waits) + [(s, v)]
        def run(e, fs):
            for f in fs:
                f(e)
        with self.nc.Block() as block:
            @block.tensor
            def _(e):
                run(e, self.streams["tensor"])
            @block.vector
            def _(e):
                run(e, self.streams["vector"])
            @block.scalar
            def _(e):
                run(e, self.streams["scalar"])
            @block.gpsimd
            def _(e):
                run(e, self.streams["gpsimd"])
            @block.sync
            def _(e):
                run(e, self.streams["sync"])


def declare_io(nc, expose=()):
    d = {}
    def di(name, shape, dtype, kind="ExternalInput"):
        if kind == "Internal" and name in expose:
            kind = "ExternalOutput"
        if kind == "Internal" and ("in:" + name) in expose:
            kind = "ExternalInput"
        d[name] = nc.dram_tensor(name, list(shape), dtype, kind=kind).ap()
    di("xT", (D, T), F32)
    di("qkvg", (8, 128, 3088), F32)
    di("f1og1", (8, 128, 128), F32)
    di("wf2", (64, D), F32)
    di("wog2t", (8, 64, 128), F32)
    di("w1t", (32, 128, 1024), F32)
    di("w2t", (32, 128, 1024), F32)
    di("h1t", (32, 128, 4096), F32)
    di("h2t", (32, 128, 4096), F32)
    di("w3t", (8, 128, 4096), F32)
    di("wot", (8, 128, 1024), F32)
    for nm, n in (("s1", 32), ("s2", 32), ("sh1", 32), ("sh2", 32), ("s3", 8), ("nw", 8)):
        di(nm, (128, n), F32)
    di("ident_b", (128, 128), BF16)
    di("ones_b", (128, 128), BF16)
    di("ones_f", (1, 128), F32)
    di("actc", (128, 2), F32)
    di("maskct", (128, 128), BF16)
    di("maskc", (128, 64), BF16)
    di("lcum", (128, 64), BF16)
    di("lcum2", (128, 64), BF16)
    di("i64d", (128, 64), BF16)
    di("y", (8, 128, T), F32, kind="ExternalOutput")
    di("h_dram", (32, 128, T), BF16, kind="Internal")
    di("g_dram", (32, 128, T), BF16, kind="Internal")
    di("o_dram", (8, 128, T), BF16, kind="Internal")
    if "dbg" in expose:
        di("dbg", (128, 4096), BF16, kind="ExternalOutput")
    if "xn" in expose:
        di("xn_dump", (128, 8 * T), BF16, kind="ExternalOutput")
    return d


def phase0_xn(kb, io, pstk, xn_stack=None):
    nc = kb.nc
    xn = kb.sb("xn_fm", [128, 8 * T], BF16, stack=xn_stack, side="right")
    ones = kb.sb("c_ones", [128, 128], BF16)
    onesf = kb.sb("c_onesf", [1, 128], F32)
    actc = kb.sb("c_actc", [128, 2], F32)
    identb = kb.sb("c_identb", [128, 128], BF16)
    maskct = kb.sb("c_maskct", [128, 128], BF16)
    maskc = kb.sb("c_maskc", [128, 64], BF16)
    lcum = kb.sb("c_lcum", [128, 64], BF16)
    lcum2 = kb.sb("c_lcum2", [128, 64], BF16)
    i64d = kb.sb("c_i64d", [128, 64], BF16)
    nwc = kb.sb("c_nw", [128, 8], F32)
    xf = kb.sb("p0_xf", [128, 2 * 8 * 512], F32, stack=pstk)
    xb = kb.sb("p0_xb", [128, 8 * 512], BF16, stack=pstk)
    x2b = kb.sb("p0_x2b", [128, 8 * 512], BF16, stack=pstk)
    st = kb.sb("p0_stats", [2, 6 * 512], F32, stack=pstk)
    psS = kb.ps("ps0_s", (2, 512), stack=pstk)
    psS2 = kb.ps("ps0_s2", (2, 512), stack=pstk)
    psR = kb.ps("ps0_r", (128, 512), stack=pstk)
    psM = kb.ps("ps0_m", (128, 512), stack=pstk)

    for nm, dst in (("ones_b", ones), ("ones_f", onesf), ("actc", actc), ("ident_b", identb),
                    ("maskct", maskct), ("maskc", maskc), ("lcum", lcum),
                    ("lcum2", lcum2), ("i64d", i64d), ("nw", nwc)):
        kb.emit("scalar", lambda e, dst=dst, nm=nm: e.dma_start(out=dst[:], in_=io[nm][:]), inc="DS")
    c_done = kb.cnt["DS"]

    ev_prev = 0
    tsum_prev = 0
    for tt in range(8):
        t0 = tt * 512
        pb = (tt % 2) * 8 * 512
        for dc in range(8):
            kb.emit("scalar", lambda e, dc=dc, t0=t0, pb=pb: e.dma_start(
                out=xf[:, pb + dc * 512: pb + (dc + 1) * 512],
                in_=io["xT"][dc * 128:(dc + 1) * 128, t0:t0 + 512]), inc="DS")
        ld = kb.cnt["DS"]
        for dc in range(8):
            vv = kb.emit("vector", lambda e, dc=dc, pb=pb: e.tensor_copy(
                xb[:, dc * 512:(dc + 1) * 512], xf[:, pb + dc * 512: pb + dc * 512 + 512]),
                waits=([("DS", ld), ("T", tsum_prev)] if dc == 0 else []), inc="V")
            kb.emit("scalar", lambda e, dc=dc: e.square(
                x2b[:, dc * 512:(dc + 1) * 512], xb[:, dc * 512:(dc + 1) * 512]),
                waits=[("V", vv)] + ([("T", tsum_prev)] if dc == 0 else []), inc="A")
        vcast = kb.cnt["V"]; acast = kb.cnt["A"]
        for dc in range(8):
            kb.emit("tensor", lambda e, dc=dc, first=(dc == 0): e.matmul(
                psS[0:1, :], ones[:, 0:1], xb[:, dc * 512:(dc + 1) * 512],
                start=first, stop=(dc == 7)),
                waits=([("V", vcast)] if dc == 0 else []))
            kb.emit("tensor", lambda e, dc=dc, first=(dc == 0): e.matmul(
                psS2[0:1, :], ones[:, 0:1], x2b[:, dc * 512:(dc + 1) * 512],
                start=first, stop=(dc == 7)),
                waits=([("A", acast)] if dc == 0 else []), inc=("T" if dc == 7 else None))
        tsum = kb.cnt["T"]
        tsum_prev = tsum
        MU, S2N, VAR, SQ, RSTD, NMR = (st[0:1, i * 512:(i + 1) * 512] for i in range(6))
        kb.emit("vector", lambda e: e.tensor_scalar(MU, psS[0:1, :], 1.0 / D, None, op0=ALU.mult),
                waits=[("T", tsum)], chain=True)
        kb.emit("vector", lambda e: e.tensor_scalar(S2N, psS2[0:1, :], 1.0 / D, None, op0=ALU.mult),
                chain=True)
        kb.emit("vector", lambda e: e.tensor_tensor(VAR, MU, MU, op=ALU.mult), chain=True)
        kb.emit("vector", lambda e: e.tensor_tensor(VAR, S2N, VAR, op=ALU.subtract), chain=True)
        vvar = kb.cnt["V"]
        kb.emit("scalar", lambda e: e.activation(SQ, VAR, AF.Sqrt, bias=actc[0:1, 0:1], scale=1.0),
                waits=[("V", vvar)], inc="A")
        asq = kb.cnt["A"]
        kb.emit("vector", lambda e: e.reciprocal(RSTD, SQ), waits=[("A", asq)], chain=True)
        vrstd = kb.cnt["V"]
        kb.emit("vector", lambda e: e.scalar_tensor_tensor(NMR, MU, -1.0, RSTD, op0=ALU.mult, op1=ALU.mult),
                chain=True)
        vnmr = kb.cnt["V"]
        kb.emit("tensor", lambda e: e.matmul(psR[:, :], onesf[0:1, :], RSTD, start=True, stop=True),
                waits=[("V", vrstd), ("V", ev_prev)])
        kb.emit("tensor", lambda e: e.matmul(psM[:, :], onesf[0:1, :], NMR, start=True, stop=True),
                waits=[("V", vnmr)], inc="T")
        trep = kb.cnt["T"]
        for dc in range(8):
            kb.emit("vector", lambda e, dc=dc, pb=pb: e.tensor_tensor(
                xb[:, dc * 512:(dc + 1) * 512],
                xf[:, pb + dc * 512: pb + dc * 512 + 512], psR[:, :], op=ALU.mult),
                waits=[("T", trep)] if dc == 0 else [], chain=True)
            kb.emit("vector", lambda e, dc=dc, t0=t0: e.tensor_tensor(
                xn[:, dc * T + t0: dc * T + t0 + 512],
                xb[:, dc * 512:(dc + 1) * 512], psM[:, :], op=ALU.add), chain=True)
        ev_prev = kb.cnt["V"]
    kb.wait("tensor", [("DS", c_done)])
    return dict(xn=xn, ones=ones, onesf=onesf, identb=identb, maskct=maskct,
                maskc=maskc, lcum=lcum, lcum2=lcum2, i64d=i64d, nwc=nwc,
                xn_ready=kb.cnt["V"])


def phase_l1(kb, io, env, pstk):
    xn = env["xn"]
    s1c = kb.sb("l1_s1", [128, 32], F32, stack=pstk)
    s2c = kb.sb("l1_s2", [128, 32], F32, stack=pstk)
    wbb = kb.sb("l1_wbb", [128, 2 * 2 * 1024], BF16, stack=pstk)
    hsb = kb.sb("l1_h", [128, 2 * 512], BF16, stack=pstk)
    tsl = kb.sb("l1_tsil", [128, 2 * 512], BF16, stack=pstk)
    psA = [kb.ps("psl1_a%d" % i, stack=pstk) for i in range(2)]
    psB = [kb.ps("psl1_b%d" % i, stack=pstk) for i in range(2)]
    kb.emit("scalar", lambda e: e.dma_start(out=s1c[:], in_=io["s1"][:]), inc="DS")
    kb.emit("scalar", lambda e: e.dma_start(out=s2c[:], in_=io["s2"][:]), inc="DS")
    sc_ds = kb.cnt["DS"]

    xn_ready = env["xn_ready"]
    ldq = []
    mm_after_panel = {0: 0, 1: 0}
    def load_panel(oc):
        b = (oc % 2)
        sm = "DG" if b == 0 else "DH"
        kb.emit("gpsimd", lambda e, oc=oc, b=b: e.dma_start(
            out=wbb[:, b * 2048: b * 2048 + 1024], in_=io["w1t"][oc]),
            waits=[("T", mm_after_panel[b])], inc=sm)
        v = kb.emit("gpsimd", lambda e, oc=oc, b=b: e.dma_start(
            out=wbb[:, b * 2048 + 1024: b * 2048 + 2048], in_=io["w2t"][oc]), inc=sm)
        ldq.append((sm, v))

    load_panel(0)
    load_panel(1)
    ev_tail = {0: 0, 1: 0}
    evA_tail = {0: 0, 1: 0}
    st_tail = {0: 0, 1: 0}
    for oc in range(32):
        wb = oc % 2
        for tt in range(8):
            b = tt % 2
            w0 = []
            if tt == 0:
                w0 = [ldq[oc], ("V", xn_ready)]
            for dc in range(8):
                kb.emit("tensor", lambda e, dc=dc, b=b, wb=wb, t0=tt * 512, first=(dc == 0): e.matmul(
                    psA[b][:, :], wbb[:, wb * 2048 + dc * 128: wb * 2048 + dc * 128 + 128],
                    xn[:, dc * T + t0: dc * T + t0 + 512], start=first, stop=(dc == 7)),
                    waits=(w0 if dc == 0 else []) + ([("A", evA_tail[b]), ("V", ev_tail[b])] if dc == 0 else []))
            for dc in range(8):
                kb.emit("tensor", lambda e, dc=dc, b=b, wb=wb, t0=tt * 512, first=(dc == 0): e.matmul(
                    psB[b][:, :], wbb[:, wb * 2048 + 1024 + dc * 128: wb * 2048 + 1024 + dc * 128 + 128],
                    xn[:, dc * T + t0: dc * T + t0 + 512], start=first, stop=(dc == 7)),
                    inc=("T" if dc == 7 else None))
            tmm = kb.cnt["T"]
            kb.emit("scalar", lambda e, b=b, oc=oc: e.activation(
                tsl[:, b * 512:(b + 1) * 512], psA[b][:, :], AF.Silu,
                scale=s1c[:, oc:oc + 1]),
                waits=[("T", tmm), ("V", ev_tail[b])]
                + ([("DS", sc_ds)] if oc == 0 and tt == 0 else []), inc="A")
            evA_tail[b] = kb.cnt["A"]
            kb.emit("vector", lambda e, b=b, oc=oc: e.scalar_tensor_tensor(
                hsb[:, b * 512:(b + 1) * 512], psB[b][:, :], s2c[:, oc:oc + 1],
                tsl[:, b * 512:(b + 1) * 512], op0=ALU.mult, op1=ALU.mult),
                waits=[("A", evA_tail[b]), ("DS", st_tail[b])]
                + ([("DS", sc_ds)] if oc == 0 and tt == 0 else []), inc="V")
            ev_tail[b] = kb.cnt["V"]
            kb.emit("scalar", lambda e, b=b, oc=oc, t0=tt * 512: e.dma_start(
                out=io["h_dram"][oc, :, t0:t0 + 512], in_=hsb[:, b * 512:(b + 1) * 512]),
                waits=[("V", ev_tail[b])], inc="DS")
            st_tail[b] = kb.cnt["DS"]
        mm_after_panel[wb] = kb.cnt["T"]
        if oc + 2 < 32:
            load_panel(oc + 2)
    env["l1_done_ds"] = kb.cnt["DS"]
    env["l1_done_T"] = kb.cnt["T"]


def phase_p2(kb, io, env, pstk, nseg=NSEG):
    xn = env["xn"]; identb = env["identb"]
    maskct = env["maskct"]; maskc = env["maskc"]
    lcum = env["lcum"]; lcum2 = env["lcum2"]; i64d = env["i64d"]
    L = lambda n, sh, dt: kb.sb(n, sh, dt, stack=pstk)
    qkvgw = L("p2_qkvgw", [128, 8 * 3088], BF16)
    f1og1w = L("p2_f1og1", [128, 8 * 128], BF16)
    wf2w = L("p2_wf2", [64, 1024], BF16)
    wog2w = L("p2_wog2", [64, 8 * 128], BF16)
    pack = L("p2_pack", [128, 2 * 4096], BF16)
    fmp = L("p2_fmp", [128, 2 * 4096], BF16)
    vtm2 = L("p2_vtm2", [128, 2 * 2048], BF16)
    ogp = L("p2_ogp", [128, 8 * 512], BF16)
    osg = L("p2_osg", [128, 8 * 512], BF16)
    z1f = L("p2_z1", [64, 512], BF16)
    og1f = L("p2_og1", [64, 512], BF16)
    states = L("p2_S", [128, 512], BF16)
    ctpt = L("p2_ctpt", [128, 128], BF16)
    cun = L("p2_cun", [128, 64], BF16)
    powsb = L("p2_pow", [128, 2 * 128], BF16)
    xw = L("p2_xw", [128, 2 * 128], BF16)
    ctptb = L("p2_ctptb", [128, 128], BF16)
    cunb = L("p2_cunb", [128, 64], BF16)
    powsbb = L("p2_powb", [128, 2 * 128], BF16)
    xwb = L("p2_xwb", [128, 2 * 128], BF16)
    sumb = L("p2_sumb", [128, 256], BF16)
    qtmp = L("p2_qtmp", [128, 1024], BF16)
    ktmp = L("p2_ktmp", [128, 1024], BF16)
    knk = L("p2_knk", [128, 1024], F32)
    kn = L("p2_kn", [128, 64], F32)
    emF = L("p2_emF", [128, 1024], BF16)
    eagF = L("p2_eagF", [128, 1024], BF16)
    gtmp = L("p2_gt", [128, 16], F32)
    gkb = L("p2_gkb", [128, 1024], BF16)
    scr1 = L("p2_scr1", [128, 512], BF16)
    scr2 = L("p2_scr2", [128, 512], BF16)
    pg4 = kb.ps("p2_pg4", (128, 1024), stack=pstk)
    pg4b = kb.ps("p2_pg4b", (128, 1024), stack=pstk)
    px2 = kb.ps("p2_px2", (128, 1024), stack=pstk)
    ps4 = kb.ps("p2_ps4", (128, 1024), stack=pstk)
    l1_s1 = L("p2l1_s1", [128, 32], F32)
    l1_s2 = L("p2l1_s2", [128, 32], F32)
    l1_wbb = L("p2l1_wbb", [128, 2 * 1024], BF16)
    l1_hsb = L("p2l1_h", [128, 512], BF16)
    l1_tsl = L("p2l1_t", [128, 512], BF16)

    for dc in range(8):
        kb.emit("gpsimd", lambda e, dc=dc: e.dma_start(
            out=qkvgw[:, dc * 3088:(dc + 1) * 3088], in_=io["qkvg"][dc]), inc="DG")
    kb.emit("gpsimd", lambda e: e.dma_start(
        out=f1og1w[:].rearrange("p (a b) -> p a b", b=128),
        in_=io["f1og1"].rearrange("a p b -> p a b")), inc="DG")
    kb.emit("gpsimd", lambda e: e.dma_start(out=wf2w[:], in_=io["wf2"][:]), inc="DG")
    kb.emit("gpsimd", lambda e: e.dma_start(
        out=wog2w[:].rearrange("p (a b) -> p a b", b=128),
        in_=io["wog2t"].rearrange("a p b -> p a b")), inc="DG")
    wready = kb.cnt["DG"]
    kb.emit("vector", lambda e: e.memset(states[:], 0.0))
    kb.emit("vector", lambda e: e.memset(ctpt[:], 0.0))
    if P2_HEADS is not None:
        kb.emit("vector", lambda e: e.memset(osg[:], 0.0))
    kb.emit("vector", lambda e: e.memset(cun[:], 0.0), inc="V")
    kb.wait("tensor", [("DG", wready), ("V", kb.cnt["V"])])
    kb.wait("scalar", [("DG", wready)])
    kb.wait("gpsimd", [("V", kb.cnt["V"])])

    fmp4 = fmp.rearrange("p (tb ti jc r) -> p tb ti jc r", tb=2, ti=4, jc=8, r=128)

    fr = dict(psG=[], psH=[], bA=[], bB=[], bC=[], bF=[],
              qtmp=[], ktmp=[], knk=[], gtmp=[], gkb=[], emF=[], eagF=[], scr=[],
              vtm=[[], []], pack=[[], []], fmpc=[[], []], og=[], osg=[], chain=[],
              ct=[], pow=[], xwg=[], sumbg=[])
    LN = [dict(bank=pg4, xoff=0, ct=ctpt, cu=cun, pw=powsb, xw=xw),
          dict(bank=pg4b, xoff=512, ct=ctptb, cu=cunb, pw=powsbb, xw=xwb)]
    frl = [dict(bA=[], bX=[], pow=[], sumev=[], psS=[], psO=[], chain=[],
                vmask=0, vpow=0, axw=0, tg=0, tx=0, tsq=0, taj=0, tch=0, gadd=0)
           for _ in range(2)]

    def W(*gs):
        out = []
        for g in gs:
            out += g
        return out

    kb.emit("scalar", lambda e: e.dma_start(out=l1_s1[:], in_=io["s1"][:]), inc="DS")
    kb.emit("scalar", lambda e: e.dma_start(out=l1_s2[:], in_=io["s2"][:]), inc="DS")
    l1_sc_ds = kb.cnt["DS"]
    l1_ldq = []
    l1_state = {"mm_after": 0, "st": 0, "ts": 0}

    def l1_load_panel(oc):
        sm = "DG" if oc % 2 == 0 else "DH"
        kb.emit("gpsimd", lambda e, oc=oc: e.dma_start(
            out=l1_wbb[:, 0:1024], in_=io["w1t"][oc]),
            waits=[("T", l1_state["mm_after"])], inc=sm)
        v = kb.emit("gpsimd", lambda e, oc=oc: e.dma_start(
            out=l1_wbb[:, 1024:2048], in_=io["w2t"][oc]), inc=sm)
        l1_ldq.append((sm, v))

    l1_load_panel(0)
    l1_q = {"u": 0, "m": 0, "t1": 0}

    def l1_tick():
        u = l1_q["u"]
        if u >= 256:
            return
        m = l1_q["m"]
        oc, tt = u // 8, u % 8
        t0 = tt * 512
        if m < 8:
            dc = m
            w = (([l1_ldq[oc]] if tt == 0 else []) + W(fr["psG"])) if dc == 0 else []
            kb.emit("tensor", lambda e, dc=dc, t0=t0, first=(dc == 0): e.matmul(
                ps4[:, 0:512], l1_wbb[:, dc * 128: dc * 128 + 128],
                xn[:, dc * T + t0: dc * T + t0 + 512], start=first, stop=(dc == 7)),
                waits=w, inc=("T" if dc == 7 else None))
            if dc == 7:
                l1_q["t1"] = kb.cnt["T"]
        else:
            dc = m - 8
            w = W(fr["psH"]) if dc == 0 else []
            kb.emit("tensor", lambda e, dc=dc, t0=t0, first=(dc == 0): e.matmul(
                ps4[:, 512:1024], l1_wbb[:, 1024 + dc * 128: 1024 + dc * 128 + 128],
                xn[:, dc * T + t0: dc * T + t0 + 512], start=first, stop=(dc == 7)),
                waits=w, inc=("T" if dc == 7 else None))
        l1_q["m"] = m + 1
        if l1_q["m"] < 16:
            return
        l1_q["m"] = 0
        l1_q["u"] = u + 1
        t2 = kb.cnt["T"]
        t1 = l1_q["t1"]
        kb.emit("scalar", lambda e, oc=oc: e.activation(
            l1_tsl[:], ps4[:, 0:512], AF.Silu, scale=l1_s1[:, oc:oc + 1]),
            waits=[("T", t1), ("V", l1_state["ts"])]
            + ([("DS", l1_sc_ds)] if u == 0 else []), inc="A")
        a1 = kb.cnt["A"]
        fr["psG"] = [("A", a1)]
        kb.emit("vector", lambda e, oc=oc: e.scalar_tensor_tensor(
            l1_hsb[:], ps4[:, 512:1024], l1_s2[:, oc:oc + 1],
            l1_tsl[:], op0=ALU.mult, op1=ALU.mult),
            waits=[("T", t2), ("A", a1), ("DS", l1_state["st"])]
            + ([("DS", l1_sc_ds)] if u == 0 else []), inc="V")
        v1 = kb.cnt["V"]
        fr["psH"] = [("V", v1)]
        l1_state["ts"] = v1
        kb.emit("scalar", lambda e, oc=oc, t0=t0: e.dma_start(
            out=io["h_dram"][oc, :, t0:t0 + 512], in_=l1_hsb[:]),
            waits=[("V", v1)], inc="DS")
        l1_state["st"] = kb.cnt["DS"]
        if tt == 7:
            l1_state["mm_after"] = kb.cnt["T"]
            if oc + 1 < 32:
                l1_load_panel(oc + 1)

    def projgrp(tk, lo, hi, bank, bguard, waits0=()):
        t0 = tk * 128
        for dc in range(8):
            w = list(waits0) + W(bguard) if dc == 0 else []
            kb.emit("tensor", lambda e, dc=dc, t0=t0, lo=lo, hi=hi, bank=bank,
                    first=(dc == 0): e.matmul(
                bank[:, 0:hi - lo], xn[:, dc * T + t0:dc * T + t0 + 128],
                qkvgw[:, dc * 3088 + lo:dc * 3088 + hi],
                start=first, stop=(dc == 7)),
                waits=w, inc=("T" if dc == 7 else None))
        return kb.cnt["T"]

    bG = lambda: ps4[:, 0:512]
    bH = lambda: ps4[:, 512:1024]
    bA = lambda: pg4[:, 0:512]
    bB = lambda: pg4[:, 512:1024]
    bC = lambda: px2[:, 0:512]
    bF = lambda: px2[:, 512:1024]

    for s in range(nseg):
        s0 = s * 512
        for dc in range(8):
            kb.emit("tensor", lambda e, dc=dc, s0=s0, first=(dc == 0): e.matmul(
                bG()[:, 0:512], f1og1w[:, dc * 128:(dc + 1) * 128],
                xn[:, dc * T + s0:dc * T + s0 + 512], start=first, stop=(dc == 7)),
                waits=W(fr["psG"]) if dc == 0 else [], inc=("T" if dc == 7 else None))
        tz = kb.cnt["T"]
        kb.emit("scalar", lambda e: e.copy(z1f[:], bG()[0:64, 0:512]), waits=[("T", tz)])
        kb.emit("scalar", lambda e: e.copy(og1f[:], bG()[64:128, 0:512]), inc="A")
        fr["psG"] = [("A", kb.cnt["A"])]
        az = kb.cnt["A"]
        for oc in range(8):
            w0 = ([("A", az)] + W(fr["psH"])) if oc == 0 else [("A", kb.cnt["A"])]
            kb.emit("tensor", lambda e, oc=oc: e.matmul(
                bH()[:, 0:512], wog2w[:, oc * 128:(oc + 1) * 128], og1f[:],
                start=True, stop=True), waits=w0, inc="T")
            tog = kb.cnt["T"]
            kb.emit("scalar", lambda e, oc=oc, tog=tog: e.activation(
                ogp[:, oc * 512:(oc + 1) * 512], bH()[:, 0:512], AF.Sigmoid),
                waits=[("T", tog)] + W(fr["og"]), inc="A")
        fr["psH"] = [("A", kb.cnt["A"])]

        for tk4 in range(TPS):
            tk = s * TPS + tk4
            tb = tk % 2
            t0 = tk * 128
            if P2_TSEC < 1:
                continue
            tq = projgrp(tk, 0, 512, bG(), fr["psG"])
            kb.emit("scalar", lambda e, tq=tq: e.activation(
                qtmp[:, 0:512], bG()[:, 0:512], AF.Silu), waits=[("T", tq)] + W(fr["qtmp"]), inc="A")
            fr["psG"] = [("A", kb.cnt["A"])]
            tq = projgrp(tk, 512, 1024, bH(), fr["psH"])
            kb.emit("scalar", lambda e, tq=tq: e.activation(
                qtmp[:, 512:1024], bH()[:, 0:512], AF.Silu), waits=[("T", tq)], inc="A")
            fr["psH"] = [("A", kb.cnt["A"])]
            tq = projgrp(tk, 1024, 1536, bG(), fr["psG"])
            kb.emit("scalar", lambda e, tq=tq: e.activation(
                ktmp[:, 0:512], bG()[:, 0:512], AF.Silu), waits=[("T", tq)] + W(fr["ktmp"]), inc="A")
            fr["psG"] = [("A", kb.cnt["A"])]
            tq = projgrp(tk, 1536, 2048, bH(), fr["psH"])
            kb.emit("scalar", lambda e, tq=tq: e.activation(
                ktmp[:, 512:1024], bH()[:, 0:512], AF.Silu), waits=[("T", tq)], inc="A")
            fr["psH"] = [("A", kb.cnt["A"])]
            kready = kb.cnt["A"]
            for half in range(2):
                tq = projgrp(tk, 2048 + half * 512, 2560 + half * 512,
                             bG() if half == 0 else bH(),
                             fr["psG"] if half == 0 else fr["psH"])
                base = half * 1024
                kb.emit("vector", lambda e, tq=tq, base=base, half=half, tb=tb: e.tensor_copy(
                    vtm2[:, tb * 2048:(tb + 1) * 2048].rearrange(
                        "p (q c) -> p q c", c=64)[:, (base // 64 + 1)::4, :][:, 0:4, :],
                    (bG() if half == 0 else bH())[:, 0:512].rearrange(
                        "p (a b) -> p a b", b=64)[:, 0::2, :]),
                    waits=[("T", tq)] + (W(fr["vtm"][tb]) if half == 0 else []), inc="V")
                kb.emit("vector", lambda e, base=base, half=half, tb=tb: e.tensor_copy(
                    vtm2[:, tb * 2048:(tb + 1) * 2048].rearrange(
                        "p (q c) -> p q c", c=64)[:, (base // 64 + 2)::4, :][:, 0:4, :],
                    (bG() if half == 0 else bH())[:, 0:512].rearrange(
                        "p (a b) -> p a b", b=64)[:, 1::2, :]), inc="V")
                if half == 0:
                    fr["psG"] = [("V", kb.cnt["V"])]
                else:
                    fr["psH"] = [("V", kb.cnt["V"])]
            tq = projgrp(tk, 3072, 3088, bG(), fr["psG"])
            kb.emit("scalar", lambda e, tq=tq: e.activation(
                gtmp[:, 0:16], bG()[:, 0:16], AF.Sigmoid),
                waits=[("T", tq)] + W(fr["gtmp"]), inc="A")
            fr["psG"] = [("A", kb.cnt["A"])]
            if P2_TSEC < 2:
                continue
            for hf in range(2):
                bank, bn = (bH(), "psH") if hf == 0 else (bG(), "psG")
                kb.emit("tensor", lambda e, hf=hf, tk4=tk4, bank=bank: e.matmul(
                    bank[:, 0:512], z1f[:, tk4 * 128:(tk4 + 1) * 128],
                    wf2w[:, hf * 512:(hf + 1) * 512], start=True, stop=True),
                    waits=W(fr[bn], fr["scr"], fr["gkb"]), inc="T")
                tzz = kb.cnt["T"]
                kb.emit("scalar", lambda e, bank=bank: e.activation(scr1[:], bank[:, 0:512], AF.Abs),
                        waits=[("T", tzz)])
                aabs = kb.cnt["A"]
                kb.emit("scalar", lambda e: e.activation(scr1[:], scr1[:], AF.Exp, scale=-1.0))
                kb.emit("scalar", lambda e: e.activation(
                    scr1[:], scr1[:], AF.Ln, bias=kb.tens["c_actc"][:, 1:2]), inc="A")
                aln = kb.cnt["A"]
                kb.emit("vector", lambda e, bank=bank: e.tensor_scalar(
                    scr2[:], bank[:, 0:512], 0.0, None, op0=ALU.min),
                    waits=[("T", tzz), ("A", aabs)])
                kb.emit("vector", lambda e, hf=hf: e.tensor_tensor(
                    gkb[:, hf * 512:(hf + 1) * 512], scr2[:], scr1[:], op=ALU.subtract),
                    waits=[("A", aln)], inc="V")
                fr[bn] = [("V", kb.cnt["V"])]
                fr["scr"] = [("V", kb.cnt["V"])]
            vgk = kb.cnt["V"]
            if P2_TSEC < 3:
                continue
            for hf in range(2):
                for ch in (0, 64):
                    kb.emit("tensor", lambda e, ch=ch, hf=hf: e.matmul(
                        bG()[ch:ch + 64, 0:512], lcum[ch:ch + 64, :],
                        gkb[ch:ch + 64, hf * 512:(hf + 1) * 512],
                        start=True, stop=True, tile_position=(ch, ch), skip_group_check=True),
                        waits=([("V", vgk)] + W(fr["psG"], fr["emF"], fr["pack"][tb])
                               if ch == 0 else []))
                    kb.emit("tensor", lambda e, ch=ch, hf=hf: e.matmul(
                        bH()[ch:ch + 64, 0:512], lcum2[ch:ch + 64, :],
                        gkb[ch:ch + 64, hf * 512:(hf + 1) * 512],
                        start=True, stop=True, tile_position=(ch, ch), skip_group_check=True),
                        waits=(W(fr["psH"], fr["eagF"]) if ch == 0 else []),
                        inc=("T" if ch == 64 else None))
                tcs = kb.cnt["T"]
                kb.emit("scalar", lambda e, hf=hf, tb=tb, tcs=tcs: e.activation(
                    pack[:, tb * 4096 + 3 * 1024 + hf * 512: tb * 4096 + 3 * 1024 + hf * 512 + 512],
                    bG()[:, 0:512], AF.Exp), waits=[("T", tcs)], inc="A")
                kb.emit("scalar", lambda e, hf=hf: e.activation(
                    emF[:, hf * 512:(hf + 1) * 512], bG()[:, 0:512], AF.Exp, scale=-1.0), inc="A")
                fr["psG"] = [("A", kb.cnt["A"])]
                fr["emF"] = [("A", kb.cnt["A"])]
                kb.emit("scalar", lambda e, hf=hf: e.activation(
                    eagF[:, hf * 512:(hf + 1) * 512], bH()[:, 0:512], AF.Exp), inc="A")
                fr["psH"] = [("A", kb.cnt["A"])]
                fr["eagF"] = [("A", kb.cnt["A"])]
            aexp = kb.cnt["A"]
            if P2_TSEC < 4:
                continue
            kb.emit("scalar", lambda e: e.square(knk[:], ktmp[:]),
                    waits=[("A", kready)] + W(fr["knk"]), inc="A")
            fr["ktmp"] = [("A", kb.cnt["A"])]
            asq = kb.cnt["A"]
            kb.emit("vector", lambda e: e.reduce_sum(
                kn[:, 0:16], knk.rearrange("p (h j) -> p h j", h=16)[:, :, :], axis=AX.X),
                waits=[("A", asq)], inc="V")
            vkn = kb.cnt["V"]
            kb.emit("scalar", lambda e: e.activation(kn[:, 16:32], kn[:, 0:16], AF.Sqrt),
                    waits=[("V", vkn)], inc="A")
            akn = kb.cnt["A"]
            kb.emit("vector", lambda e: e.tensor_scalar(
                kn[:, 32:48], kn[:, 16:32], 1e-12, None, op0=ALU.max), waits=[("A", akn)])
            kb.emit("vector", lambda e: e.reciprocal(kn[:, 48:64], kn[:, 32:48]))
            kb.emit("vector", lambda e: e.tensor_tensor(
                knk.rearrange("p (h j) -> p h j", h=16),
                ktmp.rearrange("p (h j) -> p h j", h=16),
                kn[:, 48:64].unsqueeze(-1).broadcast_to((128, 16, 64)), op=ALU.mult), inc="V")
            fr["ktmp"] = fr["ktmp"] + [("V", kb.cnt["V"])]
            kb.emit("vector", lambda e, tb=tb: e.tensor_tensor(
                pack[:, tb * 4096:tb * 4096 + 1024], knk[:], emF[:], op=ALU.mult),
                waits=[("A", aexp)], inc="V")
            fr["emF"] = [("V", kb.cnt["V"])]
            kb.emit("vector", lambda e: e.scalar_tensor_tensor(
                eagF.rearrange("p (h j) -> p h j", h=16),
                eagF.rearrange("p (h j) -> p h j", h=16), -1.0,
                gtmp[:, 0:16].unsqueeze(-1).broadcast_to((128, 16, 64)),
                op0=ALU.mult, op1=ALU.mult))
            fr["gtmp"] = [("V", kb.cnt["V"])]
            kb.emit("vector", lambda e, tb=tb: e.tensor_tensor(
                pack[:, tb * 4096 + 1024:tb * 4096 + 2048], knk[:], eagF[:], op=ALU.mult), inc="V")
            fr["eagF"] = [("V", kb.cnt["V"])]
            fr["knk"] = [("V", kb.cnt["V"])]
            kb.emit("vector", lambda e, tb=tb: e.tensor_copy(
                vtm2[:, tb * 2048:(tb + 1) * 2048].rearrange("p (q c) -> p q c", c=64)[:, 0::4, :],
                pack[:, tb * 4096 + 1024:tb * 4096 + 2048].rearrange(
                    "p (a b) -> p a b", b=64)[:, 0::2, :]), inc="V")
            kb.emit("vector", lambda e, tb=tb: e.tensor_copy(
                vtm2[:, tb * 2048:(tb + 1) * 2048].rearrange("p (q c) -> p q c", c=64)[:, 3::4, :],
                pack[:, tb * 4096 + 1024:tb * 4096 + 2048].rearrange(
                    "p (a b) -> p a b", b=64)[:, 1::2, :]), inc="V")
            kb.emit("vector", lambda e, tb=tb: e.tensor_tensor(
                pack[:, tb * 4096 + 2 * 1024:tb * 4096 + 3 * 1024], qtmp[:],
                pack[:, tb * 4096 + 3 * 1024:tb * 4096 + 4 * 1024], op=ALU.mult), inc="V")
            fr["qtmp"] = [("V", kb.cnt["V"])]
            vpack = kb.cnt["V"]
            apack = kb.cnt["A"]
            if P2_TSEC < 5:
                continue
            kb.emit("sync", lambda e, tb=tb: e.dma_start_transpose(
                fmp[:, tb * 4096:(tb + 1) * 4096].rearrange("p (m r) -> p m r", r=128),
                pack[:, tb * 4096:(tb + 1) * 4096]),
                waits=[("V", vpack), ("A", apack)] + W(fr["fmpc"][tb]), inc="DT")
            dtf = kb.cnt["DT"]
            kb.wait("tensor", [("DT", dtf)])
            hl = list(range(16)) if P2_HEADS is None else list(P2_HEADS)
            pairs = [tuple(hl[i:i + 2]) for i in range(0, len(hl), 2)]
            for l in (0, 1):
                frl[l]["psS"] = frl[l]["psS"] + fr["psG"]
                frl[l]["psO"] = frl[l]["psO"] + fr["psH"]
            for pair in pairs:
                lanes = [(h % 2, h) for h in pair]
                l1_tick()
                for l, h in lanes:
                    fl = frl[l]; B = LN[l]["bank"]
                    jr = (h % 2) * 64; jc = h // 2
                    kt0 = tb * 4096 + jc * 128
                    at0 = tb * 4096 + 1024 + jc * 128
                    for cpar in range(2):
                        r0 = cpar * 64
                        kb.emit("tensor", lambda e, jr=jr, jc=jc, kt0=kt0, r0=r0, tb=tb, B=B: e.matmul(
                            B[r0:r0 + 64, 0:128],
                            fmp[jr:jr + 64, kt0 + r0:kt0 + r0 + 64],
                            fmp4[jr:jr + 64, tb, 1:3, jc, r0:r0 + 64],
                            start=True, stop=True, tile_position=(jr, r0), skip_group_check=True),
                            waits=(W(fl["bA"], fl["psS"], fl["psO"]) if cpar == 0 else []))
                        kb.emit("tensor", lambda e, jr=jr, kt0=kt0, at0=at0, r0=r0, B=B: e.matmul(
                            B[r0:r0 + 64, 512:576],
                            fmp[jr:jr + 64, at0 + r0:at0 + r0 + 64],
                            fmp[jr:jr + 64, kt0 + r0:kt0 + r0 + 64],
                            start=True, stop=True, tile_position=(jr, r0), skip_group_check=True),
                            inc=("T" if cpar == 1 else None))
                    fl["tg"] = kb.cnt["T"]
                l1_tick()
                for l, h in lanes:
                    fl = frl[l]; B = LN[l]["bank"]
                    kb.emit("vector", lambda e, B=B, ct=LN[l]["ct"]: e.tensor_tensor(
                        ct[:], B[:, 0:128], maskct[:], op=ALU.mult),
                        waits=[("T", fl["tg"])], inc="V")
                    kb.emit("vector", lambda e, B=B, cu=LN[l]["cu"]: e.tensor_tensor(
                        cu[:], B[:, 512:576], maskc[:], op=ALU.mult), inc="V")
                    fl["vmask"] = kb.cnt["V"]
                    fl["pow"] = [("V", fl["vmask"])]
                for l, h in lanes:
                    fl = frl[l]; xo = LN[l]["xoff"]
                    for cpar in range(2):
                        r0 = cpar * 64
                        kb.emit("tensor", lambda e, h=h, r0=r0, xo=xo, tb=tb: e.matmul(
                            px2[r0:r0 + 64, xo:xo + 128], identb[r0:r0 + 64, r0:r0 + 64],
                            vtm2[r0:r0 + 64, tb * 2048 + h * 128:tb * 2048 + h * 128 + 128],
                            start=True, stop=True, tile_position=(r0, r0), skip_group_check=True),
                            waits=(W(fl["bX"]) if cpar == 0 else []))
                        kb.emit("tensor", lambda e, h=h, r0=r0, xo=xo, tb=tb, ct=LN[l]["ct"]: e.matmul(
                            px2[r0:r0 + 64, xo:xo + 128], ct[r0:r0 + 64, 0:64],
                            vtm2[r0:r0 + 64, tb * 2048 + h * 128:tb * 2048 + h * 128 + 128],
                            start=False, stop=True, tile_position=(r0, r0), skip_group_check=True),
                            waits=([("V", fl["vmask"])] if cpar == 0 else []),
                            inc=("T" if cpar == 1 else None))
                    fl["tx"] = kb.cnt["T"]
                for l, h in lanes:
                    fl = frl[l]; xo = LN[l]["xoff"]
                    kb.emit("scalar", lambda e, xo=xo, xwl=LN[l]["xw"]: e.copy(
                        xwl[:, 128:256], px2[:, xo:xo + 128]),
                        waits=[("T", fl["tx"])], inc="A")
                    fl["axw"] = kb.cnt["A"]
                for j in range(1, 6):
                    l1_tick()
                    for l, h in lanes:
                        fl = frl[l]; B = LN[l]["bank"]; pw = LN[l]["pw"]
                        for cpar in range(2):
                            r0 = cpar * 64
                            if j == 1:
                                ctp = LN[l]["ct"][r0:r0 + 64, 0:64]
                                cpp = LN[l]["cu"][r0:r0 + 64, 0:64]
                            else:
                                pbq = ((j - 1) % 2) * 128
                                ctp = pw[r0:r0 + 64, pbq:pbq + 64]
                                cpp = pw[r0:r0 + 64, pbq + 64:pbq + 128]
                            kb.emit("tensor", lambda e, ctp=ctp, cpp=cpp, r0=r0, B=B: e.matmul(
                                B[r0:r0 + 64, 0:64], cpp, ctp,
                                start=True, stop=True, tile_position=(r0, r0), skip_group_check=True),
                                waits=(W(fl["pow"]) if cpar == 0 else []))
                            kb.emit("tensor", lambda e, ctp=ctp, cpp=cpp, r0=r0, B=B: e.matmul(
                                B[r0:r0 + 64, 512:576], ctp, cpp,
                                start=True, stop=True, tile_position=(r0, r0), skip_group_check=True),
                                inc=("T" if cpar == 1 else None))
                        fl["tsq"] = kb.cnt["T"]
                    for l, h in lanes:
                        fl = frl[l]; B4 = LN[l]["bank"]; pw = LN[l]["pw"]
                        pb2 = (j % 2) * 128
                        kb.emit("vector", lambda e, pb2=pb2, B4=B4, pw=pw, tsq=fl["tsq"]: e.tensor_copy(
                            pw[:, pb2:pb2 + 128].rearrange("p (a b) -> p a b", b=64),
                            B4.rearrange("p (a b) -> p a b", b=64)[:, 0::8, :]),
                            waits=[("T", fl["tsq"])], inc="V")
                        fl["vpow"] = kb.cnt["V"]
                        fl["pow"] = [("V", fl["vpow"])]
                    l1_tick()
                    for l, h in lanes:
                        fl = frl[l]; pw = LN[l]["pw"]; xwl = LN[l]["xw"]; xo = LN[l]["xoff"]
                        pb2 = (j % 2) * 128
                        for cpar in range(2):
                            r0 = cpar * 64
                            lh = pw[r0:r0 + 64, pb2:pb2 + 64]
                            rh = xwl[r0:r0 + 64, (j % 2) * 128:(j % 2) * 128 + 128]
                            kb.emit("tensor", lambda e, lh=lh, rh=rh, r0=r0, xo=xo: e.matmul(
                                px2[r0:r0 + 64, xo:xo + 128], lh, rh,
                                start=False, stop=True, tile_position=(r0, r0), skip_group_check=True),
                                waits=([("V", fl["vpow"]), ("A", fl["axw"])] if cpar == 0 else []),
                                inc=("T" if cpar == 1 else None))
                        fl["taj"] = kb.cnt["T"]
                    for l, h in lanes:
                        fl = frl[l]; xwl = LN[l]["xw"]; xo = LN[l]["xoff"]
                        kb.emit("scalar", lambda e, j=j, xo=xo, xwl=xwl, t=fl["taj"]: e.copy(
                            xwl[:, ((j + 1) % 2) * 128:((j + 1) % 2) * 128 + 128],
                            px2[:, xo:xo + 128]),
                            waits=[("T", fl["taj"])], inc="A")
                        fl["axw"] = kb.cnt["A"]
                for cpar in range(2):
                    r0 = cpar * 64
                    l1_tick()
                    for l, h in lanes:
                        fl = frl[l]; B = LN[l]["bank"]; xwl = LN[l]["xw"]
                        jr = (h % 2) * 64; jc = h // 2
                        jo = jr ^ 64
                        wa_off = 0 if h % 2 == 0 else 64
                        wv_off = 64 if h % 2 == 0 else 0
                        wv = xwl[r0:r0 + 64, wv_off:wv_off + 64]
                        wa = xwl[r0:r0 + 64, wa_off:wa_off + 64]
                        ktm = pack[r0:r0 + 64, tb * 4096 + h * 64:tb * 4096 + h * 64 + 64]
                        ptc = LN[l]["ct"][r0:r0 + 64, 64:128]
                        w0 = (([("V", fl["vpow"]), ("A", fl["axw"])] if cpar == 0
                               else W(fl["psS"], fl["psO"])) + W(fl["sumev"]))
                        kb.emit("tensor", lambda e, r0=r0, jr=jr, wv=wv, ptc=ptc, B=B: e.matmul(
                            B[jr:jr + 64, 0:64], wv, ptc,
                            start=True, stop=True, tile_position=(r0, jr), skip_group_check=True),
                            waits=w0)
                        kb.emit("tensor", lambda e, r0=r0, jr=jr, wa=wa, ptc=ptc, B=B: e.matmul(
                            B[jr:jr + 64, 512:576], wa, ptc,
                            start=True, stop=True, tile_position=(r0, jr), skip_group_check=True))
                        kb.emit("tensor", lambda e, r0=r0, jo=jo, ktm=ktm, wv=wv, B=B: e.matmul(
                            B[jo:jo + 64, 0:64], ktm, wv,
                            start=True, stop=True, tile_position=(r0, jo), skip_group_check=True))
                        kb.emit("tensor", lambda e, r0=r0, jo=jo, ktm=ktm, wa=wa, B=B: e.matmul(
                            B[jo:jo + 64, 512:576], wa, ktm,
                            start=True, stop=True, tile_position=(r0, jo), skip_group_check=True),
                            inc="T")
                        ts1 = kb.cnt["T"]
                        kb.emit("scalar", lambda e, jr=jr, ts1=ts1, B=B: e.copy(
                            sumb[jr:jr + 64, 0:64], B[jr:jr + 64, 0:64]),
                            waits=[("T", ts1)])
                        kb.emit("scalar", lambda e, jr=jr, B=B: e.copy(
                            sumb[jr:jr + 64, 64:128], B[jr:jr + 64, 512:576]))
                        kb.emit("scalar", lambda e, jr=jr, jo=jo, B=B: e.copy(
                            sumb[jr:jr + 64, 128:192], B[jo:jo + 64, 0:64]))
                        kb.emit("scalar", lambda e, jr=jr, jo=jo, B=B: e.copy(
                            sumb[jr:jr + 64, 192:256], B[jo:jo + 64, 512:576]), inc="A")
                        as2 = kb.cnt["A"]
                        fl["sumev"] = [("A", as2)]
                        fl["bA"] = [("A", as2)]
                    l1_tick()
                    for l, h in lanes:
                        fl = frl[l]; B = LN[l]["bank"]
                        jr = (h % 2) * 64; jc = h // 2
                        qbT = fmp[jr:jr + 64, tb * 4096 + 2 * 1024 + jc * 128 + r0:
                                  tb * 4096 + 2 * 1024 + jc * 128 + r0 + 64]
                        kb.emit("tensor", lambda e, jr=jr, jc=jc, B=B: e.matmul(
                            B[jr:jr + 64, 0:64], identb[jr:jr + 64, jr:jr + 64],
                            states[jr:jr + 64, jc * 64:jc * 64 + 64],
                            start=True, stop=False, tile_position=(jr, jr), skip_group_check=True),
                            waits=W(fl["sumev"], fl["psS"], fl["chain"]))
                        kb.emit("tensor", lambda e, jr=jr, jc=jc, B=B: e.matmul(
                            B[jr:jr + 64, 0:64], sumb[jr:jr + 64, 192:256],
                            states[jr:jr + 64, jc * 64:jc * 64 + 64],
                            start=False, stop=False, tile_position=(jr, jr), skip_group_check=True))
                        kb.emit("tensor", lambda e, jr=jr, B=B: e.matmul(
                            B[jr:jr + 64, 0:64], identb[jr:jr + 64, jr:jr + 64],
                            sumb[jr:jr + 64, 128:192],
                            start=False, stop=True, tile_position=(jr, jr), skip_group_check=True))
                        kb.emit("tensor", lambda e, jr=jr, jc=jc, B=B: e.matmul(
                            B[jr:jr + 64, 512:576], states[jr:jr + 64, jc * 64:jc * 64 + 64],
                            sumb[jr:jr + 64, 64:128],
                            start=True, stop=False, tile_position=(jr, jr), skip_group_check=True),
                            waits=W(fl["psO"]))
                        kb.emit("tensor", lambda e, jr=jr, jc=jc, qbT=qbT, B=B: e.matmul(
                            B[jr:jr + 64, 512:576], states[jr:jr + 64, jc * 64:jc * 64 + 64],
                            qbT,
                            start=False, stop=False, tile_position=(jr, jr), skip_group_check=True))
                        kb.emit("tensor", lambda e, jr=jr, B=B: e.matmul(
                            B[jr:jr + 64, 512:576], identb[jr:jr + 64, jr:jr + 64],
                            sumb[jr:jr + 64, 0:64],
                            start=False, stop=True, tile_position=(jr, jr), skip_group_check=True),
                            inc="T")
                        fl["tch"] = kb.cnt["T"]
                    for l, h in lanes:
                        fl = frl[l]; B = LN[l]["bank"]
                        jr = (h % 2) * 64; jc = h // 2
                        exoff = tb * 4096 + 3 * 1024 + jc * 128 + r0 + 63
                        kb.emit("vector", lambda e, jr=jr, jc=jc, exoff=exoff, B=B: e.tensor_tensor(
                            states[jr:jr + 64, jc * 64:jc * 64 + 64], B[jr:jr + 64, 0:64],
                            fmp[jr:jr + 64, exoff:exoff + 1].broadcast_to((64, 64)),
                            op=ALU.mult), waits=[("T", fl["tch"])], inc="V")
                        fl["psS"] = [("V", kb.cnt["V"])]
                        ocol = jc * 512 + tk4 * 128 + r0
                        kb.emit("vector", lambda e, jr=jr, ocol=ocol, B=B: e.tensor_tensor(
                            osg[jr:jr + 64, ocol:ocol + 64], B[jr:jr + 64, 512:576],
                            ogp[jr:jr + 64, ocol:ocol + 64],
                            op=ALU.mult), waits=W(fr["osg"]), inc="V")
                        fl["psO"] = [("V", kb.cnt["V"])]
                        fl["chain"] = [("V", kb.cnt["V"])]
                for l, h in lanes:
                    frl[l]["bX"] = [("A", frl[l]["axw"])]
            fr["og"] = [("V", kb.cnt["V"])]
            fr["psG"] = [("V", kb.cnt["V"])]
            fr["psH"] = [("V", kb.cnt["V"])]
            fr["vtm"][tb] = [("T", kb.cnt["T"])]
            fr["pack"][tb] = [("T", kb.cnt["T"]), ("DT", kb.cnt["DT"])]
            fr["fmpc"][tb] = [("T", kb.cnt["T"]), ("V", kb.cnt["V"]), ("G", kb.cnt["G"])]
            fr["gkb"] = [("T", kb.cnt["T"])]
        vlast = kb.cnt["V"]
        for jc in range(8):
            kb.emit("scalar", lambda e, jc=jc, s0=s0: e.dma_start(
                out=io["o_dram"][jc, :, s0:s0 + 512], in_=osg[:, jc * 512:(jc + 1) * 512]),
                waits=([("V", vlast)] if jc == 0 else []), inc="DS")
        fr["osg"] = [("DS", kb.cnt["DS"])]
    while l1_q["u"] < 256:
        l1_tick()
    env["p2_done"] = True


def phase_hidden(kb, io, in_name, out_name, wt_name, sc_name, pstk):
    tag = wt_name
    L = lambda n, sh, dt: kb.sb(n + tag, sh, dt, stack=pstk)
    hblk = L("lh_hblk", [128, 32 * 2048], BF16)
    wbb = L("lh_wbb", [128, 2 * 4096], BF16)
    gout = L("lh_gout", [128, 2 * 512], BF16)
    scc = L("lh_sc", [128, 32], F32)
    psb = [kb.ps("lh_ps%d%s" % (i, tag), stack=pstk) for i in range(2)]
    kb.emit("scalar", lambda e: e.dma_start(out=scc[:], in_=io[sc_name][:]), inc="DS")
    scc_ds = kb.cnt["DS"]

    steps = [(TT, oc) for TT in range(2) for oc in range(32)]
    if LH_LIM is not None:
        steps = steps[:LH_LIM]
    ldq = {}
    st_tail = {0: 0, 1: 0}
    evA_tail = {0: 0, 1: 0}
    mm_tail = {0: 0, 1: 0}
    hld = {}

    def load_for(i):
        if i >= len(steps) or i in ldq:
            return
        TT, oc = steps[i]
        b = i % 2
        sm = "DG" if b == 0 else "DH"
        v = kb.emit("gpsimd", lambda e, oc=oc, b=b: e.dma_start(
            out=wbb[:, b * 4096:(b + 1) * 4096], in_=io[wt_name][oc]),
            waits=[("T", mm_tail[b])], inc=sm)
        ldq[i] = (sm, v)

    load_for(0)
    for i, (TT, oc) in enumerate(steps):
        b = i % 2
        if oc == 0:
            tguard = kb.cnt["T"]
            for dc in range(32):
                kb.emit("scalar", lambda e, dc=dc, TT=TT: e.dma_start(
                    out=hblk[:, dc * 2048:(dc + 1) * 2048],
                    in_=io[in_name][dc, :, TT * 2048:TT * 2048 + 2048]),
                    waits=([("T", tguard)] if dc == 0 else []), inc="DS")
            hld[TT] = kb.cnt["DS"]
        load_for(i + 1)
        for ts in range(4):
            pb = ts % 2
            w0 = ([ldq[i], ("DS", hld[TT])] if ts == 0 else []) + [("A", evA_tail[pb])]
            for dc in range(32):
                kb.emit("tensor", lambda e, dc=dc, b=b, pb=pb, ts=ts, TT=TT, first=(dc == 0): e.matmul(
                    psb[pb][:, :], wbb[:, b * 4096 + dc * 128:b * 4096 + dc * 128 + 128],
                    hblk[:, dc * 2048 + ts * 512:dc * 2048 + ts * 512 + 512],
                    start=first, stop=(dc == 31)),
                    waits=w0 if dc == 0 else [], inc=("T" if dc == 31 else None))
            tmm = kb.cnt["T"]
            kb.emit("scalar", lambda e, pb=pb, oc=oc, tmm=tmm: e.activation(
                gout[:, pb * 512:(pb + 1) * 512], psb[pb][:, :], AF.Gelu,
                scale=scc[:, oc:oc + 1]),
                waits=[("T", tmm), ("DS", st_tail[pb])]
                + ([("DS", scc_ds)] if i == 0 and ts == 0 else []), inc="A")
            evA_tail[pb] = kb.cnt["A"]
            kb.emit("scalar", lambda e, pb=pb, oc=oc, t0=TT * 2048 + ts * 512: e.dma_start(
                out=io[out_name][oc, :, t0:t0 + 512], in_=gout[:, pb * 512:(pb + 1) * 512]),
                waits=[("A", evA_tail[pb])], inc="DS")
            st_tail[pb] = kb.cnt["DS"]
        mm_tail[b] = kb.cnt["T"]


def phase_final(kb, io, pstk):
    L = lambda n, sh, dt: kb.sb(n, sh, dt, stack=pstk)
    w3b = L("f_w3b", [128, 8 * 4096], BF16)
    wob = L("f_wob", [128, 8 * 1024], BF16)
    g2b = L("f_g2b", [128, 2 * 32 * 512], BF16)
    olod = L("f_olod", [128, 8 * 512], BF16)
    olnb = L("f_oln", [128, 8 * 512], BF16)
    osq = L("f_osq", [128, 512], BF16)
    ysb = L("f_ysb", [128, 2 * 512], F32)
    xfb = L("f_xfb", [128, 2 * 512], F32)
    stt = L("f_stats", [2, 6 * 512], F32)
    s3c = L("f_s3", [128, 8], F32)
    ones = kb.tens["c_ones"]
    onesf = kb.tens["c_onesf"]
    nwc = kb.tens["c_nw"]
    pFs = [kb.ps("f_pF%d" % i, stack=pstk) for i in range(2)]
    pWs = [kb.ps("f_pW%d" % i, stack=pstk) for i in range(2)]
    pS = kb.ps("f_pS", (2, 512), stack=pstk)
    pS2 = kb.ps("f_pS2", (2, 512), stack=pstk)
    pR = kb.ps("f_pR", stack=pstk)
    pM = kb.ps("f_pM", stack=pstk)

    kb.emit("scalar", lambda e: e.dma_start(out=s3c[:], in_=io["s3"][:]), inc="DS")
    for oc in range(8):
        kb.emit("gpsimd", lambda e, oc=oc: e.dma_start(
            out=w3b[:, oc * 4096:(oc + 1) * 4096], in_=io["w3t"][oc]), inc="DG")
        kb.emit("gpsimd", lambda e, oc=oc: e.dma_start(
            out=wob[:, oc * 1024:(oc + 1) * 1024], in_=io["wot"][oc]), inc="DG")
    cast_done = kb.cnt["DG"]
    kb.wait("tensor", [("DG", cast_done)])

    MU, S2N, VAR, SQ, RSTD, NMU = (stt[0:1, i * 512:(i + 1) * 512] for i in range(6))
    ev_prev = 0
    yst_tail = {0: 0, 1: 0}
    aw_tail = {0: 0, 1: 0}
    g2_rd = {0: 0, 1: 0}
    for tt in range(8):
        t0 = tt * 512
        gb = (tt % 2)
        for jc in range(8):
            kb.emit("scalar", lambda e, jc=jc, t0=t0: e.dma_start(
                out=olod[:, jc * 512:(jc + 1) * 512], in_=io["o_dram"][jc, :, t0:t0 + 512]),
                waits=([("V", ev_prev)] if jc == 0 else []), inc="DS")
        old = kb.cnt["DS"]
        gsm = "DH" if gb == 0 else "DG"
        for dc in range(32):
            w = [("T", g2_rd[gb])] if dc == 0 else []
            if dc == 0 and gsm == "DG" and tt == 1:
                w = w + [("DG", cast_done)]
            kb.emit("gpsimd", lambda e, dc=dc, t0=t0, gb=gb: e.dma_start(
                out=g2b[:, gb * 16384 + dc * 512:gb * 16384 + (dc + 1) * 512],
                in_=io["h_dram"][dc, :, t0:t0 + 512]),
                waits=w, inc=gsm)
        g2ld = (gsm, kb.cnt[gsm])
        sq_guard = 0
        for jc in range(8):
            kb.emit("scalar", lambda e, jc=jc: e.square(
                osq[:], olod[:, jc * 512:(jc + 1) * 512]),
                waits=([("DS", old)] if jc == 0 else []) + [("T", sq_guard)], inc="A")
            aq = kb.cnt["A"]
            kb.emit("tensor", lambda e, jc=jc, first=(jc == 0): e.matmul(
                pS[0:1, :], ones[:, 0:1], olod[:, jc * 512:(jc + 1) * 512],
                start=first, stop=(jc == 7)), waits=[("DS", old)] if jc == 0 else [])
            kb.emit("tensor", lambda e, jc=jc, first=(jc == 0), aq=aq: e.matmul(
                pS2[0:1, :], ones[:, 0:1], osq[:],
                start=first, stop=(jc == 7)), waits=[("A", aq)], inc="T")
            sq_guard = kb.cnt["T"]
        tsum = kb.cnt["T"]
        kb.emit("vector", lambda e: e.tensor_scalar(MU, pS[0:1, :], 1.0 / D, None, op0=ALU.mult),
                waits=[("T", tsum)], chain=True)
        kb.emit("vector", lambda e: e.tensor_scalar(S2N, pS2[0:1, :], 1.0 / D, None, op0=ALU.mult),
                chain=True)
        kb.emit("vector", lambda e: e.tensor_tensor(VAR, MU, MU, op=ALU.mult), chain=True)
        kb.emit("vector", lambda e: e.tensor_tensor(VAR, S2N, VAR, op=ALU.subtract), chain=True)
        vvar = kb.cnt["V"]
        kb.emit("scalar", lambda e: e.activation(SQ, VAR, AF.Sqrt, bias=kb.tens["c_actc"][0:1, 0:1], scale=1.0),
                waits=[("V", vvar)], inc="A")
        asq2 = kb.cnt["A"]
        kb.emit("vector", lambda e: e.reciprocal(RSTD, SQ), waits=[("A", asq2)], chain=True)
        kb.emit("vector", lambda e: e.tensor_scalar(NMU, MU, -1.0, None, op0=ALU.mult), chain=True)
        vst = kb.cnt["V"]
        kb.emit("tensor", lambda e: e.matmul(pR[:, :], onesf[0:1, :], RSTD, start=True, stop=True),
                waits=[("V", vst), ("V", ev_prev)])
        kb.emit("tensor", lambda e: e.matmul(pM[:, :], onesf[0:1, :], NMU, start=True, stop=True),
                inc="T")
        trep = kb.cnt["T"]
        for jc in range(8):
            kb.emit("vector", lambda e, jc=jc: e.tensor_tensor(
                olnb[:, jc * 512:(jc + 1) * 512], olod[:, jc * 512:(jc + 1) * 512],
                pM[:, :], op=ALU.add), waits=[("T", trep)] if jc == 0 else [], chain=True)
            kb.emit("vector", lambda e, jc=jc: e.scalar_tensor_tensor(
                olnb[:, jc * 512:(jc + 1) * 512], olnb[:, jc * 512:(jc + 1) * 512],
                nwc[:, jc:jc + 1], pR[:, :], op0=ALU.mult, op1=ALU.mult), chain=True)
        ev_prev = kb.cnt["V"]
        voln = kb.cnt["V"]
        for oc in range(8):
            ob = oc % 2
            kb.emit("scalar", lambda e, oc=oc, t0=t0, ob=ob: e.dma_start(
                out=xfb[:, ob * 512:(ob + 1) * 512], in_=io["xT"][oc * 128:(oc + 1) * 128, t0:t0 + 512]),
                waits=[("V", yst_tail[ob])], inc="DS")
            xld = kb.cnt["DS"]
            for dc in range(32):
                kb.emit("tensor", lambda e, dc=dc, oc=oc, gb=gb, ob=ob, first=(dc == 0): e.matmul(
                    pFs[ob][:, :], w3b[:, oc * 4096 + dc * 128:oc * 4096 + dc * 128 + 128],
                    g2b[:, gb * 16384 + dc * 512:gb * 16384 + dc * 512 + 512],
                    start=first, stop=(dc == 31)),
                    waits=[g2ld, ("V", yst_tail[ob]), ("A", aw_tail[ob])] if dc == 0 else [])
            for jc in range(8):
                kb.emit("tensor", lambda e, jc=jc, oc=oc, ob=ob, first=(jc == 0): e.matmul(
                    pWs[ob][:, :], wob[:, oc * 1024 + jc * 128:oc * 1024 + jc * 128 + 128],
                    olnb[:, jc * 512:(jc + 1) * 512],
                    start=first, stop=(jc == 7)),
                    waits=[("V", voln)] if jc == 0 else [], inc=("T" if jc == 7 else None))
            tmm = kb.cnt["T"]
            kb.emit("scalar", lambda e, oc=oc, ob=ob: e.copy(
                ysb[:, ob * 512:(ob + 1) * 512], pWs[ob][:, :]),
                waits=[("T", tmm), ("DS", xld)], inc="A")
            ayw = kb.cnt["A"]
            aw_tail[ob] = ayw
            kb.emit("vector", lambda e, oc=oc, ob=ob: e.scalar_tensor_tensor(
                ysb[:, ob * 512:(ob + 1) * 512], pFs[ob][:, :], s3c[:, oc:oc + 1],
                ysb[:, ob * 512:(ob + 1) * 512],
                op0=ALU.mult, op1=ALU.add), waits=[("A", ayw), ("DS", xld)], chain=True)
            kb.emit("vector", lambda e, oc=oc, ob=ob: e.tensor_tensor(
                ysb[:, ob * 512:(ob + 1) * 512], ysb[:, ob * 512:(ob + 1) * 512],
                xfb[:, ob * 512:(ob + 1) * 512], op=ALU.add), chain=True)
            yv = kb.cnt["V"]
            yst_tail[ob] = yv
            kb.emit("scalar", lambda e, oc=oc, t0=t0, ob=ob, yv=yv: e.dma_start(
                out=io["y"][oc, :, t0:t0 + 512], in_=ysb[:, ob * 512:(ob + 1) * 512]),
                waits=[("V", yv)], inc="DS")
        g2_rd[gb] = kb.cnt["T"]


def build_program(phases=("p0", "p2", "l2", "l3", "fin"), expose=(), p2_nseg=NSEG):
    nc = bass.Bass("TRN2", target_bir_lowering=False)
    io = declare_io(nc, expose)
    stack = ExitStack()
    with stack:
        kb = KB(nc, stack)
        env = None
        xn_stack = ExitStack()
        with ExitStack() as ps0:
            env = phase0_xn(kb, io, ps0, xn_stack=xn_stack)
        if "xn" in expose:
            v = kb.cnt["V"]
            kb.emit("scalar", lambda e: e.dma_start(out=io["xn_dump"][:], in_=kb.tens["xn_fm"][:]),
                    waits=[("V", v)], inc="DS")
        kb.fence()
        if "l1" in phases:
            with ExitStack() as ps:
                phase_l1(kb, io, env, ps)
            kb.fence()
        if "p2" in phases:
            with ExitStack() as ps:
                phase_p2(kb, io, env, ps, nseg=p2_nseg)
            kb.fence()
        xn_stack.close()
        if "l2" in phases:
            with ExitStack() as ps:
                phase_hidden(kb, io, "h_dram", "g_dram", "h1t", "sh1", ps)
            kb.fence()
        if "l3" in phases:
            with ExitStack() as ps:
                phase_hidden(kb, io, "g_dram", "h_dram", "h2t", "sh2", ps)
            kb.fence()
        if "fin" in phases:
            with ExitStack() as ps:
                phase_final(kb, io, ps)
        kb.wait("tensor", [("DS", kb.cnt["DS"]), ("DT", kb.cnt["DT"]), ("DG", kb.cnt["DG"]),
                           ("V", kb.cnt["V"]), ("A", kb.cnt["A"])])
        kb.finalize()
    return nc


def kernel(**inputs):
    cm, xs = host_prepare(inputs)
    nc = build_program()
    in_maps = []
    for b in range(8):
        m = dict(cm)
        m["xT"] = xs[b]
        in_maps.append(m)
    res = run_bass_kernel_spmd(nc, in_maps, core_ids=list(range(8)))
    outs = []
    for b in range(8):
        yfm = np.asarray(res.results[b]["y"], dtype=np.float32)
        outs.append(yfm.reshape(1024, 4096).T)
    return np.ascontiguousarray(np.stack(outs)).astype(np.float32)


# version 13
# speedup vs baseline: 1.0294x; 1.0160x over previous
import numpy as np
import ml_dtypes
import concourse.bass as bass
import concourse.mybir as mybir
from concourse.bass_utils import run_bass_kernel_spmd
from contextlib import ExitStack

BF16 = mybir.dt.bfloat16
F32 = mybir.dt.float32
AF = mybir.ActivationFunctionType
ALU = mybir.AluOpType
AX = mybir.AxisListType
BND = ml_dtypes.bfloat16

T, D, NH, HD, CH = 4096, 1024, 16, 64, 64
NT = T // 128
NSEG = 8
TPS = 4
DHID = 4096
P2_HEADS = None
P2_FENCE_HEADS = False
P2_DBG = None
P2_SEC = {}
P2_TSEC = 99
LH_LIM = None

def _tile_ocp(w):
    din, dout = w.shape
    dc, oc = din // 128, dout // 128
    return np.ascontiguousarray(
        w.reshape(dc, 128, oc, 128).transpose(2, 1, 0, 3).reshape(oc, 128, dc * 128))

def host_prepare(inputs):
    g = lambda n: np.ascontiguousarray(np.asarray(inputs[n], dtype=np.float32))
    cm = {}
    cm["w1t"] = _tile_ocp(g("L0_ffn_w1_w").T)
    cm["w2t"] = _tile_ocp(g("L0_ffn_w2_w").T)
    cm["h1t"] = _tile_ocp(g("L0_ffn_h1_w").T)
    cm["h2t"] = _tile_ocp(g("L0_ffn_h2_w").T)
    cm["w3t"] = _tile_ocp(g("L0_ffn_w3_w").T)
    cm["wot"] = _tile_ocp(g("L0_attn_Wo"))
    for nm, k in (("s1", "L0_ffn_w1_s"), ("s2", "L0_ffn_w2_s"),
                  ("sh1", "L0_ffn_h1_s"), ("sh2", "L0_ffn_h2_s"),
                  ("s3", "L0_ffn_w3_s"), ("nw", "L0_attn_norm_w")):
        v = g(k)
        cm[nm] = np.ascontiguousarray(v.reshape(-1, 128).T)
    qkvg = np.concatenate([g("L0_attn_Wq"), g("L0_attn_Wk"), g("L0_attn_Wv"),
                           g("L0_attn_Wgamma")], axis=1)
    cm["qkvg"] = np.ascontiguousarray(qkvg.reshape(8, 128, 3088))
    f1og1 = np.concatenate([g("L0_attn_Wf1"), g("L0_attn_Wog1")], axis=1)
    cm["f1og1"] = np.ascontiguousarray(f1og1.reshape(8, 128, 128))
    cm["wf2"] = g("L0_attn_Wf2")
    wog2 = g("L0_attn_Wog2")
    cm["wog2t"] = np.ascontiguousarray(wog2.reshape(64, 8, 128).transpose(1, 0, 2))
    cm["ident_b"] = np.eye(128).astype(BND)
    cm["ones_b"] = np.ones((128, 128), dtype=BND)
    cm["ones_f"] = np.ones((1, 128), dtype=np.float32)
    cm["actc"] = np.tile(np.array([[1e-5, 1.0]], np.float32), (128, 1))
    su = np.triu(np.ones((CH, CH), np.float32), 1)
    iu = np.triu(np.ones((CH, CH), np.float32), 0)
    sl = np.tril(np.ones((CH, CH), np.float32), -1)
    cm["maskct"] = np.ascontiguousarray(
        np.concatenate([np.concatenate([su, iu], 1)] * 2, 0)).astype(BND)
    cm["maskc"] = np.ascontiguousarray(np.concatenate([sl, sl], 0)).astype(BND)
    cm["lcum"] = np.ascontiguousarray(np.concatenate([iu] * 2, 0)).astype(BND)
    cm["lcum2"] = np.ascontiguousarray(
        np.concatenate([iu + np.eye(CH, dtype=np.float32)] * 2, 0)).astype(BND)
    i64d = np.concatenate([np.eye(64, dtype=np.float32)] * 2, 0)
    cm["i64d"] = i64d.astype(BND)
    xs = [np.ascontiguousarray(np.asarray(inputs["L0_attn_x"][b], np.float32).T)
          for b in range(8)]
    return cm, xs


class KB:
    ENGS = ("tensor", "vector", "scalar", "gpsimd", "sync")

    def __init__(self, nc, stack):
        self.nc = nc
        self.stack = stack
        self.streams = {e: [] for e in self.ENGS}
        self.sems = {}
        self.cnt = {}
        for s in ("T", "V", "A", "G", "F", "DS", "DT", "DG", "DH"):
            self.sems[s] = stack.enter_context(nc.semaphore("sem_" + s))
            self.cnt[s] = 0
        self.tens = {}
        self.waited = {s: set() for s in self.DMA_SEMS}

    def sb(self, name, shape, dtype, stack=None, side=None):
        kw = {"side": side} if side else {}
        t = (stack or self.stack).enter_context(self.nc.sbuf_tensor(name, shape, dtype, **kw))
        self.tens[name] = t
        return t

    def ps(self, name, shape=(128, 512), dtype=F32, stack=None):
        t = (stack or self.stack).enter_context(self.nc.psum_tensor(name, list(shape), dtype))
        self.tens[name] = t
        return t

    def fence(self):
        for eng in ("tensor", "vector", "scalar", "gpsimd"):
            self.emit(eng, lambda e: e.nop(), inc="F")
        snap = dict(self.cnt)
        waits = [(s, snap[s]) for s in ("F", "DS", "DT", "DG", "DH")]
        for eng in self.ENGS:
            self.wait(eng, waits)

    SELF_SEM = {"tensor": "T", "vector": "V", "scalar": "A", "gpsimd": "G"}
    DMA_SEMS = ("DS", "DT", "DG", "DH")

    def emit(self, eng, make, waits=(), inc=None, chain=False):
        v = None
        if eng in ("vector", "scalar", "gpsimd"):
            chain = True
        if chain:
            s = self.SELF_SEM[eng]
            waits = list(waits) + [(s, self.cnt[s])]
            if inc is None:
                inc = s
        if inc is not None:
            k = 16 if inc in self.DMA_SEMS else 1
            self.cnt[inc] += k
            v = self.cnt[inc]
        waits = [(s, w) for (s, w) in waits if w > 0]
        sems = self.sems
        kk = (16 if inc in self.DMA_SEMS else 1) if inc else 0
        def f(e, make=make, inc=inc, k=kk):
            for s, w in f.waits:
                e.wait_ge(sems[s], w)
            ins = make(e)
            if inc is not None:
                ins.then_inc(sems[inc], k)
        f.waits = waits
        f.inc_sem = inc
        f.inc_k = kk
        f.dma_sem = inc if inc in self.DMA_SEMS else None
        f.dma_count = v
        self.streams[eng].append(f)
        return v

    def wait(self, eng, waits):
        waits = [(s, w) for (s, w) in waits if w > 0]
        if not waits:
            return
        sems = self.sems
        def f(e):
            for s, w in f.waits:
                e.wait_ge(sems[s], w)
        f.waits = waits
        f.inc_sem = None
        f.inc_k = 0
        f.dma_sem = None
        f.dma_count = None
        self.streams[eng].append(f)

    def finalize(self):
        waited = {s: set() for s in self.DMA_SEMS}
        succ = {s: {} for s in self.DMA_SEMS}
        for eng in self.ENGS:
            for f in self.streams[eng]:
                for s, w in f.waits:
                    if s in self.DMA_SEMS:
                        waited[s].add(w)
                if f.dma_sem is not None:
                    succ[f.dma_sem][f.dma_count] = f
        for s, vals in waited.items():
            for v in vals:
                nf = succ[s].get(v + 16)
                if nf is not None and not any(
                        ss == s and ww >= v for ss, ww in nf.waits):
                    nf.waits = list(nf.waits) + [(s, v)]
        def run(e, fs):
            for f in fs:
                f(e)
        with self.nc.Block() as block:
            @block.tensor
            def _(e):
                run(e, self.streams["tensor"])
            @block.vector
            def _(e):
                run(e, self.streams["vector"])
            @block.scalar
            def _(e):
                run(e, self.streams["scalar"])
            @block.gpsimd
            def _(e):
                run(e, self.streams["gpsimd"])
            @block.sync
            def _(e):
                run(e, self.streams["sync"])


def declare_io(nc, expose=()):
    d = {}
    def di(name, shape, dtype, kind="ExternalInput"):
        if kind == "Internal" and name in expose:
            kind = "ExternalOutput"
        if kind == "Internal" and ("in:" + name) in expose:
            kind = "ExternalInput"
        d[name] = nc.dram_tensor(name, list(shape), dtype, kind=kind).ap()
    di("xT", (D, T), F32)
    di("qkvg", (8, 128, 3088), F32)
    di("f1og1", (8, 128, 128), F32)
    di("wf2", (64, D), F32)
    di("wog2t", (8, 64, 128), F32)
    di("w1t", (32, 128, 1024), F32)
    di("w2t", (32, 128, 1024), F32)
    di("h1t", (32, 128, 4096), F32)
    di("h2t", (32, 128, 4096), F32)
    di("w3t", (8, 128, 4096), F32)
    di("wot", (8, 128, 1024), F32)
    for nm, n in (("s1", 32), ("s2", 32), ("sh1", 32), ("sh2", 32), ("s3", 8), ("nw", 8)):
        di(nm, (128, n), F32)
    di("ident_b", (128, 128), BF16)
    di("ones_b", (128, 128), BF16)
    di("ones_f", (1, 128), F32)
    di("actc", (128, 2), F32)
    di("maskct", (128, 128), BF16)
    di("maskc", (128, 64), BF16)
    di("lcum", (128, 64), BF16)
    di("lcum2", (128, 64), BF16)
    di("i64d", (128, 64), BF16)
    di("y", (8, 128, T), F32, kind="ExternalOutput")
    di("h_dram", (32, 128, T), BF16, kind="Internal")
    di("g_dram", (32, 128, T), BF16, kind="Internal")
    di("o_dram", (8, 128, T), BF16, kind="Internal")
    if "dbg" in expose:
        di("dbg", (128, 4096), BF16, kind="ExternalOutput")
    if "xn" in expose:
        di("xn_dump", (128, 8 * T), BF16, kind="ExternalOutput")
    return d


def phase0_xn(kb, io, pstk, xn_stack=None):
    nc = kb.nc
    xn = kb.sb("xn_fm", [128, 8 * T], BF16, stack=xn_stack, side="right")
    ones = kb.sb("c_ones", [128, 128], BF16)
    onesf = kb.sb("c_onesf", [1, 128], F32)
    actc = kb.sb("c_actc", [128, 2], F32)
    identb = kb.sb("c_identb", [128, 128], BF16)
    maskct = kb.sb("c_maskct", [128, 128], BF16)
    maskc = kb.sb("c_maskc", [128, 64], BF16)
    lcum = kb.sb("c_lcum", [128, 64], BF16)
    lcum2 = kb.sb("c_lcum2", [128, 64], BF16)
    i64d = kb.sb("c_i64d", [128, 64], BF16)
    nwc = kb.sb("c_nw", [128, 8], F32)
    xf = kb.sb("p0_xf", [128, 2 * 8 * 512], F32, stack=pstk)
    xb = kb.sb("p0_xb", [128, 8 * 512], BF16, stack=pstk)
    x2b = kb.sb("p0_x2b", [128, 8 * 512], BF16, stack=pstk)
    st = kb.sb("p0_stats", [2, 6 * 512], F32, stack=pstk)
    psS = kb.ps("ps0_s", (2, 512), stack=pstk)
    psS2 = kb.ps("ps0_s2", (2, 512), stack=pstk)
    psR = kb.ps("ps0_r", (128, 512), stack=pstk)
    psM = kb.ps("ps0_m", (128, 512), stack=pstk)

    for nm, dst in (("ones_b", ones), ("ones_f", onesf), ("actc", actc), ("ident_b", identb),
                    ("maskct", maskct), ("maskc", maskc), ("lcum", lcum),
                    ("lcum2", lcum2), ("i64d", i64d), ("nw", nwc)):
        kb.emit("scalar", lambda e, dst=dst, nm=nm: e.dma_start(out=dst[:], in_=io[nm][:]), inc="DS")
    c_done = kb.cnt["DS"]

    ev_prev = 0
    tsum_prev = 0
    for tt in range(8):
        t0 = tt * 512
        pb = (tt % 2) * 8 * 512
        for dc in range(8):
            kb.emit("scalar", lambda e, dc=dc, t0=t0, pb=pb: e.dma_start(
                out=xf[:, pb + dc * 512: pb + (dc + 1) * 512],
                in_=io["xT"][dc * 128:(dc + 1) * 128, t0:t0 + 512]), inc="DS")
        ld = kb.cnt["DS"]
        for dc in range(8):
            vv = kb.emit("vector", lambda e, dc=dc, pb=pb: e.tensor_copy(
                xb[:, dc * 512:(dc + 1) * 512], xf[:, pb + dc * 512: pb + dc * 512 + 512]),
                waits=([("DS", ld), ("T", tsum_prev)] if dc == 0 else []), inc="V")
            kb.emit("scalar", lambda e, dc=dc: e.square(
                x2b[:, dc * 512:(dc + 1) * 512], xb[:, dc * 512:(dc + 1) * 512]),
                waits=[("V", vv)] + ([("T", tsum_prev)] if dc == 0 else []), inc="A")
        vcast = kb.cnt["V"]; acast = kb.cnt["A"]
        for dc in range(8):
            kb.emit("tensor", lambda e, dc=dc, first=(dc == 0): e.matmul(
                psS[0:1, :], ones[:, 0:1], xb[:, dc * 512:(dc + 1) * 512],
                start=first, stop=(dc == 7)),
                waits=([("V", vcast)] if dc == 0 else []))
            kb.emit("tensor", lambda e, dc=dc, first=(dc == 0): e.matmul(
                psS2[0:1, :], ones[:, 0:1], x2b[:, dc * 512:(dc + 1) * 512],
                start=first, stop=(dc == 7)),
                waits=([("A", acast)] if dc == 0 else []), inc=("T" if dc == 7 else None))
        tsum = kb.cnt["T"]
        tsum_prev = tsum
        MU, S2N, VAR, SQ, RSTD, NMR = (st[0:1, i * 512:(i + 1) * 512] for i in range(6))
        kb.emit("vector", lambda e: e.tensor_scalar(MU, psS[0:1, :], 1.0 / D, None, op0=ALU.mult),
                waits=[("T", tsum)], chain=True)
        kb.emit("vector", lambda e: e.tensor_scalar(S2N, psS2[0:1, :], 1.0 / D, None, op0=ALU.mult),
                chain=True)
        kb.emit("vector", lambda e: e.tensor_tensor(VAR, MU, MU, op=ALU.mult), chain=True)
        kb.emit("vector", lambda e: e.tensor_tensor(VAR, S2N, VAR, op=ALU.subtract), chain=True)
        vvar = kb.cnt["V"]
        kb.emit("scalar", lambda e: e.activation(SQ, VAR, AF.Sqrt, bias=actc[0:1, 0:1], scale=1.0),
                waits=[("V", vvar)], inc="A")
        asq = kb.cnt["A"]
        kb.emit("vector", lambda e: e.reciprocal(RSTD, SQ), waits=[("A", asq)], chain=True)
        vrstd = kb.cnt["V"]
        kb.emit("vector", lambda e: e.scalar_tensor_tensor(NMR, MU, -1.0, RSTD, op0=ALU.mult, op1=ALU.mult),
                chain=True)
        vnmr = kb.cnt["V"]
        kb.emit("tensor", lambda e: e.matmul(psR[:, :], onesf[0:1, :], RSTD, start=True, stop=True),
                waits=[("V", vrstd), ("V", ev_prev)])
        kb.emit("tensor", lambda e: e.matmul(psM[:, :], onesf[0:1, :], NMR, start=True, stop=True),
                waits=[("V", vnmr)], inc="T")
        trep = kb.cnt["T"]
        for dc in range(8):
            kb.emit("vector", lambda e, dc=dc, pb=pb: e.tensor_tensor(
                xb[:, dc * 512:(dc + 1) * 512],
                xf[:, pb + dc * 512: pb + dc * 512 + 512], psR[:, :], op=ALU.mult),
                waits=[("T", trep)] if dc == 0 else [], chain=True)
            kb.emit("vector", lambda e, dc=dc, t0=t0: e.tensor_tensor(
                xn[:, dc * T + t0: dc * T + t0 + 512],
                xb[:, dc * 512:(dc + 1) * 512], psM[:, :], op=ALU.add), chain=True)
        ev_prev = kb.cnt["V"]
    kb.wait("tensor", [("DS", c_done)])
    return dict(xn=xn, ones=ones, onesf=onesf, identb=identb, maskct=maskct,
                maskc=maskc, lcum=lcum, lcum2=lcum2, i64d=i64d, nwc=nwc,
                xn_ready=kb.cnt["V"])


def phase_l1(kb, io, env, pstk):
    xn = env["xn"]
    s1c = kb.sb("l1_s1", [128, 32], F32, stack=pstk)
    s2c = kb.sb("l1_s2", [128, 32], F32, stack=pstk)
    wbb = kb.sb("l1_wbb", [128, 2 * 2 * 1024], BF16, stack=pstk)
    hsb = kb.sb("l1_h", [128, 2 * 512], BF16, stack=pstk)
    tsl = kb.sb("l1_tsil", [128, 2 * 512], BF16, stack=pstk)
    psA = [kb.ps("psl1_a%d" % i, stack=pstk) for i in range(2)]
    psB = [kb.ps("psl1_b%d" % i, stack=pstk) for i in range(2)]
    kb.emit("scalar", lambda e: e.dma_start(out=s1c[:], in_=io["s1"][:]), inc="DS")
    kb.emit("scalar", lambda e: e.dma_start(out=s2c[:], in_=io["s2"][:]), inc="DS")
    sc_ds = kb.cnt["DS"]

    xn_ready = env["xn_ready"]
    ldq = []
    mm_after_panel = {0: 0, 1: 0}
    def load_panel(oc):
        b = (oc % 2)
        sm = "DG" if b == 0 else "DH"
        kb.emit("gpsimd", lambda e, oc=oc, b=b: e.dma_start(
            out=wbb[:, b * 2048: b * 2048 + 1024], in_=io["w1t"][oc]),
            waits=[("T", mm_after_panel[b])], inc=sm)
        v = kb.emit("gpsimd", lambda e, oc=oc, b=b: e.dma_start(
            out=wbb[:, b * 2048 + 1024: b * 2048 + 2048], in_=io["w2t"][oc]), inc=sm)
        ldq.append((sm, v))

    load_panel(0)
    load_panel(1)
    ev_tail = {0: 0, 1: 0}
    evA_tail = {0: 0, 1: 0}
    st_tail = {0: 0, 1: 0}
    for oc in range(32):
        wb = oc % 2
        for tt in range(8):
            b = tt % 2
            w0 = []
            if tt == 0:
                w0 = [ldq[oc], ("V", xn_ready)]
            for dc in range(8):
                kb.emit("tensor", lambda e, dc=dc, b=b, wb=wb, t0=tt * 512, first=(dc == 0): e.matmul(
                    psA[b][:, :], wbb[:, wb * 2048 + dc * 128: wb * 2048 + dc * 128 + 128],
                    xn[:, dc * T + t0: dc * T + t0 + 512], start=first, stop=(dc == 7)),
                    waits=(w0 if dc == 0 else []) + ([("A", evA_tail[b]), ("V", ev_tail[b])] if dc == 0 else []))
            for dc in range(8):
                kb.emit("tensor", lambda e, dc=dc, b=b, wb=wb, t0=tt * 512, first=(dc == 0): e.matmul(
                    psB[b][:, :], wbb[:, wb * 2048 + 1024 + dc * 128: wb * 2048 + 1024 + dc * 128 + 128],
                    xn[:, dc * T + t0: dc * T + t0 + 512], start=first, stop=(dc == 7)),
                    inc=("T" if dc == 7 else None))
            tmm = kb.cnt["T"]
            kb.emit("scalar", lambda e, b=b, oc=oc: e.activation(
                tsl[:, b * 512:(b + 1) * 512], psA[b][:, :], AF.Silu,
                scale=s1c[:, oc:oc + 1]),
                waits=[("T", tmm), ("V", ev_tail[b])]
                + ([("DS", sc_ds)] if oc == 0 and tt == 0 else []), inc="A")
            evA_tail[b] = kb.cnt["A"]
            kb.emit("vector", lambda e, b=b, oc=oc: e.scalar_tensor_tensor(
                hsb[:, b * 512:(b + 1) * 512], psB[b][:, :], s2c[:, oc:oc + 1],
                tsl[:, b * 512:(b + 1) * 512], op0=ALU.mult, op1=ALU.mult),
                waits=[("A", evA_tail[b]), ("DS", st_tail[b])]
                + ([("DS", sc_ds)] if oc == 0 and tt == 0 else []), inc="V")
            ev_tail[b] = kb.cnt["V"]
            kb.emit("scalar", lambda e, b=b, oc=oc, t0=tt * 512: e.dma_start(
                out=io["h_dram"][oc, :, t0:t0 + 512], in_=hsb[:, b * 512:(b + 1) * 512]),
                waits=[("V", ev_tail[b])], inc="DS")
            st_tail[b] = kb.cnt["DS"]
        mm_after_panel[wb] = kb.cnt["T"]
        if oc + 2 < 32:
            load_panel(oc + 2)
    env["l1_done_ds"] = kb.cnt["DS"]
    env["l1_done_T"] = kb.cnt["T"]


def phase_p2(kb, io, env, pstk, nseg=NSEG):
    xn = env["xn"]; identb = env["identb"]
    maskct = env["maskct"]; maskc = env["maskc"]
    lcum = env["lcum"]; lcum2 = env["lcum2"]; i64d = env["i64d"]
    L = lambda n, sh, dt: kb.sb(n, sh, dt, stack=pstk)
    qkvgw = L("p2_qkvgw", [128, 8 * 3088], BF16)
    f1og1w = L("p2_f1og1", [128, 8 * 128], BF16)
    wf2w = L("p2_wf2", [64, 1024], BF16)
    wog2w = L("p2_wog2", [64, 8 * 128], BF16)
    pack = L("p2_pack", [128, 2 * 4096], BF16)
    fmp = L("p2_fmp", [128, 2 * 4096], BF16)
    vtm2 = L("p2_vtm2", [128, 2 * 2048], BF16)
    ogp = L("p2_ogp", [128, 8 * 512], BF16)
    osg = L("p2_osg", [128, 8 * 512], BF16)
    z1f = L("p2_z1", [64, 512], BF16)
    og1f = L("p2_og1", [64, 512], BF16)
    states = L("p2_S", [128, 512], BF16)
    ctpt = L("p2_ctpt", [128, 128], BF16)
    cun = L("p2_cun", [128, 64], BF16)
    powsb = L("p2_pow", [128, 2 * 128], BF16)
    xw = L("p2_xw", [128, 2 * 128], BF16)
    ctptb = L("p2_ctptb", [128, 128], BF16)
    cunb = L("p2_cunb", [128, 64], BF16)
    powsbb = L("p2_powb", [128, 2 * 128], BF16)
    xwb = L("p2_xwb", [128, 2 * 128], BF16)
    sumb = L("p2_sumb", [128, 256], BF16)
    qtmp = L("p2_qtmp", [128, 1024], BF16)
    ktmp = L("p2_ktmp", [128, 1024], BF16)
    knk = L("p2_knk", [128, 1024], F32)
    kn = L("p2_kn", [128, 64], F32)
    emF = L("p2_emF", [128, 1024], BF16)
    eagF = L("p2_eagF", [128, 1024], BF16)
    gtmp = L("p2_gt", [128, 16], F32)
    gkb = L("p2_gkb", [128, 1024], BF16)
    scr1 = L("p2_scr1", [128, 512], BF16)
    scr2 = L("p2_scr2", [128, 512], BF16)
    pg4 = kb.ps("p2_pg4", (128, 1024), stack=pstk)
    pg4b = kb.ps("p2_pg4b", (128, 1024), stack=pstk)
    px2 = kb.ps("p2_px2", (128, 1024), stack=pstk)
    ps4 = kb.ps("p2_ps4", (128, 1024), stack=pstk)
    l1_s1 = L("p2l1_s1", [128, 32], F32)
    l1_s2 = L("p2l1_s2", [128, 32], F32)
    l1_wbb = L("p2l1_wbb", [128, 2 * 1024], BF16)
    l1_hsb = L("p2l1_h", [128, 512], BF16)
    l1_tsl = L("p2l1_t", [128, 512], BF16)

    for dc in range(8):
        kb.emit("gpsimd", lambda e, dc=dc: e.dma_start(
            out=qkvgw[:, dc * 3088:(dc + 1) * 3088], in_=io["qkvg"][dc]), inc="DG")
    kb.emit("gpsimd", lambda e: e.dma_start(
        out=f1og1w[:].rearrange("p (a b) -> p a b", b=128),
        in_=io["f1og1"].rearrange("a p b -> p a b")), inc="DG")
    kb.emit("gpsimd", lambda e: e.dma_start(out=wf2w[:], in_=io["wf2"][:]), inc="DG")
    kb.emit("gpsimd", lambda e: e.dma_start(
        out=wog2w[:].rearrange("p (a b) -> p a b", b=128),
        in_=io["wog2t"].rearrange("a p b -> p a b")), inc="DG")
    wready = kb.cnt["DG"]
    kb.emit("vector", lambda e: e.memset(states[:], 0.0))
    kb.emit("vector", lambda e: e.memset(ctpt[:], 0.0))
    if P2_HEADS is not None:
        kb.emit("vector", lambda e: e.memset(osg[:], 0.0))
    kb.emit("vector", lambda e: e.memset(cun[:], 0.0), inc="V")
    kb.wait("tensor", [("DG", wready), ("V", kb.cnt["V"])])
    kb.wait("scalar", [("DG", wready)])
    kb.wait("gpsimd", [("V", kb.cnt["V"])])

    fmp4 = fmp.rearrange("p (tb ti jc r) -> p tb ti jc r", tb=2, ti=4, jc=8, r=128)

    fr = dict(psG=[], psH=[], bA=[], bB=[], bC=[], bF=[],
              qtmp=[], ktmp=[], knk=[], gtmp=[], gkb=[], emF=[], eagF=[], scr=[],
              vtm=[[], []], pack=[[], []], fmpc=[[], []], og=[], osg=[], chain=[],
              ct=[], pow=[], xwg=[], sumbg=[])
    LN = [dict(bank=pg4, xoff=0, ct=ctpt, cu=cun, pw=powsb, xw=xw),
          dict(bank=pg4b, xoff=512, ct=ctptb, cu=cunb, pw=powsbb, xw=xwb)]
    frl = [dict(bA=[], bX=[], pow=[], sumev=[], psS=[], psO=[], chain=[],
                vmask=0, vpow=0, axw=0, tg=0, tx=0, tsq=0, taj=0, tch=0, gadd=0)
           for _ in range(2)]

    def W(*gs):
        out = []
        for g in gs:
            out += g
        return out

    kb.emit("scalar", lambda e: e.dma_start(out=l1_s1[:], in_=io["s1"][:]), inc="DS")
    kb.emit("scalar", lambda e: e.dma_start(out=l1_s2[:], in_=io["s2"][:]), inc="DS")
    l1_sc_ds = kb.cnt["DS"]
    l1_ldq = []
    l1_state = {"mm_after": 0, "st": 0, "ts": 0}

    def l1_load_panel(oc):
        sm = "DG" if oc % 2 == 0 else "DH"
        kb.emit("gpsimd", lambda e, oc=oc: e.dma_start(
            out=l1_wbb[:, 0:1024], in_=io["w1t"][oc]),
            waits=[("T", l1_state["mm_after"])], inc=sm)
        v = kb.emit("gpsimd", lambda e, oc=oc: e.dma_start(
            out=l1_wbb[:, 1024:2048], in_=io["w2t"][oc]), inc=sm)
        l1_ldq.append((sm, v))

    l1_load_panel(0)
    l1_q = {"u": 0, "m": 0, "t1": 0}

    def l1_tick():
        u = l1_q["u"]
        if u >= 256:
            return
        m = l1_q["m"]
        oc, tt = u // 8, u % 8
        t0 = tt * 512
        if m < 8:
            dc = m
            w = (([l1_ldq[oc]] if tt == 0 else []) + W(fr["psG"])) if dc == 0 else []
            kb.emit("tensor", lambda e, dc=dc, t0=t0, first=(dc == 0): e.matmul(
                ps4[:, 0:512], l1_wbb[:, dc * 128: dc * 128 + 128],
                xn[:, dc * T + t0: dc * T + t0 + 512], start=first, stop=(dc == 7)),
                waits=w, inc=("T" if dc == 7 else None))
            if dc == 7:
                l1_q["t1"] = kb.cnt["T"]
        else:
            dc = m - 8
            w = W(fr["psH"]) if dc == 0 else []
            kb.emit("tensor", lambda e, dc=dc, t0=t0, first=(dc == 0): e.matmul(
                ps4[:, 512:1024], l1_wbb[:, 1024 + dc * 128: 1024 + dc * 128 + 128],
                xn[:, dc * T + t0: dc * T + t0 + 512], start=first, stop=(dc == 7)),
                waits=w, inc=("T" if dc == 7 else None))
        l1_q["m"] = m + 1
        if l1_q["m"] < 16:
            return
        l1_q["m"] = 0
        l1_q["u"] = u + 1
        t2 = kb.cnt["T"]
        t1 = l1_q["t1"]
        kb.emit("scalar", lambda e, oc=oc: e.activation(
            l1_tsl[:], ps4[:, 0:512], AF.Silu, scale=l1_s1[:, oc:oc + 1]),
            waits=[("T", t1), ("V", l1_state["ts"])]
            + ([("DS", l1_sc_ds)] if u == 0 else []), inc="A")
        a1 = kb.cnt["A"]
        fr["psG"] = [("A", a1)]
        kb.emit("vector", lambda e, oc=oc: e.scalar_tensor_tensor(
            l1_hsb[:], ps4[:, 512:1024], l1_s2[:, oc:oc + 1],
            l1_tsl[:], op0=ALU.mult, op1=ALU.mult),
            waits=[("T", t2), ("A", a1), ("DS", l1_state["st"])]
            + ([("DS", l1_sc_ds)] if u == 0 else []), inc="V")
        v1 = kb.cnt["V"]
        fr["psH"] = [("V", v1)]
        l1_state["ts"] = v1
        kb.emit("scalar", lambda e, oc=oc, t0=t0: e.dma_start(
            out=io["h_dram"][oc, :, t0:t0 + 512], in_=l1_hsb[:]),
            waits=[("V", v1)], inc="DS")
        l1_state["st"] = kb.cnt["DS"]
        if tt == 7:
            l1_state["mm_after"] = kb.cnt["T"]
            if oc + 1 < 32:
                l1_load_panel(oc + 1)

    def projgrp(tk, lo, hi, bank, bguard, waits0=()):
        t0 = tk * 128
        for dc in range(8):
            w = list(waits0) + W(bguard) if dc == 0 else []
            kb.emit("tensor", lambda e, dc=dc, t0=t0, lo=lo, hi=hi, bank=bank,
                    first=(dc == 0): e.matmul(
                bank[:, 0:hi - lo], xn[:, dc * T + t0:dc * T + t0 + 128],
                qkvgw[:, dc * 3088 + lo:dc * 3088 + hi],
                start=first, stop=(dc == 7)),
                waits=w, inc=("T" if dc == 7 else None))
        return kb.cnt["T"]

    bG = lambda: ps4[:, 0:512]
    bH = lambda: ps4[:, 512:1024]
    bA = lambda: pg4[:, 0:512]
    bB = lambda: pg4[:, 512:1024]
    bC = lambda: px2[:, 0:512]
    bF = lambda: px2[:, 512:1024]

    for s in range(nseg):
        s0 = s * 512
        for dc in range(8):
            kb.emit("tensor", lambda e, dc=dc, s0=s0, first=(dc == 0): e.matmul(
                bG()[:, 0:512], f1og1w[:, dc * 128:(dc + 1) * 128],
                xn[:, dc * T + s0:dc * T + s0 + 512], start=first, stop=(dc == 7)),
                waits=W(fr["psG"]) if dc == 0 else [], inc=("T" if dc == 7 else None))
        tz = kb.cnt["T"]
        kb.emit("scalar", lambda e: e.copy(z1f[:], bG()[0:64, 0:512]), waits=[("T", tz)])
        kb.emit("scalar", lambda e: e.copy(og1f[:], bG()[64:128, 0:512]), inc="A")
        fr["psG"] = [("A", kb.cnt["A"])]
        az = kb.cnt["A"]
        for oc in range(8):
            w0 = ([("A", az)] + W(fr["psH"])) if oc == 0 else [("A", kb.cnt["A"])]
            kb.emit("tensor", lambda e, oc=oc: e.matmul(
                bH()[:, 0:512], wog2w[:, oc * 128:(oc + 1) * 128], og1f[:],
                start=True, stop=True), waits=w0, inc="T")
            tog = kb.cnt["T"]
            kb.emit("scalar", lambda e, oc=oc, tog=tog: e.activation(
                ogp[:, oc * 512:(oc + 1) * 512], bH()[:, 0:512], AF.Sigmoid),
                waits=[("T", tog)] + W(fr["og"]), inc="A")
        fr["psH"] = [("A", kb.cnt["A"])]

        def prologue(tk4):
            tk = s * TPS + tk4
            tb = tk % 2
            t0 = tk * 128
            if P2_TSEC < 1:
                return None
            tq = projgrp(tk, 0, 512, bG(), fr["psG"])
            kb.emit("scalar", lambda e, tq=tq: e.activation(
                qtmp[:, 0:512], bG()[:, 0:512], AF.Silu), waits=[("T", tq)] + W(fr["qtmp"]), inc="A")
            fr["psG"] = [("A", kb.cnt["A"])]
            tq = projgrp(tk, 512, 1024, bH(), fr["psH"])
            kb.emit("scalar", lambda e, tq=tq: e.activation(
                qtmp[:, 512:1024], bH()[:, 0:512], AF.Silu), waits=[("T", tq)], inc="A")
            fr["psH"] = [("A", kb.cnt["A"])]
            tq = projgrp(tk, 1024, 1536, bG(), fr["psG"])
            kb.emit("scalar", lambda e, tq=tq: e.activation(
                ktmp[:, 0:512], bG()[:, 0:512], AF.Silu), waits=[("T", tq)] + W(fr["ktmp"]), inc="A")
            fr["psG"] = [("A", kb.cnt["A"])]
            tq = projgrp(tk, 1536, 2048, bH(), fr["psH"])
            kb.emit("scalar", lambda e, tq=tq: e.activation(
                ktmp[:, 512:1024], bH()[:, 0:512], AF.Silu), waits=[("T", tq)], inc="A")
            fr["psH"] = [("A", kb.cnt["A"])]
            kready = kb.cnt["A"]
            for half in range(2):
                tq = projgrp(tk, 2048 + half * 512, 2560 + half * 512,
                             bG() if half == 0 else bH(),
                             fr["psG"] if half == 0 else fr["psH"])
                base = half * 1024
                kb.emit("vector", lambda e, tq=tq, base=base, half=half, tb=tb: e.tensor_copy(
                    vtm2[:, tb * 2048:(tb + 1) * 2048].rearrange(
                        "p (q c) -> p q c", c=64)[:, (base // 64 + 1)::4, :][:, 0:4, :],
                    (bG() if half == 0 else bH())[:, 0:512].rearrange(
                        "p (a b) -> p a b", b=64)[:, 0::2, :]),
                    waits=[("T", tq)] + (W(fr["vtm"][tb]) if half == 0 else []), inc="V")
                kb.emit("vector", lambda e, base=base, half=half, tb=tb: e.tensor_copy(
                    vtm2[:, tb * 2048:(tb + 1) * 2048].rearrange(
                        "p (q c) -> p q c", c=64)[:, (base // 64 + 2)::4, :][:, 0:4, :],
                    (bG() if half == 0 else bH())[:, 0:512].rearrange(
                        "p (a b) -> p a b", b=64)[:, 1::2, :]), inc="V")
                if half == 0:
                    fr["psG"] = [("V", kb.cnt["V"])]
                else:
                    fr["psH"] = [("V", kb.cnt["V"])]
            tq = projgrp(tk, 3072, 3088, bG(), fr["psG"])
            kb.emit("scalar", lambda e, tq=tq: e.activation(
                gtmp[:, 0:16], bG()[:, 0:16], AF.Sigmoid),
                waits=[("T", tq)] + W(fr["gtmp"]), inc="A")
            fr["psG"] = [("A", kb.cnt["A"])]
            if P2_TSEC < 2:
                return None
            for hf in range(2):
                bank, bn = (bH(), "psH") if hf == 0 else (bG(), "psG")
                kb.emit("tensor", lambda e, hf=hf, tk4=tk4, bank=bank: e.matmul(
                    bank[:, 0:512], z1f[:, tk4 * 128:(tk4 + 1) * 128],
                    wf2w[:, hf * 512:(hf + 1) * 512], start=True, stop=True),
                    waits=W(fr[bn], fr["scr"], fr["gkb"]), inc="T")
                tzz = kb.cnt["T"]
                kb.emit("scalar", lambda e, bank=bank: e.activation(scr1[:], bank[:, 0:512], AF.Abs),
                        waits=[("T", tzz)])
                aabs = kb.cnt["A"]
                kb.emit("scalar", lambda e: e.activation(scr1[:], scr1[:], AF.Exp, scale=-1.0))
                kb.emit("scalar", lambda e: e.activation(
                    scr1[:], scr1[:], AF.Ln, bias=kb.tens["c_actc"][:, 1:2]), inc="A")
                aln = kb.cnt["A"]
                kb.emit("vector", lambda e, bank=bank: e.tensor_scalar(
                    scr2[:], bank[:, 0:512], 0.0, None, op0=ALU.min),
                    waits=[("T", tzz), ("A", aabs)])
                kb.emit("vector", lambda e, hf=hf: e.tensor_tensor(
                    gkb[:, hf * 512:(hf + 1) * 512], scr2[:], scr1[:], op=ALU.subtract),
                    waits=[("A", aln)], inc="V")
                fr[bn] = [("V", kb.cnt["V"])]
                fr["scr"] = [("V", kb.cnt["V"])]
            vgk = kb.cnt["V"]
            if P2_TSEC < 3:
                return None
            for hf in range(2):
                for ch in (0, 64):
                    kb.emit("tensor", lambda e, ch=ch, hf=hf: e.matmul(
                        bG()[ch:ch + 64, 0:512], lcum[ch:ch + 64, :],
                        gkb[ch:ch + 64, hf * 512:(hf + 1) * 512],
                        start=True, stop=True, tile_position=(ch, ch), skip_group_check=True),
                        waits=([("V", vgk)] + W(fr["psG"], fr["emF"], fr["pack"][tb])
                               if ch == 0 else []))
                    kb.emit("tensor", lambda e, ch=ch, hf=hf: e.matmul(
                        bH()[ch:ch + 64, 0:512], lcum2[ch:ch + 64, :],
                        gkb[ch:ch + 64, hf * 512:(hf + 1) * 512],
                        start=True, stop=True, tile_position=(ch, ch), skip_group_check=True),
                        waits=(W(fr["psH"], fr["eagF"]) if ch == 0 else []),
                        inc=("T" if ch == 64 else None))
                tcs = kb.cnt["T"]
                kb.emit("scalar", lambda e, hf=hf, tb=tb, tcs=tcs: e.activation(
                    pack[:, tb * 4096 + 3 * 1024 + hf * 512: tb * 4096 + 3 * 1024 + hf * 512 + 512],
                    bG()[:, 0:512], AF.Exp), waits=[("T", tcs)], inc="A")
                kb.emit("scalar", lambda e, hf=hf: e.activation(
                    emF[:, hf * 512:(hf + 1) * 512], bG()[:, 0:512], AF.Exp, scale=-1.0), inc="A")
                fr["psG"] = [("A", kb.cnt["A"])]
                fr["emF"] = [("A", kb.cnt["A"])]
                kb.emit("scalar", lambda e, hf=hf: e.activation(
                    eagF[:, hf * 512:(hf + 1) * 512], bH()[:, 0:512], AF.Exp), inc="A")
                fr["psH"] = [("A", kb.cnt["A"])]
                fr["eagF"] = [("A", kb.cnt["A"])]
            aexp = kb.cnt["A"]
            if P2_TSEC < 4:
                return None
            kb.emit("scalar", lambda e: e.square(knk[:], ktmp[:]),
                    waits=[("A", kready)] + W(fr["knk"]), inc="A")
            fr["ktmp"] = [("A", kb.cnt["A"])]
            asq = kb.cnt["A"]
            kb.emit("vector", lambda e: e.reduce_sum(
                kn[:, 0:16], knk.rearrange("p (h j) -> p h j", h=16)[:, :, :], axis=AX.X),
                waits=[("A", asq)], inc="V")
            vkn = kb.cnt["V"]
            kb.emit("scalar", lambda e: e.activation(kn[:, 16:32], kn[:, 0:16], AF.Sqrt),
                    waits=[("V", vkn)], inc="A")
            akn = kb.cnt["A"]
            kb.emit("vector", lambda e: e.tensor_scalar(
                kn[:, 32:48], kn[:, 16:32], 1e-12, None, op0=ALU.max), waits=[("A", akn)])
            kb.emit("vector", lambda e: e.reciprocal(kn[:, 48:64], kn[:, 32:48]))
            kb.emit("vector", lambda e: e.tensor_tensor(
                knk.rearrange("p (h j) -> p h j", h=16),
                ktmp.rearrange("p (h j) -> p h j", h=16),
                kn[:, 48:64].unsqueeze(-1).broadcast_to((128, 16, 64)), op=ALU.mult), inc="V")
            fr["ktmp"] = fr["ktmp"] + [("V", kb.cnt["V"])]
            kb.emit("vector", lambda e, tb=tb: e.tensor_tensor(
                pack[:, tb * 4096:tb * 4096 + 1024], knk[:], emF[:], op=ALU.mult),
                waits=[("A", aexp)], inc="V")
            fr["emF"] = [("V", kb.cnt["V"])]
            kb.emit("vector", lambda e: e.scalar_tensor_tensor(
                eagF.rearrange("p (h j) -> p h j", h=16),
                eagF.rearrange("p (h j) -> p h j", h=16), -1.0,
                gtmp[:, 0:16].unsqueeze(-1).broadcast_to((128, 16, 64)),
                op0=ALU.mult, op1=ALU.mult))
            fr["gtmp"] = [("V", kb.cnt["V"])]
            kb.emit("vector", lambda e, tb=tb: e.tensor_tensor(
                pack[:, tb * 4096 + 1024:tb * 4096 + 2048], knk[:], eagF[:], op=ALU.mult), inc="V")
            fr["eagF"] = [("V", kb.cnt["V"])]
            fr["knk"] = [("V", kb.cnt["V"])]
            kb.emit("vector", lambda e, tb=tb: e.tensor_copy(
                vtm2[:, tb * 2048:(tb + 1) * 2048].rearrange("p (q c) -> p q c", c=64)[:, 0::4, :],
                pack[:, tb * 4096 + 1024:tb * 4096 + 2048].rearrange(
                    "p (a b) -> p a b", b=64)[:, 0::2, :]), inc="V")
            kb.emit("vector", lambda e, tb=tb: e.tensor_copy(
                vtm2[:, tb * 2048:(tb + 1) * 2048].rearrange("p (q c) -> p q c", c=64)[:, 3::4, :],
                pack[:, tb * 4096 + 1024:tb * 4096 + 2048].rearrange(
                    "p (a b) -> p a b", b=64)[:, 1::2, :]), inc="V")
            kb.emit("vector", lambda e, tb=tb: e.tensor_tensor(
                pack[:, tb * 4096 + 2 * 1024:tb * 4096 + 3 * 1024], qtmp[:],
                pack[:, tb * 4096 + 3 * 1024:tb * 4096 + 4 * 1024], op=ALU.mult), inc="V")
            fr["qtmp"] = [("V", kb.cnt["V"])]
            vpack = kb.cnt["V"]
            apack = kb.cnt["A"]
            if P2_TSEC < 5:
                return None
            kb.emit("sync", lambda e, tb=tb: e.dma_start_transpose(
                fmp[:, tb * 4096:(tb + 1) * 4096].rearrange("p (m r) -> p m r", r=128),
                pack[:, tb * 4096:(tb + 1) * 4096]),
                waits=[("V", vpack), ("A", apack)] + W(fr["fmpc"][tb]), inc="DT")
            return kb.cnt["DT"]

        def waves(tk4, dtf):
            if dtf is None:
                return
            tk = s * TPS + tk4
            tb = tk % 2
            kb.wait("tensor", [("DT", dtf)])
            hl = list(range(16)) if P2_HEADS is None else list(P2_HEADS)
            pairs = [tuple(hl[i:i + 2]) for i in range(0, len(hl), 2)]
            for l in (0, 1):
                frl[l]["psS"] = frl[l]["psS"] + fr["psG"]
                frl[l]["psO"] = frl[l]["psO"] + fr["psH"]
            for pair in pairs:
                lanes = [(h % 2, h) for h in pair]
                l1_tick()
                for l, h in lanes:
                    fl = frl[l]; B = LN[l]["bank"]
                    jr = (h % 2) * 64; jc = h // 2
                    kt0 = tb * 4096 + jc * 128
                    at0 = tb * 4096 + 1024 + jc * 128
                    for cpar in range(2):
                        r0 = cpar * 64
                        kb.emit("tensor", lambda e, jr=jr, jc=jc, kt0=kt0, r0=r0, tb=tb, B=B: e.matmul(
                            B[r0:r0 + 64, 0:128],
                            fmp[jr:jr + 64, kt0 + r0:kt0 + r0 + 64],
                            fmp4[jr:jr + 64, tb, 1:3, jc, r0:r0 + 64],
                            start=True, stop=True, tile_position=(jr, r0), skip_group_check=True),
                            waits=(W(fl["bA"], fl["psS"], fl["psO"]) if cpar == 0 else []))
                        kb.emit("tensor", lambda e, jr=jr, kt0=kt0, at0=at0, r0=r0, B=B: e.matmul(
                            B[r0:r0 + 64, 512:576],
                            fmp[jr:jr + 64, at0 + r0:at0 + r0 + 64],
                            fmp[jr:jr + 64, kt0 + r0:kt0 + r0 + 64],
                            start=True, stop=True, tile_position=(jr, r0), skip_group_check=True),
                            inc=("T" if cpar == 1 else None))
                    fl["tg"] = kb.cnt["T"]
                l1_tick()
                for l, h in lanes:
                    fl = frl[l]; B = LN[l]["bank"]
                    kb.emit("vector", lambda e, B=B, ct=LN[l]["ct"]: e.tensor_tensor(
                        ct[:], B[:, 0:128], maskct[:], op=ALU.mult),
                        waits=[("T", fl["tg"])], inc="V")
                    kb.emit("vector", lambda e, B=B, cu=LN[l]["cu"]: e.tensor_tensor(
                        cu[:], B[:, 512:576], maskc[:], op=ALU.mult), inc="V")
                    fl["vmask"] = kb.cnt["V"]
                    fl["pow"] = [("V", fl["vmask"])]
                for l, h in lanes:
                    fl = frl[l]; xo = LN[l]["xoff"]
                    for cpar in range(2):
                        r0 = cpar * 64
                        kb.emit("tensor", lambda e, h=h, r0=r0, xo=xo, tb=tb: e.matmul(
                            px2[r0:r0 + 64, xo:xo + 128], identb[r0:r0 + 64, r0:r0 + 64],
                            vtm2[r0:r0 + 64, tb * 2048 + h * 128:tb * 2048 + h * 128 + 128],
                            start=True, stop=True, tile_position=(r0, r0), skip_group_check=True),
                            waits=(W(fl["bX"]) if cpar == 0 else []))
                        kb.emit("tensor", lambda e, h=h, r0=r0, xo=xo, tb=tb, ct=LN[l]["ct"]: e.matmul(
                            px2[r0:r0 + 64, xo:xo + 128], ct[r0:r0 + 64, 0:64],
                            vtm2[r0:r0 + 64, tb * 2048 + h * 128:tb * 2048 + h * 128 + 128],
                            start=False, stop=True, tile_position=(r0, r0), skip_group_check=True),
                            waits=([("V", fl["vmask"])] if cpar == 0 else []),
                            inc=("T" if cpar == 1 else None))
                    fl["tx"] = kb.cnt["T"]
                for l, h in lanes:
                    fl = frl[l]; xo = LN[l]["xoff"]
                    kb.emit("scalar", lambda e, xo=xo, xwl=LN[l]["xw"]: e.copy(
                        xwl[:, 128:256], px2[:, xo:xo + 128]),
                        waits=[("T", fl["tx"])], inc="A")
                    fl["axw"] = kb.cnt["A"]
                for j in range(1, 6):
                    l1_tick()
                    for l, h in lanes:
                        fl = frl[l]; B = LN[l]["bank"]; pw = LN[l]["pw"]
                        for cpar in range(2):
                            r0 = cpar * 64
                            if j == 1:
                                ctp = LN[l]["ct"][r0:r0 + 64, 0:64]
                                cpp = LN[l]["cu"][r0:r0 + 64, 0:64]
                            else:
                                pbq = ((j - 1) % 2) * 128
                                ctp = pw[r0:r0 + 64, pbq:pbq + 64]
                                cpp = pw[r0:r0 + 64, pbq + 64:pbq + 128]
                            kb.emit("tensor", lambda e, ctp=ctp, cpp=cpp, r0=r0, B=B: e.matmul(
                                B[r0:r0 + 64, 0:64], cpp, ctp,
                                start=True, stop=True, tile_position=(r0, r0), skip_group_check=True),
                                waits=(W(fl["pow"]) if cpar == 0 else []))
                            kb.emit("tensor", lambda e, ctp=ctp, cpp=cpp, r0=r0, B=B: e.matmul(
                                B[r0:r0 + 64, 512:576], ctp, cpp,
                                start=True, stop=True, tile_position=(r0, r0), skip_group_check=True),
                                inc=("T" if cpar == 1 else None))
                        fl["tsq"] = kb.cnt["T"]
                    for l, h in lanes:
                        fl = frl[l]; B4 = LN[l]["bank"]; pw = LN[l]["pw"]
                        pb2 = (j % 2) * 128
                        kb.emit("vector", lambda e, pb2=pb2, B4=B4, pw=pw, tsq=fl["tsq"]: e.tensor_copy(
                            pw[:, pb2:pb2 + 128].rearrange("p (a b) -> p a b", b=64),
                            B4.rearrange("p (a b) -> p a b", b=64)[:, 0::8, :]),
                            waits=[("T", fl["tsq"])], inc="V")
                        fl["vpow"] = kb.cnt["V"]
                        fl["pow"] = [("V", fl["vpow"])]
                    l1_tick()
                    for l, h in lanes:
                        fl = frl[l]; pw = LN[l]["pw"]; xwl = LN[l]["xw"]; xo = LN[l]["xoff"]
                        pb2 = (j % 2) * 128
                        for cpar in range(2):
                            r0 = cpar * 64
                            lh = pw[r0:r0 + 64, pb2:pb2 + 64]
                            rh = xwl[r0:r0 + 64, (j % 2) * 128:(j % 2) * 128 + 128]
                            kb.emit("tensor", lambda e, lh=lh, rh=rh, r0=r0, xo=xo: e.matmul(
                                px2[r0:r0 + 64, xo:xo + 128], lh, rh,
                                start=False, stop=True, tile_position=(r0, r0), skip_group_check=True),
                                waits=([("V", fl["vpow"]), ("A", fl["axw"])] if cpar == 0 else []),
                                inc=("T" if cpar == 1 else None))
                        fl["taj"] = kb.cnt["T"]
                    for l, h in lanes:
                        fl = frl[l]; xwl = LN[l]["xw"]; xo = LN[l]["xoff"]
                        kb.emit("scalar", lambda e, j=j, xo=xo, xwl=xwl, t=fl["taj"]: e.copy(
                            xwl[:, ((j + 1) % 2) * 128:((j + 1) % 2) * 128 + 128],
                            px2[:, xo:xo + 128]),
                            waits=[("T", fl["taj"])], inc="A")
                        fl["axw"] = kb.cnt["A"]
                for cpar in range(2):
                    r0 = cpar * 64
                    l1_tick()
                    for l, h in lanes:
                        fl = frl[l]; B = LN[l]["bank"]; xwl = LN[l]["xw"]
                        jr = (h % 2) * 64; jc = h // 2
                        jo = jr ^ 64
                        wa_off = 0 if h % 2 == 0 else 64
                        wv_off = 64 if h % 2 == 0 else 0
                        wv = xwl[r0:r0 + 64, wv_off:wv_off + 64]
                        wa = xwl[r0:r0 + 64, wa_off:wa_off + 64]
                        ktm = pack[r0:r0 + 64, tb * 4096 + h * 64:tb * 4096 + h * 64 + 64]
                        ptc = LN[l]["ct"][r0:r0 + 64, 64:128]
                        w0 = (([("V", fl["vpow"]), ("A", fl["axw"])] if cpar == 0
                               else W(fl["psS"], fl["psO"])) + W(fl["sumev"]))
                        kb.emit("tensor", lambda e, r0=r0, jr=jr, wv=wv, ptc=ptc, B=B: e.matmul(
                            B[jr:jr + 64, 0:64], wv, ptc,
                            start=True, stop=True, tile_position=(r0, jr), skip_group_check=True),
                            waits=w0)
                        kb.emit("tensor", lambda e, r0=r0, jr=jr, wa=wa, ptc=ptc, B=B: e.matmul(
                            B[jr:jr + 64, 512:576], wa, ptc,
                            start=True, stop=True, tile_position=(r0, jr), skip_group_check=True))
                        kb.emit("tensor", lambda e, r0=r0, jo=jo, ktm=ktm, wv=wv, B=B: e.matmul(
                            B[jo:jo + 64, 0:64], ktm, wv,
                            start=True, stop=True, tile_position=(r0, jo), skip_group_check=True))
                        kb.emit("tensor", lambda e, r0=r0, jo=jo, ktm=ktm, wa=wa, B=B: e.matmul(
                            B[jo:jo + 64, 512:576], wa, ktm,
                            start=True, stop=True, tile_position=(r0, jo), skip_group_check=True),
                            inc="T")
                        ts1 = kb.cnt["T"]
                        kb.emit("scalar", lambda e, jr=jr, ts1=ts1, B=B: e.copy(
                            sumb[jr:jr + 64, 0:64], B[jr:jr + 64, 0:64]),
                            waits=[("T", ts1)])
                        kb.emit("scalar", lambda e, jr=jr, B=B: e.copy(
                            sumb[jr:jr + 64, 64:128], B[jr:jr + 64, 512:576]))
                        kb.emit("scalar", lambda e, jr=jr, jo=jo, B=B: e.copy(
                            sumb[jr:jr + 64, 128:192], B[jo:jo + 64, 0:64]))
                        kb.emit("scalar", lambda e, jr=jr, jo=jo, B=B: e.copy(
                            sumb[jr:jr + 64, 192:256], B[jo:jo + 64, 512:576]), inc="A")
                        as2 = kb.cnt["A"]
                        fl["sumev"] = [("A", as2)]
                        fl["bA"] = [("A", as2)]
                    l1_tick()
                    for l, h in lanes:
                        fl = frl[l]; B = LN[l]["bank"]
                        jr = (h % 2) * 64; jc = h // 2
                        qbT = fmp[jr:jr + 64, tb * 4096 + 2 * 1024 + jc * 128 + r0:
                                  tb * 4096 + 2 * 1024 + jc * 128 + r0 + 64]
                        kb.emit("tensor", lambda e, jr=jr, jc=jc, B=B: e.matmul(
                            B[jr:jr + 64, 0:64], identb[jr:jr + 64, jr:jr + 64],
                            states[jr:jr + 64, jc * 64:jc * 64 + 64],
                            start=True, stop=False, tile_position=(jr, jr), skip_group_check=True),
                            waits=W(fl["sumev"], fl["psS"], fl["chain"]))
                        kb.emit("tensor", lambda e, jr=jr, jc=jc, B=B: e.matmul(
                            B[jr:jr + 64, 0:64], sumb[jr:jr + 64, 192:256],
                            states[jr:jr + 64, jc * 64:jc * 64 + 64],
                            start=False, stop=False, tile_position=(jr, jr), skip_group_check=True))
                        kb.emit("tensor", lambda e, jr=jr, B=B: e.matmul(
                            B[jr:jr + 64, 0:64], identb[jr:jr + 64, jr:jr + 64],
                            sumb[jr:jr + 64, 128:192],
                            start=False, stop=True, tile_position=(jr, jr), skip_group_check=True))
                        kb.emit("tensor", lambda e, jr=jr, jc=jc, B=B: e.matmul(
                            B[jr:jr + 64, 512:576], states[jr:jr + 64, jc * 64:jc * 64 + 64],
                            sumb[jr:jr + 64, 64:128],
                            start=True, stop=False, tile_position=(jr, jr), skip_group_check=True),
                            waits=W(fl["psO"]))
                        kb.emit("tensor", lambda e, jr=jr, jc=jc, qbT=qbT, B=B: e.matmul(
                            B[jr:jr + 64, 512:576], states[jr:jr + 64, jc * 64:jc * 64 + 64],
                            qbT,
                            start=False, stop=False, tile_position=(jr, jr), skip_group_check=True))
                        kb.emit("tensor", lambda e, jr=jr, B=B: e.matmul(
                            B[jr:jr + 64, 512:576], identb[jr:jr + 64, jr:jr + 64],
                            sumb[jr:jr + 64, 0:64],
                            start=False, stop=True, tile_position=(jr, jr), skip_group_check=True),
                            inc="T")
                        fl["tch"] = kb.cnt["T"]
                    for l, h in lanes:
                        fl = frl[l]; B = LN[l]["bank"]
                        jr = (h % 2) * 64; jc = h // 2
                        exoff = tb * 4096 + 3 * 1024 + jc * 128 + r0 + 63
                        kb.emit("vector", lambda e, jr=jr, jc=jc, exoff=exoff, B=B: e.tensor_tensor(
                            states[jr:jr + 64, jc * 64:jc * 64 + 64], B[jr:jr + 64, 0:64],
                            fmp[jr:jr + 64, exoff:exoff + 1].broadcast_to((64, 64)),
                            op=ALU.mult), waits=[("T", fl["tch"])], inc="V")
                        fl["psS"] = [("V", kb.cnt["V"])]
                        ocol = jc * 512 + tk4 * 128 + r0
                        kb.emit("vector", lambda e, jr=jr, ocol=ocol, B=B: e.tensor_tensor(
                            osg[jr:jr + 64, ocol:ocol + 64], B[jr:jr + 64, 512:576],
                            ogp[jr:jr + 64, ocol:ocol + 64],
                            op=ALU.mult), waits=W(fr["osg"]), inc="V")
                        fl["psO"] = [("V", kb.cnt["V"])]
                        fl["chain"] = [("V", kb.cnt["V"])]
                for l, h in lanes:
                    frl[l]["bX"] = [("A", frl[l]["axw"])]
            fr["og"] = [("V", kb.cnt["V"])]
            fr["psG"] = [("V", kb.cnt["V"])]
            fr["psH"] = [("V", kb.cnt["V"])]
            fr["vtm"][tb] = [("T", kb.cnt["T"])]
            fr["pack"][tb] = [("T", kb.cnt["T"]), ("DT", kb.cnt["DT"])]
            fr["fmpc"][tb] = [("T", kb.cnt["T"]), ("V", kb.cnt["V"]), ("G", kb.cnt["G"])]
            fr["gkb"] = [("T", kb.cnt["T"])]

        dtn = {0: prologue(0)}
        for tk4 in range(TPS):
            if tk4 + 1 < TPS:
                dtn[tk4 + 1] = prologue(tk4 + 1)
            waves(tk4, dtn[tk4])
        vlast = kb.cnt["V"]
        for jc in range(8):
            kb.emit("scalar", lambda e, jc=jc, s0=s0: e.dma_start(
                out=io["o_dram"][jc, :, s0:s0 + 512], in_=osg[:, jc * 512:(jc + 1) * 512]),
                waits=([("V", vlast)] if jc == 0 else []), inc="DS")
        fr["osg"] = [("DS", kb.cnt["DS"])]
    while l1_q["u"] < 256:
        l1_tick()
    env["p2_done"] = True


def phase_hidden(kb, io, in_name, out_name, wt_name, sc_name, pstk):
    tag = wt_name
    L = lambda n, sh, dt: kb.sb(n + tag, sh, dt, stack=pstk)
    hblk = L("lh_hblk", [128, 32 * 2048], BF16)
    wbb = L("lh_wbb", [128, 2 * 4096], BF16)
    gout = L("lh_gout", [128, 2 * 512], BF16)
    scc = L("lh_sc", [128, 32], F32)
    psb = [kb.ps("lh_ps%d%s" % (i, tag), stack=pstk) for i in range(2)]
    kb.emit("scalar", lambda e: e.dma_start(out=scc[:], in_=io[sc_name][:]), inc="DS")
    scc_ds = kb.cnt["DS"]

    steps = [(TT, oc) for TT in range(2) for oc in range(32)]
    if LH_LIM is not None:
        steps = steps[:LH_LIM]
    ldq = {}
    st_tail = {0: 0, 1: 0}
    evA_tail = {0: 0, 1: 0}
    mm_tail = {0: 0, 1: 0}
    hld = {}

    def load_for(i):
        if i >= len(steps) or i in ldq:
            return
        TT, oc = steps[i]
        b = i % 2
        sm = "DG" if b == 0 else "DH"
        v = kb.emit("gpsimd", lambda e, oc=oc, b=b: e.dma_start(
            out=wbb[:, b * 4096:(b + 1) * 4096], in_=io[wt_name][oc]),
            waits=[("T", mm_tail[b])], inc=sm)
        ldq[i] = (sm, v)

    load_for(0)
    for i, (TT, oc) in enumerate(steps):
        b = i % 2
        if oc == 0:
            tguard = kb.cnt["T"]
            for dc in range(32):
                kb.emit("scalar", lambda e, dc=dc, TT=TT: e.dma_start(
                    out=hblk[:, dc * 2048:(dc + 1) * 2048],
                    in_=io[in_name][dc, :, TT * 2048:TT * 2048 + 2048]),
                    waits=([("T", tguard)] if dc == 0 else []), inc="DS")
            hld[TT] = kb.cnt["DS"]
        load_for(i + 1)
        for ts in range(4):
            pb = ts % 2
            w0 = ([ldq[i], ("DS", hld[TT])] if ts == 0 else []) + [("A", evA_tail[pb])]
            for dc in range(32):
                kb.emit("tensor", lambda e, dc=dc, b=b, pb=pb, ts=ts, TT=TT, first=(dc == 0): e.matmul(
                    psb[pb][:, :], wbb[:, b * 4096 + dc * 128:b * 4096 + dc * 128 + 128],
                    hblk[:, dc * 2048 + ts * 512:dc * 2048 + ts * 512 + 512],
                    start=first, stop=(dc == 31)),
                    waits=w0 if dc == 0 else [], inc=("T" if dc == 31 else None))
            tmm = kb.cnt["T"]
            kb.emit("scalar", lambda e, pb=pb, oc=oc, tmm=tmm: e.activation(
                gout[:, pb * 512:(pb + 1) * 512], psb[pb][:, :], AF.Gelu,
                scale=scc[:, oc:oc + 1]),
                waits=[("T", tmm), ("DS", st_tail[pb])]
                + ([("DS", scc_ds)] if i == 0 and ts == 0 else []), inc="A")
            evA_tail[pb] = kb.cnt["A"]
            kb.emit("scalar", lambda e, pb=pb, oc=oc, t0=TT * 2048 + ts * 512: e.dma_start(
                out=io[out_name][oc, :, t0:t0 + 512], in_=gout[:, pb * 512:(pb + 1) * 512]),
                waits=[("A", evA_tail[pb])], inc="DS")
            st_tail[pb] = kb.cnt["DS"]
        mm_tail[b] = kb.cnt["T"]


def phase_final(kb, io, pstk):
    L = lambda n, sh, dt: kb.sb(n, sh, dt, stack=pstk)
    w3b = L("f_w3b", [128, 8 * 4096], BF16)
    wob = L("f_wob", [128, 8 * 1024], BF16)
    g2b = L("f_g2b", [128, 2 * 32 * 512], BF16)
    olod = L("f_olod", [128, 8 * 512], BF16)
    olnb = L("f_oln", [128, 8 * 512], BF16)
    osq = L("f_osq", [128, 512], BF16)
    ysb = L("f_ysb", [128, 2 * 512], F32)
    xfb = L("f_xfb", [128, 2 * 512], F32)
    stt = L("f_stats", [2, 6 * 512], F32)
    s3c = L("f_s3", [128, 8], F32)
    ones = kb.tens["c_ones"]
    onesf = kb.tens["c_onesf"]
    nwc = kb.tens["c_nw"]
    pFs = [kb.ps("f_pF%d" % i, stack=pstk) for i in range(2)]
    pWs = [kb.ps("f_pW%d" % i, stack=pstk) for i in range(2)]
    pS = kb.ps("f_pS", (2, 512), stack=pstk)
    pS2 = kb.ps("f_pS2", (2, 512), stack=pstk)
    pR = kb.ps("f_pR", stack=pstk)
    pM = kb.ps("f_pM", stack=pstk)

    kb.emit("scalar", lambda e: e.dma_start(out=s3c[:], in_=io["s3"][:]), inc="DS")
    for oc in range(8):
        kb.emit("gpsimd", lambda e, oc=oc: e.dma_start(
            out=w3b[:, oc * 4096:(oc + 1) * 4096], in_=io["w3t"][oc]), inc="DG")
        kb.emit("gpsimd", lambda e, oc=oc: e.dma_start(
            out=wob[:, oc * 1024:(oc + 1) * 1024], in_=io["wot"][oc]), inc="DG")
    cast_done = kb.cnt["DG"]
    kb.wait("tensor", [("DG", cast_done)])

    MU, S2N, VAR, SQ, RSTD, NMU = (stt[0:1, i * 512:(i + 1) * 512] for i in range(6))
    ev_prev = 0
    yst_tail = {0: 0, 1: 0}
    aw_tail = {0: 0, 1: 0}
    g2_rd = {0: 0, 1: 0}
    for tt in range(8):
        t0 = tt * 512
        gb = (tt % 2)
        for jc in range(8):
            kb.emit("scalar", lambda e, jc=jc, t0=t0: e.dma_start(
                out=olod[:, jc * 512:(jc + 1) * 512], in_=io["o_dram"][jc, :, t0:t0 + 512]),
                waits=([("V", ev_prev)] if jc == 0 else []), inc="DS")
        old = kb.cnt["DS"]
        gsm = "DH" if gb == 0 else "DG"
        for dc in range(32):
            w = [("T", g2_rd[gb])] if dc == 0 else []
            if dc == 0 and gsm == "DG" and tt == 1:
                w = w + [("DG", cast_done)]
            kb.emit("gpsimd", lambda e, dc=dc, t0=t0, gb=gb: e.dma_start(
                out=g2b[:, gb * 16384 + dc * 512:gb * 16384 + (dc + 1) * 512],
                in_=io["h_dram"][dc, :, t0:t0 + 512]),
                waits=w, inc=gsm)
        g2ld = (gsm, kb.cnt[gsm])
        sq_guard = 0
        for jc in range(8):
            kb.emit("scalar", lambda e, jc=jc: e.square(
                osq[:], olod[:, jc * 512:(jc + 1) * 512]),
                waits=([("DS", old)] if jc == 0 else []) + [("T", sq_guard)], inc="A")
            aq = kb.cnt["A"]
            kb.emit("tensor", lambda e, jc=jc, first=(jc == 0): e.matmul(
                pS[0:1, :], ones[:, 0:1], olod[:, jc * 512:(jc + 1) * 512],
                start=first, stop=(jc == 7)), waits=[("DS", old)] if jc == 0 else [])
            kb.emit("tensor", lambda e, jc=jc, first=(jc == 0), aq=aq: e.matmul(
                pS2[0:1, :], ones[:, 0:1], osq[:],
                start=first, stop=(jc == 7)), waits=[("A", aq)], inc="T")
            sq_guard = kb.cnt["T"]
        tsum = kb.cnt["T"]
        kb.emit("vector", lambda e: e.tensor_scalar(MU, pS[0:1, :], 1.0 / D, None, op0=ALU.mult),
                waits=[("T", tsum)], chain=True)
        kb.emit("vector", lambda e: e.tensor_scalar(S2N, pS2[0:1, :], 1.0 / D, None, op0=ALU.mult),
                chain=True)
        kb.emit("vector", lambda e: e.tensor_tensor(VAR, MU, MU, op=ALU.mult), chain=True)
        kb.emit("vector", lambda e: e.tensor_tensor(VAR, S2N, VAR, op=ALU.subtract), chain=True)
        vvar = kb.cnt["V"]
        kb.emit("scalar", lambda e: e.activation(SQ, VAR, AF.Sqrt, bias=kb.tens["c_actc"][0:1, 0:1], scale=1.0),
                waits=[("V", vvar)], inc="A")
        asq2 = kb.cnt["A"]
        kb.emit("vector", lambda e: e.reciprocal(RSTD, SQ), waits=[("A", asq2)], chain=True)
        kb.emit("vector", lambda e: e.tensor_scalar(NMU, MU, -1.0, None, op0=ALU.mult), chain=True)
        vst = kb.cnt["V"]
        kb.emit("tensor", lambda e: e.matmul(pR[:, :], onesf[0:1, :], RSTD, start=True, stop=True),
                waits=[("V", vst), ("V", ev_prev)])
        kb.emit("tensor", lambda e: e.matmul(pM[:, :], onesf[0:1, :], NMU, start=True, stop=True),
                inc="T")
        trep = kb.cnt["T"]
        for jc in range(8):
            kb.emit("vector", lambda e, jc=jc: e.tensor_tensor(
                olnb[:, jc * 512:(jc + 1) * 512], olod[:, jc * 512:(jc + 1) * 512],
                pM[:, :], op=ALU.add), waits=[("T", trep)] if jc == 0 else [], chain=True)
            kb.emit("vector", lambda e, jc=jc: e.scalar_tensor_tensor(
                olnb[:, jc * 512:(jc + 1) * 512], olnb[:, jc * 512:(jc + 1) * 512],
                nwc[:, jc:jc + 1], pR[:, :], op0=ALU.mult, op1=ALU.mult), chain=True)
        ev_prev = kb.cnt["V"]
        voln = kb.cnt["V"]
        for oc in range(8):
            ob = oc % 2
            kb.emit("scalar", lambda e, oc=oc, t0=t0, ob=ob: e.dma_start(
                out=xfb[:, ob * 512:(ob + 1) * 512], in_=io["xT"][oc * 128:(oc + 1) * 128, t0:t0 + 512]),
                waits=[("V", yst_tail[ob])], inc="DS")
            xld = kb.cnt["DS"]
            for dc in range(32):
                kb.emit("tensor", lambda e, dc=dc, oc=oc, gb=gb, ob=ob, first=(dc == 0): e.matmul(
                    pFs[ob][:, :], w3b[:, oc * 4096 + dc * 128:oc * 4096 + dc * 128 + 128],
                    g2b[:, gb * 16384 + dc * 512:gb * 16384 + dc * 512 + 512],
                    start=first, stop=(dc == 31)),
                    waits=[g2ld, ("V", yst_tail[ob]), ("A", aw_tail[ob])] if dc == 0 else [])
            for jc in range(8):
                kb.emit("tensor", lambda e, jc=jc, oc=oc, ob=ob, first=(jc == 0): e.matmul(
                    pWs[ob][:, :], wob[:, oc * 1024 + jc * 128:oc * 1024 + jc * 128 + 128],
                    olnb[:, jc * 512:(jc + 1) * 512],
                    start=first, stop=(jc == 7)),
                    waits=[("V", voln)] if jc == 0 else [], inc=("T" if jc == 7 else None))
            tmm = kb.cnt["T"]
            kb.emit("scalar", lambda e, oc=oc, ob=ob: e.copy(
                ysb[:, ob * 512:(ob + 1) * 512], pWs[ob][:, :]),
                waits=[("T", tmm), ("DS", xld)], inc="A")
            ayw = kb.cnt["A"]
            aw_tail[ob] = ayw
            kb.emit("vector", lambda e, oc=oc, ob=ob: e.scalar_tensor_tensor(
                ysb[:, ob * 512:(ob + 1) * 512], pFs[ob][:, :], s3c[:, oc:oc + 1],
                ysb[:, ob * 512:(ob + 1) * 512],
                op0=ALU.mult, op1=ALU.add), waits=[("A", ayw), ("DS", xld)], chain=True)
            kb.emit("vector", lambda e, oc=oc, ob=ob: e.tensor_tensor(
                ysb[:, ob * 512:(ob + 1) * 512], ysb[:, ob * 512:(ob + 1) * 512],
                xfb[:, ob * 512:(ob + 1) * 512], op=ALU.add), chain=True)
            yv = kb.cnt["V"]
            yst_tail[ob] = yv
            kb.emit("scalar", lambda e, oc=oc, t0=t0, ob=ob, yv=yv: e.dma_start(
                out=io["y"][oc, :, t0:t0 + 512], in_=ysb[:, ob * 512:(ob + 1) * 512]),
                waits=[("V", yv)], inc="DS")
        g2_rd[gb] = kb.cnt["T"]


def build_program(phases=("p0", "p2", "l2", "l3", "fin"), expose=(), p2_nseg=NSEG):
    nc = bass.Bass("TRN2", target_bir_lowering=False)
    io = declare_io(nc, expose)
    stack = ExitStack()
    with stack:
        kb = KB(nc, stack)
        env = None
        xn_stack = ExitStack()
        with ExitStack() as ps0:
            env = phase0_xn(kb, io, ps0, xn_stack=xn_stack)
        if "xn" in expose:
            v = kb.cnt["V"]
            kb.emit("scalar", lambda e: e.dma_start(out=io["xn_dump"][:], in_=kb.tens["xn_fm"][:]),
                    waits=[("V", v)], inc="DS")
        kb.fence()
        if "l1" in phases:
            with ExitStack() as ps:
                phase_l1(kb, io, env, ps)
            kb.fence()
        if "p2" in phases:
            with ExitStack() as ps:
                phase_p2(kb, io, env, ps, nseg=p2_nseg)
            kb.fence()
        xn_stack.close()
        if "l2" in phases:
            with ExitStack() as ps:
                phase_hidden(kb, io, "h_dram", "g_dram", "h1t", "sh1", ps)
            kb.fence()
        if "l3" in phases:
            with ExitStack() as ps:
                phase_hidden(kb, io, "g_dram", "h_dram", "h2t", "sh2", ps)
            kb.fence()
        if "fin" in phases:
            with ExitStack() as ps:
                phase_final(kb, io, ps)
        kb.wait("tensor", [("DS", kb.cnt["DS"]), ("DT", kb.cnt["DT"]), ("DG", kb.cnt["DG"]),
                           ("V", kb.cnt["V"]), ("A", kb.cnt["A"])])
        kb.finalize()
    return nc


def kernel(**inputs):
    cm, xs = host_prepare(inputs)
    nc = build_program()
    in_maps = []
    for b in range(8):
        m = dict(cm)
        m["xT"] = xs[b]
        in_maps.append(m)
    res = run_bass_kernel_spmd(nc, in_maps, core_ids=list(range(8)))
    outs = []
    for b in range(8):
        yfm = np.asarray(res.results[b]["y"], dtype=np.float32)
        outs.append(yfm.reshape(1024, 4096).T)
    return np.ascontiguousarray(np.stack(outs)).astype(np.float32)


# version 14
# speedup vs baseline: 1.0476x; 1.0177x over previous
import numpy as np
import ml_dtypes
import concourse.bass as bass
import concourse.mybir as mybir
from concourse.bass_utils import run_bass_kernel_spmd
from contextlib import ExitStack

BF16 = mybir.dt.bfloat16
F32 = mybir.dt.float32
AF = mybir.ActivationFunctionType
ALU = mybir.AluOpType
AX = mybir.AxisListType
BND = ml_dtypes.bfloat16

T, D, NH, HD, CH = 4096, 1024, 16, 64, 64
NT = T // 128
NSEG = 8
TPS = 4
DHID = 4096
P2_HEADS = None
P2_FENCE_HEADS = False
P2_DBG = None
P2_SEC = {}
P2_TSEC = 99
LH_LIM = None

def _tile_ocp(w):
    din, dout = w.shape
    dc, oc = din // 128, dout // 128
    return np.ascontiguousarray(
        w.reshape(dc, 128, oc, 128).transpose(2, 1, 0, 3).reshape(oc, 128, dc * 128))

def host_prepare(inputs):
    g = lambda n: np.ascontiguousarray(np.asarray(inputs[n], dtype=np.float32))
    cm = {}
    cm["w1t"] = _tile_ocp(g("L0_ffn_w1_w").T)
    cm["w2t"] = _tile_ocp(g("L0_ffn_w2_w").T)
    cm["h1t"] = _tile_ocp(g("L0_ffn_h1_w").T)
    cm["h2t"] = _tile_ocp(g("L0_ffn_h2_w").T)
    cm["w3t"] = _tile_ocp(g("L0_ffn_w3_w").T)
    cm["wot"] = _tile_ocp(g("L0_attn_Wo"))
    for nm, k in (("s1", "L0_ffn_w1_s"), ("s2", "L0_ffn_w2_s"),
                  ("sh1", "L0_ffn_h1_s"), ("sh2", "L0_ffn_h2_s"),
                  ("s3", "L0_ffn_w3_s"), ("nw", "L0_attn_norm_w")):
        v = g(k)
        cm[nm] = np.ascontiguousarray(v.reshape(-1, 128).T)
    qkvg = np.concatenate([g("L0_attn_Wq"), g("L0_attn_Wk"), g("L0_attn_Wv"),
                           g("L0_attn_Wgamma")], axis=1)
    cm["qkvg"] = np.ascontiguousarray(qkvg.reshape(8, 128, 3088))
    f1og1 = np.concatenate([g("L0_attn_Wf1"), g("L0_attn_Wog1")], axis=1)
    cm["f1og1"] = np.ascontiguousarray(f1og1.reshape(8, 128, 128))
    cm["wf2"] = g("L0_attn_Wf2")
    wog2 = g("L0_attn_Wog2")
    cm["wog2t"] = np.ascontiguousarray(wog2.reshape(64, 8, 128).transpose(1, 0, 2))
    cm["ident_b"] = np.eye(128).astype(BND)
    cm["ones_b"] = np.ones((128, 128), dtype=BND)
    cm["ones_f"] = np.ones((1, 128), dtype=np.float32)
    cm["actc"] = np.tile(np.array([[1e-5, 1.0]], np.float32), (128, 1))
    su = np.triu(np.ones((CH, CH), np.float32), 1)
    iu = np.triu(np.ones((CH, CH), np.float32), 0)
    sl = np.tril(np.ones((CH, CH), np.float32), -1)
    cm["maskct"] = np.ascontiguousarray(
        np.concatenate([np.concatenate([su, iu], 1)] * 2, 0)).astype(BND)
    cm["maskc"] = np.ascontiguousarray(np.concatenate([sl, sl], 0)).astype(BND)
    cm["lcum"] = np.ascontiguousarray(np.concatenate([iu] * 2, 0)).astype(BND)
    cm["lcum2"] = np.ascontiguousarray(
        np.concatenate([iu + np.eye(CH, dtype=np.float32)] * 2, 0)).astype(BND)
    i64d = np.concatenate([np.eye(64, dtype=np.float32)] * 2, 0)
    cm["i64d"] = i64d.astype(BND)
    xs = [np.ascontiguousarray(np.asarray(inputs["L0_attn_x"][b], np.float32).T)
          for b in range(8)]
    return cm, xs


class KB:
    ENGS = ("tensor", "vector", "scalar", "gpsimd", "sync")

    def __init__(self, nc, stack):
        self.nc = nc
        self.stack = stack
        self.streams = {e: [] for e in self.ENGS}
        self.sems = {}
        self.cnt = {}
        for s in ("T", "V", "A", "G", "F", "DS", "DT", "DG", "DH"):
            self.sems[s] = stack.enter_context(nc.semaphore("sem_" + s))
            self.cnt[s] = 0
        self.tens = {}
        self.waited = {s: set() for s in self.DMA_SEMS}

    def sb(self, name, shape, dtype, stack=None, side=None):
        kw = {"side": side} if side else {}
        t = (stack or self.stack).enter_context(self.nc.sbuf_tensor(name, shape, dtype, **kw))
        self.tens[name] = t
        return t

    def ps(self, name, shape=(128, 512), dtype=F32, stack=None):
        t = (stack or self.stack).enter_context(self.nc.psum_tensor(name, list(shape), dtype))
        self.tens[name] = t
        return t

    def fence(self):
        for eng in ("tensor", "vector", "scalar", "gpsimd"):
            self.emit(eng, lambda e: e.nop(), inc="F")
        snap = dict(self.cnt)
        waits = [(s, snap[s]) for s in ("F", "DS", "DT", "DG", "DH")]
        for eng in self.ENGS:
            self.wait(eng, waits)

    SELF_SEM = {"tensor": "T", "vector": "V", "scalar": "A", "gpsimd": "G"}
    DMA_SEMS = ("DS", "DT", "DG", "DH")

    def emit(self, eng, make, waits=(), inc=None, chain=False):
        v = None
        if eng in ("vector", "scalar", "gpsimd"):
            chain = True
        if chain:
            s = self.SELF_SEM[eng]
            waits = list(waits) + [(s, self.cnt[s])]
            if inc is None:
                inc = s
        if inc is not None:
            k = 16 if inc in self.DMA_SEMS else 1
            self.cnt[inc] += k
            v = self.cnt[inc]
        waits = [(s, w) for (s, w) in waits if w > 0]
        sems = self.sems
        kk = (16 if inc in self.DMA_SEMS else 1) if inc else 0
        def f(e, make=make, inc=inc, k=kk):
            for s, w in f.waits:
                e.wait_ge(sems[s], w)
            ins = make(e)
            if inc is not None:
                ins.then_inc(sems[inc], k)
        f.waits = waits
        f.inc_sem = inc
        f.inc_k = kk
        f.dma_sem = inc if inc in self.DMA_SEMS else None
        f.dma_count = v
        self.streams[eng].append(f)
        return v

    def wait(self, eng, waits):
        waits = [(s, w) for (s, w) in waits if w > 0]
        if not waits:
            return
        sems = self.sems
        def f(e):
            for s, w in f.waits:
                e.wait_ge(sems[s], w)
        f.waits = waits
        f.inc_sem = None
        f.inc_k = 0
        f.dma_sem = None
        f.dma_count = None
        self.streams[eng].append(f)

    def finalize(self):
        waited = {s: set() for s in self.DMA_SEMS}
        succ = {s: {} for s in self.DMA_SEMS}
        for eng in self.ENGS:
            for f in self.streams[eng]:
                for s, w in f.waits:
                    if s in self.DMA_SEMS:
                        waited[s].add(w)
                if f.dma_sem is not None:
                    succ[f.dma_sem][f.dma_count] = f
        for s, vals in waited.items():
            for v in vals:
                nf = succ[s].get(v + 16)
                if nf is not None and not any(
                        ss == s and ww >= v for ss, ww in nf.waits):
                    nf.waits = list(nf.waits) + [(s, v)]
        def run(e, fs):
            for f in fs:
                f(e)
        with self.nc.Block() as block:
            @block.tensor
            def _(e):
                run(e, self.streams["tensor"])
            @block.vector
            def _(e):
                run(e, self.streams["vector"])
            @block.scalar
            def _(e):
                run(e, self.streams["scalar"])
            @block.gpsimd
            def _(e):
                run(e, self.streams["gpsimd"])
            @block.sync
            def _(e):
                run(e, self.streams["sync"])


def declare_io(nc, expose=()):
    d = {}
    def di(name, shape, dtype, kind="ExternalInput"):
        if kind == "Internal" and name in expose:
            kind = "ExternalOutput"
        if kind == "Internal" and ("in:" + name) in expose:
            kind = "ExternalInput"
        d[name] = nc.dram_tensor(name, list(shape), dtype, kind=kind).ap()
    di("xT", (D, T), F32)
    di("qkvg", (8, 128, 3088), F32)
    di("f1og1", (8, 128, 128), F32)
    di("wf2", (64, D), F32)
    di("wog2t", (8, 64, 128), F32)
    di("w1t", (32, 128, 1024), F32)
    di("w2t", (32, 128, 1024), F32)
    di("h1t", (32, 128, 4096), F32)
    di("h2t", (32, 128, 4096), F32)
    di("w3t", (8, 128, 4096), F32)
    di("wot", (8, 128, 1024), F32)
    for nm, n in (("s1", 32), ("s2", 32), ("sh1", 32), ("sh2", 32), ("s3", 8), ("nw", 8)):
        di(nm, (128, n), F32)
    di("ident_b", (128, 128), BF16)
    di("ones_b", (128, 128), BF16)
    di("ones_f", (1, 128), F32)
    di("actc", (128, 2), F32)
    di("maskct", (128, 128), BF16)
    di("maskc", (128, 64), BF16)
    di("lcum", (128, 64), BF16)
    di("lcum2", (128, 64), BF16)
    di("i64d", (128, 64), BF16)
    di("y", (8, 128, T), F32, kind="ExternalOutput")
    di("h_dram", (32, 128, T), BF16, kind="Internal")
    di("g_dram", (32, 128, T), BF16, kind="Internal")
    di("o_dram", (8, 128, T), BF16, kind="Internal")
    if "dbg" in expose:
        di("dbg", (128, 4096), BF16, kind="ExternalOutput")
    if "xn" in expose:
        di("xn_dump", (128, 8 * T), BF16, kind="ExternalOutput")
    return d


def phase0_xn(kb, io, pstk, xn_stack=None):
    nc = kb.nc
    xn = kb.sb("xn_fm", [128, 8 * T], BF16, stack=xn_stack, side="right")
    ones = kb.sb("c_ones", [128, 128], BF16)
    onesf = kb.sb("c_onesf", [1, 128], F32)
    actc = kb.sb("c_actc", [128, 2], F32)
    identb = kb.sb("c_identb", [128, 128], BF16)
    maskct = kb.sb("c_maskct", [128, 128], BF16)
    maskc = kb.sb("c_maskc", [128, 64], BF16)
    lcum = kb.sb("c_lcum", [128, 64], BF16)
    lcum2 = kb.sb("c_lcum2", [128, 64], BF16)
    i64d = kb.sb("c_i64d", [128, 64], BF16)
    nwc = kb.sb("c_nw", [128, 8], F32)
    xf = kb.sb("p0_xf", [128, 2 * 8 * 512], F32, stack=pstk)
    xb = kb.sb("p0_xb", [128, 8 * 512], BF16, stack=pstk)
    x2b = kb.sb("p0_x2b", [128, 8 * 512], BF16, stack=pstk)
    st = kb.sb("p0_stats", [2, 6 * 512], F32, stack=pstk)
    psS = kb.ps("ps0_s", (2, 512), stack=pstk)
    psS2 = kb.ps("ps0_s2", (2, 512), stack=pstk)
    psR = kb.ps("ps0_r", (128, 512), stack=pstk)
    psM = kb.ps("ps0_m", (128, 512), stack=pstk)

    for nm, dst in (("ones_b", ones), ("ones_f", onesf), ("actc", actc), ("ident_b", identb),
                    ("maskct", maskct), ("maskc", maskc), ("lcum", lcum),
                    ("lcum2", lcum2), ("i64d", i64d), ("nw", nwc)):
        kb.emit("scalar", lambda e, dst=dst, nm=nm: e.dma_start(out=dst[:], in_=io[nm][:]), inc="DS")
    c_done = kb.cnt["DS"]

    ev_prev = 0
    tsum_prev = 0
    for tt in range(8):
        t0 = tt * 512
        pb = (tt % 2) * 8 * 512
        for dc in range(8):
            kb.emit("scalar", lambda e, dc=dc, t0=t0, pb=pb: e.dma_start(
                out=xf[:, pb + dc * 512: pb + (dc + 1) * 512],
                in_=io["xT"][dc * 128:(dc + 1) * 128, t0:t0 + 512]), inc="DS")
        ld = kb.cnt["DS"]
        for dc in range(8):
            vv = kb.emit("vector", lambda e, dc=dc, pb=pb: e.tensor_copy(
                xb[:, dc * 512:(dc + 1) * 512], xf[:, pb + dc * 512: pb + dc * 512 + 512]),
                waits=([("DS", ld), ("T", tsum_prev)] if dc == 0 else []), inc="V")
            kb.emit("scalar", lambda e, dc=dc: e.square(
                x2b[:, dc * 512:(dc + 1) * 512], xb[:, dc * 512:(dc + 1) * 512]),
                waits=[("V", vv)] + ([("T", tsum_prev)] if dc == 0 else []), inc="A")
        vcast = kb.cnt["V"]; acast = kb.cnt["A"]
        for dc in range(8):
            kb.emit("tensor", lambda e, dc=dc, first=(dc == 0): e.matmul(
                psS[0:1, :], ones[:, 0:1], xb[:, dc * 512:(dc + 1) * 512],
                start=first, stop=(dc == 7)),
                waits=([("V", vcast)] if dc == 0 else []))
            kb.emit("tensor", lambda e, dc=dc, first=(dc == 0): e.matmul(
                psS2[0:1, :], ones[:, 0:1], x2b[:, dc * 512:(dc + 1) * 512],
                start=first, stop=(dc == 7)),
                waits=([("A", acast)] if dc == 0 else []), inc=("T" if dc == 7 else None))
        tsum = kb.cnt["T"]
        tsum_prev = tsum
        MU, S2N, VAR, SQ, RSTD, NMR = (st[0:1, i * 512:(i + 1) * 512] for i in range(6))
        kb.emit("vector", lambda e: e.tensor_scalar(MU, psS[0:1, :], 1.0 / D, None, op0=ALU.mult),
                waits=[("T", tsum)], chain=True)
        kb.emit("vector", lambda e: e.tensor_scalar(S2N, psS2[0:1, :], 1.0 / D, None, op0=ALU.mult),
                chain=True)
        kb.emit("vector", lambda e: e.tensor_tensor(VAR, MU, MU, op=ALU.mult), chain=True)
        kb.emit("vector", lambda e: e.tensor_tensor(VAR, S2N, VAR, op=ALU.subtract), chain=True)
        vvar = kb.cnt["V"]
        kb.emit("scalar", lambda e: e.activation(SQ, VAR, AF.Sqrt, bias=actc[0:1, 0:1], scale=1.0),
                waits=[("V", vvar)], inc="A")
        asq = kb.cnt["A"]
        kb.emit("vector", lambda e: e.reciprocal(RSTD, SQ), waits=[("A", asq)], chain=True)
        vrstd = kb.cnt["V"]
        kb.emit("vector", lambda e: e.scalar_tensor_tensor(NMR, MU, -1.0, RSTD, op0=ALU.mult, op1=ALU.mult),
                chain=True)
        vnmr = kb.cnt["V"]
        kb.emit("tensor", lambda e: e.matmul(psR[:, :], onesf[0:1, :], RSTD, start=True, stop=True),
                waits=[("V", vrstd), ("V", ev_prev)])
        kb.emit("tensor", lambda e: e.matmul(psM[:, :], onesf[0:1, :], NMR, start=True, stop=True),
                waits=[("V", vnmr)], inc="T")
        trep = kb.cnt["T"]
        for dc in range(8):
            kb.emit("vector", lambda e, dc=dc, pb=pb: e.tensor_tensor(
                xb[:, dc * 512:(dc + 1) * 512],
                xf[:, pb + dc * 512: pb + dc * 512 + 512], psR[:, :], op=ALU.mult),
                waits=[("T", trep)] if dc == 0 else [], chain=True)
            kb.emit("vector", lambda e, dc=dc, t0=t0: e.tensor_tensor(
                xn[:, dc * T + t0: dc * T + t0 + 512],
                xb[:, dc * 512:(dc + 1) * 512], psM[:, :], op=ALU.add), chain=True)
        ev_prev = kb.cnt["V"]
    kb.wait("tensor", [("DS", c_done)])
    return dict(xn=xn, ones=ones, onesf=onesf, identb=identb, maskct=maskct,
                maskc=maskc, lcum=lcum, lcum2=lcum2, i64d=i64d, nwc=nwc,
                xn_ready=kb.cnt["V"])


def phase_l1(kb, io, env, pstk):
    xn = env["xn"]
    s1c = kb.sb("l1_s1", [128, 32], F32, stack=pstk)
    s2c = kb.sb("l1_s2", [128, 32], F32, stack=pstk)
    wbb = kb.sb("l1_wbb", [128, 2 * 2 * 1024], BF16, stack=pstk)
    hsb = kb.sb("l1_h", [128, 2 * 512], BF16, stack=pstk)
    tsl = kb.sb("l1_tsil", [128, 2 * 512], BF16, stack=pstk)
    psA = [kb.ps("psl1_a%d" % i, stack=pstk) for i in range(2)]
    psB = [kb.ps("psl1_b%d" % i, stack=pstk) for i in range(2)]
    kb.emit("scalar", lambda e: e.dma_start(out=s1c[:], in_=io["s1"][:]), inc="DS")
    kb.emit("scalar", lambda e: e.dma_start(out=s2c[:], in_=io["s2"][:]), inc="DS")
    sc_ds = kb.cnt["DS"]

    xn_ready = env["xn_ready"]
    ldq = []
    mm_after_panel = {0: 0, 1: 0}
    def load_panel(oc):
        b = (oc % 2)
        sm = "DG" if b == 0 else "DH"
        kb.emit("gpsimd", lambda e, oc=oc, b=b: e.dma_start(
            out=wbb[:, b * 2048: b * 2048 + 1024], in_=io["w1t"][oc]),
            waits=[("T", mm_after_panel[b])], inc=sm)
        v = kb.emit("gpsimd", lambda e, oc=oc, b=b: e.dma_start(
            out=wbb[:, b * 2048 + 1024: b * 2048 + 2048], in_=io["w2t"][oc]), inc=sm)
        ldq.append((sm, v))

    load_panel(0)
    load_panel(1)
    ev_tail = {0: 0, 1: 0}
    evA_tail = {0: 0, 1: 0}
    st_tail = {0: 0, 1: 0}
    for oc in range(32):
        wb = oc % 2
        for tt in range(8):
            b = tt % 2
            w0 = []
            if tt == 0:
                w0 = [ldq[oc], ("V", xn_ready)]
            for dc in range(8):
                kb.emit("tensor", lambda e, dc=dc, b=b, wb=wb, t0=tt * 512, first=(dc == 0): e.matmul(
                    psA[b][:, :], wbb[:, wb * 2048 + dc * 128: wb * 2048 + dc * 128 + 128],
                    xn[:, dc * T + t0: dc * T + t0 + 512], start=first, stop=(dc == 7)),
                    waits=(w0 if dc == 0 else []) + ([("A", evA_tail[b]), ("V", ev_tail[b])] if dc == 0 else []))
            for dc in range(8):
                kb.emit("tensor", lambda e, dc=dc, b=b, wb=wb, t0=tt * 512, first=(dc == 0): e.matmul(
                    psB[b][:, :], wbb[:, wb * 2048 + 1024 + dc * 128: wb * 2048 + 1024 + dc * 128 + 128],
                    xn[:, dc * T + t0: dc * T + t0 + 512], start=first, stop=(dc == 7)),
                    inc=("T" if dc == 7 else None))
            tmm = kb.cnt["T"]
            kb.emit("scalar", lambda e, b=b, oc=oc: e.activation(
                tsl[:, b * 512:(b + 1) * 512], psA[b][:, :], AF.Silu,
                scale=s1c[:, oc:oc + 1]),
                waits=[("T", tmm), ("V", ev_tail[b])]
                + ([("DS", sc_ds)] if oc == 0 and tt == 0 else []), inc="A")
            evA_tail[b] = kb.cnt["A"]
            kb.emit("vector", lambda e, b=b, oc=oc: e.scalar_tensor_tensor(
                hsb[:, b * 512:(b + 1) * 512], psB[b][:, :], s2c[:, oc:oc + 1],
                tsl[:, b * 512:(b + 1) * 512], op0=ALU.mult, op1=ALU.mult),
                waits=[("A", evA_tail[b]), ("DS", st_tail[b])]
                + ([("DS", sc_ds)] if oc == 0 and tt == 0 else []), inc="V")
            ev_tail[b] = kb.cnt["V"]
            kb.emit("scalar", lambda e, b=b, oc=oc, t0=tt * 512: e.dma_start(
                out=io["h_dram"][oc, :, t0:t0 + 512], in_=hsb[:, b * 512:(b + 1) * 512]),
                waits=[("V", ev_tail[b])], inc="DS")
            st_tail[b] = kb.cnt["DS"]
        mm_after_panel[wb] = kb.cnt["T"]
        if oc + 2 < 32:
            load_panel(oc + 2)
    env["l1_done_ds"] = kb.cnt["DS"]
    env["l1_done_T"] = kb.cnt["T"]


def phase_p2(kb, io, env, pstk, nseg=NSEG):
    xn = env["xn"]; identb = env["identb"]
    maskct = env["maskct"]; maskc = env["maskc"]
    lcum = env["lcum"]; lcum2 = env["lcum2"]; i64d = env["i64d"]
    L = lambda n, sh, dt: kb.sb(n, sh, dt, stack=pstk)
    qkvgw = L("p2_qkvgw", [128, 8 * 3088], BF16)
    f1og1w = L("p2_f1og1", [128, 8 * 128], BF16)
    wf2w = L("p2_wf2", [64, 1024], BF16)
    wog2w = L("p2_wog2", [64, 8 * 128], BF16)
    pack = L("p2_pack", [128, 2 * 4096], BF16)
    fmp = L("p2_fmp", [128, 2 * 4096], BF16)
    vtm2 = L("p2_vtm2", [128, 2 * 2048], BF16)
    ogp = L("p2_ogp", [128, 8 * 512], BF16)
    osg = L("p2_osg", [128, 8 * 512], BF16)
    z1f = L("p2_z1", [64, 512], BF16)
    og1f = L("p2_og1", [64, 512], BF16)
    states = L("p2_S", [128, 512], BF16)
    ctpt = L("p2_ctpt", [128, 128], BF16)
    cun = L("p2_cun", [128, 64], BF16)
    powsb = L("p2_pow", [128, 2 * 128], BF16)
    xw = L("p2_xw", [128, 2 * 128], BF16)
    ctptb = L("p2_ctptb", [128, 128], BF16)
    cunb = L("p2_cunb", [128, 64], BF16)
    powsbb = L("p2_powb", [128, 2 * 128], BF16)
    xwb = L("p2_xwb", [128, 2 * 128], BF16)
    sumb = L("p2_sumb", [128, 256], BF16)
    qtmp = L("p2_qtmp", [128, 1024], BF16)
    ktmp = L("p2_ktmp", [128, 1024], BF16)
    knk = L("p2_knk", [128, 1024], F32)
    kn = L("p2_kn", [128, 64], F32)
    emF = L("p2_emF", [128, 1024], BF16)
    eagF = L("p2_eagF", [128, 1024], BF16)
    gtmp = L("p2_gt", [128, 16], F32)
    gkb = L("p2_gkb", [128, 1024], BF16)
    scr1 = L("p2_scr1", [128, 512], BF16)
    scr2 = L("p2_scr2", [128, 512], BF16)
    pg4 = kb.ps("p2_pg4", (128, 1024), stack=pstk)
    pg4b = kb.ps("p2_pg4b", (128, 1024), stack=pstk)
    px2 = kb.ps("p2_px2", (128, 1024), stack=pstk)
    ps4 = kb.ps("p2_ps4", (128, 1024), stack=pstk)
    l1_s1 = L("p2l1_s1", [128, 32], F32)
    l1_s2 = L("p2l1_s2", [128, 32], F32)
    l1_wbb = L("p2l1_wbb", [128, 2 * 1024], BF16)
    l1_hsb = L("p2l1_h", [128, 512], BF16)
    l1_tsl = L("p2l1_t", [128, 512], BF16)

    for dc in range(8):
        kb.emit("gpsimd", lambda e, dc=dc: e.dma_start(
            out=qkvgw[:, dc * 3088:(dc + 1) * 3088], in_=io["qkvg"][dc]), inc="DG")
    kb.emit("gpsimd", lambda e: e.dma_start(
        out=f1og1w[:].rearrange("p (a b) -> p a b", b=128),
        in_=io["f1og1"].rearrange("a p b -> p a b")), inc="DG")
    kb.emit("gpsimd", lambda e: e.dma_start(out=wf2w[:], in_=io["wf2"][:]), inc="DG")
    kb.emit("gpsimd", lambda e: e.dma_start(
        out=wog2w[:].rearrange("p (a b) -> p a b", b=128),
        in_=io["wog2t"].rearrange("a p b -> p a b")), inc="DG")
    wready = kb.cnt["DG"]
    kb.emit("vector", lambda e: e.memset(states[:], 0.0))
    kb.emit("vector", lambda e: e.memset(ctpt[:], 0.0))
    if P2_HEADS is not None:
        kb.emit("vector", lambda e: e.memset(osg[:], 0.0))
    kb.emit("vector", lambda e: e.memset(cun[:], 0.0), inc="V")
    kb.wait("tensor", [("DG", wready), ("V", kb.cnt["V"])])
    kb.wait("scalar", [("DG", wready)])
    kb.wait("gpsimd", [("V", kb.cnt["V"])])

    fmp4 = fmp.rearrange("p (tb ti jc r) -> p tb ti jc r", tb=2, ti=4, jc=8, r=128)

    fr = dict(psG=[], psH=[], bA=[], bB=[], bC=[], bF=[],
              qtmp=[], ktmp=[], knk=[], gtmp=[], gkb=[], emF=[], eagF=[], scr=[],
              vtm=[[], []], pack=[[], []], fmpc=[[], []], og=[], osg=[], chain=[],
              ct=[], pow=[], xwg=[], sumbg=[])
    LN = [dict(bank=pg4, xoff=0, ct=ctpt, cu=cun, pw=powsb, xw=xw),
          dict(bank=pg4b, xoff=512, ct=ctptb, cu=cunb, pw=powsbb, xw=xwb)]
    frl = [dict(bA=[], bX=[], pow=[], sumev=[], psS=[], psO=[], chain=[],
                vmask=0, vpow=0, axw=0, tg=0, tx=0, tsq=0, taj=0, tch=0, gadd=0)
           for _ in range(2)]

    def W(*gs):
        out = []
        for g in gs:
            out += g
        return out

    kb.emit("scalar", lambda e: e.dma_start(out=l1_s1[:], in_=io["s1"][:]), inc="DS")
    kb.emit("scalar", lambda e: e.dma_start(out=l1_s2[:], in_=io["s2"][:]), inc="DS")
    l1_sc_ds = kb.cnt["DS"]
    l1_ldq = []
    l1_state = {"mm_after": 0, "st": 0, "ts": 0}

    def l1_load_panel(oc):
        sm = "DG" if oc % 2 == 0 else "DH"
        kb.emit("gpsimd", lambda e, oc=oc: e.dma_start(
            out=l1_wbb[:, 0:1024], in_=io["w1t"][oc]),
            waits=[("T", l1_state["mm_after"])], inc=sm)
        v = kb.emit("gpsimd", lambda e, oc=oc: e.dma_start(
            out=l1_wbb[:, 1024:2048], in_=io["w2t"][oc]), inc=sm)
        l1_ldq.append((sm, v))

    l1_load_panel(0)
    l1_q = {"u": 0, "m": 0, "t1": 0}

    def l1_tick():
        u = l1_q["u"]
        if u >= 256:
            return
        m = l1_q["m"]
        oc, tt = u // 8, u % 8
        t0 = tt * 512
        if m < 8:
            dc = m
            w = (([l1_ldq[oc]] if tt == 0 else []) + W(fr["psG"])) if dc == 0 else []
            kb.emit("tensor", lambda e, dc=dc, t0=t0, first=(dc == 0): e.matmul(
                ps4[:, 0:512], l1_wbb[:, dc * 128: dc * 128 + 128],
                xn[:, dc * T + t0: dc * T + t0 + 512], start=first, stop=(dc == 7)),
                waits=w, inc=("T" if dc == 7 else None))
            if dc == 7:
                l1_q["t1"] = kb.cnt["T"]
        else:
            dc = m - 8
            w = W(fr["psH"]) if dc == 0 else []
            kb.emit("tensor", lambda e, dc=dc, t0=t0, first=(dc == 0): e.matmul(
                ps4[:, 512:1024], l1_wbb[:, 1024 + dc * 128: 1024 + dc * 128 + 128],
                xn[:, dc * T + t0: dc * T + t0 + 512], start=first, stop=(dc == 7)),
                waits=w, inc=("T" if dc == 7 else None))
        l1_q["m"] = m + 1
        if l1_q["m"] < 16:
            return
        l1_q["m"] = 0
        l1_q["u"] = u + 1
        t2 = kb.cnt["T"]
        t1 = l1_q["t1"]
        kb.emit("scalar", lambda e, oc=oc: e.activation(
            l1_tsl[:], ps4[:, 0:512], AF.Silu, scale=l1_s1[:, oc:oc + 1]),
            waits=[("T", t1), ("V", l1_state["ts"])]
            + ([("DS", l1_sc_ds)] if u == 0 else []), inc="A")
        a1 = kb.cnt["A"]
        fr["psG"] = [("A", a1)]
        kb.emit("vector", lambda e, oc=oc: e.scalar_tensor_tensor(
            l1_hsb[:], ps4[:, 512:1024], l1_s2[:, oc:oc + 1],
            l1_tsl[:], op0=ALU.mult, op1=ALU.mult),
            waits=[("T", t2), ("A", a1), ("DS", l1_state["st"])]
            + ([("DS", l1_sc_ds)] if u == 0 else []), inc="V")
        v1 = kb.cnt["V"]
        fr["psH"] = [("V", v1)]
        l1_state["ts"] = v1
        kb.emit("scalar", lambda e, oc=oc, t0=t0: e.dma_start(
            out=io["h_dram"][oc, :, t0:t0 + 512], in_=l1_hsb[:]),
            waits=[("V", v1)], inc="DS")
        l1_state["st"] = kb.cnt["DS"]
        if tt == 7:
            l1_state["mm_after"] = kb.cnt["T"]
            if oc + 1 < 32:
                l1_load_panel(oc + 1)

    def projgrp(tk, lo, hi, bank, bguard, waits0=()):
        t0 = tk * 128
        for dc in range(8):
            w = list(waits0) + W(bguard) if dc == 0 else []
            kb.emit("tensor", lambda e, dc=dc, t0=t0, lo=lo, hi=hi, bank=bank,
                    first=(dc == 0): e.matmul(
                bank[:, 0:hi - lo], xn[:, dc * T + t0:dc * T + t0 + 128],
                qkvgw[:, dc * 3088 + lo:dc * 3088 + hi],
                start=first, stop=(dc == 7)),
                waits=w, inc=("T" if dc == 7 else None))
        return kb.cnt["T"]

    bG = lambda: ps4[:, 0:512]
    bH = lambda: ps4[:, 512:1024]
    bA = lambda: pg4[:, 0:512]
    bB = lambda: pg4[:, 512:1024]
    bC = lambda: px2[:, 0:512]
    bF = lambda: px2[:, 512:1024]

    for s in range(nseg):
        s0 = s * 512
        for dc in range(8):
            kb.emit("tensor", lambda e, dc=dc, s0=s0, first=(dc == 0): e.matmul(
                bG()[:, 0:512], f1og1w[:, dc * 128:(dc + 1) * 128],
                xn[:, dc * T + s0:dc * T + s0 + 512], start=first, stop=(dc == 7)),
                waits=W(fr["psG"]) if dc == 0 else [], inc=("T" if dc == 7 else None))
        tz = kb.cnt["T"]
        kb.emit("scalar", lambda e: e.copy(z1f[:], bG()[0:64, 0:512]), waits=[("T", tz)])
        kb.emit("scalar", lambda e: e.copy(og1f[:], bG()[64:128, 0:512]), inc="A")
        fr["psG"] = [("A", kb.cnt["A"])]
        az = kb.cnt["A"]
        for oc in range(8):
            w0 = ([("A", az)] + W(fr["psH"])) if oc == 0 else [("A", kb.cnt["A"])]
            kb.emit("tensor", lambda e, oc=oc: e.matmul(
                bH()[:, 0:512], wog2w[:, oc * 128:(oc + 1) * 128], og1f[:],
                start=True, stop=True), waits=w0, inc="T")
            tog = kb.cnt["T"]
            kb.emit("scalar", lambda e, oc=oc, tog=tog: e.activation(
                ogp[:, oc * 512:(oc + 1) * 512], bH()[:, 0:512], AF.Sigmoid),
                waits=[("T", tog)] + W(fr["og"]), inc="A")
        fr["psH"] = [("A", kb.cnt["A"])]

        def prologue(tk4):
            tk = s * TPS + tk4
            tb = tk % 2
            t0 = tk * 128
            if P2_TSEC < 1:
                return None
            tq = projgrp(tk, 0, 512, bG(), fr["psG"])
            kb.emit("scalar", lambda e, tq=tq: e.activation(
                qtmp[:, 0:512], bG()[:, 0:512], AF.Silu), waits=[("T", tq)] + W(fr["qtmp"]), inc="A")
            fr["psG"] = [("A", kb.cnt["A"])]
            tq = projgrp(tk, 512, 1024, bH(), fr["psH"])
            kb.emit("scalar", lambda e, tq=tq: e.activation(
                qtmp[:, 512:1024], bH()[:, 0:512], AF.Silu), waits=[("T", tq)], inc="A")
            fr["psH"] = [("A", kb.cnt["A"])]
            tq = projgrp(tk, 1024, 1536, bG(), fr["psG"])
            kb.emit("scalar", lambda e, tq=tq: e.activation(
                ktmp[:, 0:512], bG()[:, 0:512], AF.Silu), waits=[("T", tq)] + W(fr["ktmp"]), inc="A")
            fr["psG"] = [("A", kb.cnt["A"])]
            tq = projgrp(tk, 1536, 2048, bH(), fr["psH"])
            kb.emit("scalar", lambda e, tq=tq: e.activation(
                ktmp[:, 512:1024], bH()[:, 0:512], AF.Silu), waits=[("T", tq)], inc="A")
            fr["psH"] = [("A", kb.cnt["A"])]
            kready = kb.cnt["A"]
            for half in range(2):
                tq = projgrp(tk, 2048 + half * 512, 2560 + half * 512,
                             bG() if half == 0 else bH(),
                             fr["psG"] if half == 0 else fr["psH"])
                base = half * 1024
                kb.emit("vector", lambda e, tq=tq, base=base, half=half, tb=tb: e.tensor_copy(
                    vtm2[:, tb * 2048:(tb + 1) * 2048].rearrange(
                        "p (q c) -> p q c", c=64)[:, (base // 64 + 1)::4, :][:, 0:4, :],
                    (bG() if half == 0 else bH())[:, 0:512].rearrange(
                        "p (a b) -> p a b", b=64)[:, 0::2, :]),
                    waits=[("T", tq)] + (W(fr["vtm"][tb]) if half == 0 else []), inc="V")
                kb.emit("vector", lambda e, base=base, half=half, tb=tb: e.tensor_copy(
                    vtm2[:, tb * 2048:(tb + 1) * 2048].rearrange(
                        "p (q c) -> p q c", c=64)[:, (base // 64 + 2)::4, :][:, 0:4, :],
                    (bG() if half == 0 else bH())[:, 0:512].rearrange(
                        "p (a b) -> p a b", b=64)[:, 1::2, :]), inc="V")
                if half == 0:
                    fr["psG"] = [("V", kb.cnt["V"])]
                else:
                    fr["psH"] = [("V", kb.cnt["V"])]
            tq = projgrp(tk, 3072, 3088, bG(), fr["psG"])
            kb.emit("scalar", lambda e, tq=tq: e.activation(
                gtmp[:, 0:16], bG()[:, 0:16], AF.Sigmoid),
                waits=[("T", tq)] + W(fr["gtmp"]), inc="A")
            fr["psG"] = [("A", kb.cnt["A"])]
            if P2_TSEC < 2:
                return None
            for hf in range(2):
                bank, bn = (bH(), "psH") if hf == 0 else (bG(), "psG")
                kb.emit("tensor", lambda e, hf=hf, tk4=tk4, bank=bank: e.matmul(
                    bank[:, 0:512], z1f[:, tk4 * 128:(tk4 + 1) * 128],
                    wf2w[:, hf * 512:(hf + 1) * 512], start=True, stop=True),
                    waits=W(fr[bn], fr["scr"], fr["gkb"]), inc="T")
                tzz = kb.cnt["T"]
                kb.emit("scalar", lambda e, bank=bank: e.activation(scr1[:], bank[:, 0:512], AF.Abs),
                        waits=[("T", tzz)])
                aabs = kb.cnt["A"]
                kb.emit("scalar", lambda e: e.activation(scr1[:], scr1[:], AF.Exp, scale=-1.0))
                kb.emit("scalar", lambda e: e.activation(
                    scr1[:], scr1[:], AF.Ln, bias=kb.tens["c_actc"][:, 1:2]), inc="A")
                aln = kb.cnt["A"]
                kb.emit("vector", lambda e, bank=bank: e.tensor_scalar(
                    scr2[:], bank[:, 0:512], 0.0, None, op0=ALU.min),
                    waits=[("T", tzz), ("A", aabs)])
                kb.emit("vector", lambda e, hf=hf: e.tensor_tensor(
                    gkb[:, hf * 512:(hf + 1) * 512], scr2[:], scr1[:], op=ALU.subtract),
                    waits=[("A", aln)], inc="V")
                fr[bn] = [("V", kb.cnt["V"])]
                fr["scr"] = [("V", kb.cnt["V"])]
            vgk = kb.cnt["V"]
            if P2_TSEC < 3:
                return None
            for hf in range(2):
                for ch in (0, 64):
                    kb.emit("tensor", lambda e, ch=ch, hf=hf: e.matmul(
                        bG()[ch:ch + 64, 0:512], lcum[ch:ch + 64, :],
                        gkb[ch:ch + 64, hf * 512:(hf + 1) * 512],
                        start=True, stop=True, tile_position=(ch, ch), skip_group_check=True),
                        waits=([("V", vgk)] + W(fr["psG"], fr["emF"], fr["pack"][tb])
                               if ch == 0 else []))
                    kb.emit("tensor", lambda e, ch=ch, hf=hf: e.matmul(
                        bH()[ch:ch + 64, 0:512], lcum2[ch:ch + 64, :],
                        gkb[ch:ch + 64, hf * 512:(hf + 1) * 512],
                        start=True, stop=True, tile_position=(ch, ch), skip_group_check=True),
                        waits=(W(fr["psH"], fr["eagF"]) if ch == 0 else []),
                        inc=("T" if ch == 64 else None))
                tcs = kb.cnt["T"]
                kb.emit("scalar", lambda e, hf=hf, tb=tb, tcs=tcs: e.activation(
                    pack[:, tb * 4096 + 3 * 1024 + hf * 512: tb * 4096 + 3 * 1024 + hf * 512 + 512],
                    bG()[:, 0:512], AF.Exp), waits=[("T", tcs)], inc="A")
                kb.emit("scalar", lambda e, hf=hf: e.activation(
                    emF[:, hf * 512:(hf + 1) * 512], bG()[:, 0:512], AF.Exp, scale=-1.0), inc="A")
                fr["psG"] = [("A", kb.cnt["A"])]
                fr["emF"] = [("A", kb.cnt["A"])]
                kb.emit("scalar", lambda e, hf=hf: e.activation(
                    eagF[:, hf * 512:(hf + 1) * 512], bH()[:, 0:512], AF.Exp), inc="A")
                fr["psH"] = [("A", kb.cnt["A"])]
                fr["eagF"] = [("A", kb.cnt["A"])]
            aexp = kb.cnt["A"]
            if P2_TSEC < 4:
                return None
            kb.emit("scalar", lambda e: e.square(knk[:], ktmp[:]),
                    waits=[("A", kready)] + W(fr["knk"]), inc="A")
            fr["ktmp"] = [("A", kb.cnt["A"])]
            asq = kb.cnt["A"]
            kb.emit("vector", lambda e: e.reduce_sum(
                kn[:, 0:16], knk.rearrange("p (h j) -> p h j", h=16)[:, :, :], axis=AX.X),
                waits=[("A", asq)], inc="V")
            vkn = kb.cnt["V"]
            kb.emit("scalar", lambda e: e.activation(kn[:, 16:32], kn[:, 0:16], AF.Sqrt),
                    waits=[("V", vkn)], inc="A")
            akn = kb.cnt["A"]
            kb.emit("vector", lambda e: e.tensor_scalar(
                kn[:, 32:48], kn[:, 16:32], 1e-12, None, op0=ALU.max), waits=[("A", akn)])
            kb.emit("vector", lambda e: e.reciprocal(kn[:, 48:64], kn[:, 32:48]))
            kb.emit("vector", lambda e: e.tensor_tensor(
                knk.rearrange("p (h j) -> p h j", h=16),
                ktmp.rearrange("p (h j) -> p h j", h=16),
                kn[:, 48:64].unsqueeze(-1).broadcast_to((128, 16, 64)), op=ALU.mult), inc="V")
            fr["ktmp"] = fr["ktmp"] + [("V", kb.cnt["V"])]
            kb.emit("vector", lambda e, tb=tb: e.tensor_tensor(
                pack[:, tb * 4096:tb * 4096 + 1024], knk[:], emF[:], op=ALU.mult),
                waits=[("A", aexp)], inc="V")
            fr["emF"] = [("V", kb.cnt["V"])]
            kb.emit("vector", lambda e: e.scalar_tensor_tensor(
                eagF.rearrange("p (h j) -> p h j", h=16),
                eagF.rearrange("p (h j) -> p h j", h=16), -1.0,
                gtmp[:, 0:16].unsqueeze(-1).broadcast_to((128, 16, 64)),
                op0=ALU.mult, op1=ALU.mult))
            fr["gtmp"] = [("V", kb.cnt["V"])]
            kb.emit("vector", lambda e, tb=tb: e.tensor_tensor(
                pack[:, tb * 4096 + 1024:tb * 4096 + 2048], knk[:], eagF[:], op=ALU.mult), inc="V")
            fr["eagF"] = [("V", kb.cnt["V"])]
            fr["knk"] = [("V", kb.cnt["V"])]
            kb.emit("vector", lambda e, tb=tb: e.tensor_copy(
                vtm2[:, tb * 2048:(tb + 1) * 2048].rearrange("p (q c) -> p q c", c=64)[:, 0::4, :],
                pack[:, tb * 4096 + 1024:tb * 4096 + 2048].rearrange(
                    "p (a b) -> p a b", b=64)[:, 0::2, :]), inc="V")
            kb.emit("vector", lambda e, tb=tb: e.tensor_copy(
                vtm2[:, tb * 2048:(tb + 1) * 2048].rearrange("p (q c) -> p q c", c=64)[:, 3::4, :],
                pack[:, tb * 4096 + 1024:tb * 4096 + 2048].rearrange(
                    "p (a b) -> p a b", b=64)[:, 1::2, :]), inc="V")
            kb.emit("vector", lambda e, tb=tb: e.tensor_tensor(
                pack[:, tb * 4096 + 2 * 1024:tb * 4096 + 3 * 1024], qtmp[:],
                pack[:, tb * 4096 + 3 * 1024:tb * 4096 + 4 * 1024], op=ALU.mult), inc="V")
            fr["qtmp"] = [("V", kb.cnt["V"])]
            vpack = kb.cnt["V"]
            apack = kb.cnt["A"]
            if P2_TSEC < 5:
                return None
            kb.emit("sync", lambda e, tb=tb: e.dma_start_transpose(
                fmp[:, tb * 4096:(tb + 1) * 4096].rearrange("p (m r) -> p m r", r=128),
                pack[:, tb * 4096:(tb + 1) * 4096]),
                waits=[("V", vpack), ("A", apack)] + W(fr["fmpc"][tb]), inc="DT")
            return kb.cnt["DT"]

        def waves(tk4, dtf, snap):
            if dtf is None:
                return
            tk = s * TPS + tk4
            tb = tk % 2
            kb.wait("tensor", [("DT", dtf)])
            hl = list(range(16)) if P2_HEADS is None else list(P2_HEADS)
            pairs = [tuple(hl[i:i + 2]) for i in range(0, len(hl), 2)]
            for l in (0, 1):
                frl[l]["psS"] = frl[l]["psS"] + snap[0]
                frl[l]["psO"] = frl[l]["psO"] + snap[1]
            for pi, pair in enumerate(pairs):
                lanes = [(h % 2, h) for h in pair]
                if pi >= 1:
                    l1_tick()
                if pi >= 4:
                    l1_tick()
                    l1_tick()
                for l, h in lanes:
                    fl = frl[l]; B = LN[l]["bank"]
                    jr = (h % 2) * 64; jc = h // 2
                    kt0 = tb * 4096 + jc * 128
                    at0 = tb * 4096 + 1024 + jc * 128
                    for cpar in range(2):
                        r0 = cpar * 64
                        kb.emit("tensor", lambda e, jr=jr, jc=jc, kt0=kt0, r0=r0, tb=tb, B=B: e.matmul(
                            B[r0:r0 + 64, 0:128],
                            fmp[jr:jr + 64, kt0 + r0:kt0 + r0 + 64],
                            fmp4[jr:jr + 64, tb, 1:3, jc, r0:r0 + 64],
                            start=True, stop=True, tile_position=(jr, r0), skip_group_check=True),
                            waits=(W(fl["bA"], fl["psS"], fl["psO"]) if cpar == 0 else []))
                        kb.emit("tensor", lambda e, jr=jr, kt0=kt0, at0=at0, r0=r0, B=B: e.matmul(
                            B[r0:r0 + 64, 512:576],
                            fmp[jr:jr + 64, at0 + r0:at0 + r0 + 64],
                            fmp[jr:jr + 64, kt0 + r0:kt0 + r0 + 64],
                            start=True, stop=True, tile_position=(jr, r0), skip_group_check=True),
                            inc=("T" if cpar == 1 else None))
                    fl["tg"] = kb.cnt["T"]
                if pi >= 1:
                    l1_tick()
                if pi >= 4:
                    l1_tick()
                    l1_tick()
                for l, h in lanes:
                    fl = frl[l]; B = LN[l]["bank"]
                    kb.emit("vector", lambda e, B=B, ct=LN[l]["ct"]: e.tensor_tensor(
                        ct[:], B[:, 0:128], maskct[:], op=ALU.mult),
                        waits=[("T", fl["tg"])], inc="V")
                    kb.emit("vector", lambda e, B=B, cu=LN[l]["cu"]: e.tensor_tensor(
                        cu[:], B[:, 512:576], maskc[:], op=ALU.mult), inc="V")
                    fl["vmask"] = kb.cnt["V"]
                    fl["pow"] = [("V", fl["vmask"])]
                for l, h in lanes:
                    fl = frl[l]; xo = LN[l]["xoff"]
                    for cpar in range(2):
                        r0 = cpar * 64
                        kb.emit("tensor", lambda e, h=h, r0=r0, xo=xo, tb=tb: e.matmul(
                            px2[r0:r0 + 64, xo:xo + 128], identb[r0:r0 + 64, r0:r0 + 64],
                            vtm2[r0:r0 + 64, tb * 2048 + h * 128:tb * 2048 + h * 128 + 128],
                            start=True, stop=True, tile_position=(r0, r0), skip_group_check=True),
                            waits=(W(fl["bX"]) if cpar == 0 else []))
                        kb.emit("tensor", lambda e, h=h, r0=r0, xo=xo, tb=tb, ct=LN[l]["ct"]: e.matmul(
                            px2[r0:r0 + 64, xo:xo + 128], ct[r0:r0 + 64, 0:64],
                            vtm2[r0:r0 + 64, tb * 2048 + h * 128:tb * 2048 + h * 128 + 128],
                            start=False, stop=True, tile_position=(r0, r0), skip_group_check=True),
                            waits=([("V", fl["vmask"])] if cpar == 0 else []),
                            inc=("T" if cpar == 1 else None))
                    fl["tx"] = kb.cnt["T"]
                for l, h in lanes:
                    fl = frl[l]; xo = LN[l]["xoff"]
                    kb.emit("scalar", lambda e, xo=xo, xwl=LN[l]["xw"]: e.copy(
                        xwl[:, 128:256], px2[:, xo:xo + 128]),
                        waits=[("T", fl["tx"])], inc="A")
                    fl["axw"] = kb.cnt["A"]
                for j in range(1, 6):
                    if pi >= 1:
                        l1_tick()
                    for l, h in lanes:
                        fl = frl[l]; B = LN[l]["bank"]; pw = LN[l]["pw"]
                        for cpar in range(2):
                            r0 = cpar * 64
                            if j == 1:
                                ctp = LN[l]["ct"][r0:r0 + 64, 0:64]
                                cpp = LN[l]["cu"][r0:r0 + 64, 0:64]
                            else:
                                pbq = ((j - 1) % 2) * 128
                                ctp = pw[r0:r0 + 64, pbq:pbq + 64]
                                cpp = pw[r0:r0 + 64, pbq + 64:pbq + 128]
                            kb.emit("tensor", lambda e, ctp=ctp, cpp=cpp, r0=r0, B=B: e.matmul(
                                B[r0:r0 + 64, 0:64], cpp, ctp,
                                start=True, stop=True, tile_position=(r0, r0), skip_group_check=True),
                                waits=(W(fl["pow"]) if cpar == 0 else []))
                            kb.emit("tensor", lambda e, ctp=ctp, cpp=cpp, r0=r0, B=B: e.matmul(
                                B[r0:r0 + 64, 512:576], ctp, cpp,
                                start=True, stop=True, tile_position=(r0, r0), skip_group_check=True),
                                inc=("T" if cpar == 1 else None))
                        fl["tsq"] = kb.cnt["T"]
                    for l, h in lanes:
                        fl = frl[l]; B4 = LN[l]["bank"]; pw = LN[l]["pw"]
                        pb2 = (j % 2) * 128
                        kb.emit("vector", lambda e, pb2=pb2, B4=B4, pw=pw, tsq=fl["tsq"]: e.tensor_copy(
                            pw[:, pb2:pb2 + 128].rearrange("p (a b) -> p a b", b=64),
                            B4.rearrange("p (a b) -> p a b", b=64)[:, 0::8, :]),
                            waits=[("T", fl["tsq"])], inc="V")
                        fl["vpow"] = kb.cnt["V"]
                        fl["pow"] = [("V", fl["vpow"])]
                    if pi >= 1:
                        l1_tick()
                    for l, h in lanes:
                        fl = frl[l]; pw = LN[l]["pw"]; xwl = LN[l]["xw"]; xo = LN[l]["xoff"]
                        pb2 = (j % 2) * 128
                        for cpar in range(2):
                            r0 = cpar * 64
                            lh = pw[r0:r0 + 64, pb2:pb2 + 64]
                            rh = xwl[r0:r0 + 64, (j % 2) * 128:(j % 2) * 128 + 128]
                            kb.emit("tensor", lambda e, lh=lh, rh=rh, r0=r0, xo=xo: e.matmul(
                                px2[r0:r0 + 64, xo:xo + 128], lh, rh,
                                start=False, stop=True, tile_position=(r0, r0), skip_group_check=True),
                                waits=([("V", fl["vpow"]), ("A", fl["axw"])] if cpar == 0 else []),
                                inc=("T" if cpar == 1 else None))
                        fl["taj"] = kb.cnt["T"]
                    for l, h in lanes:
                        fl = frl[l]; xwl = LN[l]["xw"]; xo = LN[l]["xoff"]
                        kb.emit("scalar", lambda e, j=j, xo=xo, xwl=xwl, t=fl["taj"]: e.copy(
                            xwl[:, ((j + 1) % 2) * 128:((j + 1) % 2) * 128 + 128],
                            px2[:, xo:xo + 128]),
                            waits=[("T", fl["taj"])], inc="A")
                        fl["axw"] = kb.cnt["A"]
                for cpar in range(2):
                    r0 = cpar * 64
                    if pi >= 1:
                        l1_tick()
                    for l, h in lanes:
                        fl = frl[l]; B = LN[l]["bank"]; xwl = LN[l]["xw"]
                        jr = (h % 2) * 64; jc = h // 2
                        jo = jr ^ 64
                        wa_off = 0 if h % 2 == 0 else 64
                        wv_off = 64 if h % 2 == 0 else 0
                        wv = xwl[r0:r0 + 64, wv_off:wv_off + 64]
                        wa = xwl[r0:r0 + 64, wa_off:wa_off + 64]
                        ktm = pack[r0:r0 + 64, tb * 4096 + h * 64:tb * 4096 + h * 64 + 64]
                        ptc = LN[l]["ct"][r0:r0 + 64, 64:128]
                        w0 = (([("V", fl["vpow"]), ("A", fl["axw"])] if cpar == 0
                               else W(fl["psS"], fl["psO"])) + W(fl["sumev"]))
                        kb.emit("tensor", lambda e, r0=r0, jr=jr, wv=wv, ptc=ptc, B=B: e.matmul(
                            B[jr:jr + 64, 0:64], wv, ptc,
                            start=True, stop=True, tile_position=(r0, jr), skip_group_check=True),
                            waits=w0)
                        kb.emit("tensor", lambda e, r0=r0, jr=jr, wa=wa, ptc=ptc, B=B: e.matmul(
                            B[jr:jr + 64, 512:576], wa, ptc,
                            start=True, stop=True, tile_position=(r0, jr), skip_group_check=True))
                        kb.emit("tensor", lambda e, r0=r0, jo=jo, ktm=ktm, wv=wv, B=B: e.matmul(
                            B[jo:jo + 64, 0:64], ktm, wv,
                            start=True, stop=True, tile_position=(r0, jo), skip_group_check=True))
                        kb.emit("tensor", lambda e, r0=r0, jo=jo, ktm=ktm, wa=wa, B=B: e.matmul(
                            B[jo:jo + 64, 512:576], wa, ktm,
                            start=True, stop=True, tile_position=(r0, jo), skip_group_check=True),
                            inc="T")
                        ts1 = kb.cnt["T"]
                        kb.emit("scalar", lambda e, jr=jr, ts1=ts1, B=B: e.copy(
                            sumb[jr:jr + 64, 0:64], B[jr:jr + 64, 0:64]),
                            waits=[("T", ts1)])
                        kb.emit("scalar", lambda e, jr=jr, B=B: e.copy(
                            sumb[jr:jr + 64, 64:128], B[jr:jr + 64, 512:576]))
                        kb.emit("scalar", lambda e, jr=jr, jo=jo, B=B: e.copy(
                            sumb[jr:jr + 64, 128:192], B[jo:jo + 64, 0:64]))
                        kb.emit("scalar", lambda e, jr=jr, jo=jo, B=B: e.copy(
                            sumb[jr:jr + 64, 192:256], B[jo:jo + 64, 512:576]), inc="A")
                        as2 = kb.cnt["A"]
                        fl["sumev"] = [("A", as2)]
                        fl["bA"] = [("A", as2)]
                    if pi >= 1:
                        l1_tick()
                    for l, h in lanes:
                        fl = frl[l]; B = LN[l]["bank"]
                        jr = (h % 2) * 64; jc = h // 2
                        qbT = fmp[jr:jr + 64, tb * 4096 + 2 * 1024 + jc * 128 + r0:
                                  tb * 4096 + 2 * 1024 + jc * 128 + r0 + 64]
                        kb.emit("tensor", lambda e, jr=jr, jc=jc, B=B: e.matmul(
                            B[jr:jr + 64, 0:64], identb[jr:jr + 64, jr:jr + 64],
                            states[jr:jr + 64, jc * 64:jc * 64 + 64],
                            start=True, stop=False, tile_position=(jr, jr), skip_group_check=True),
                            waits=W(fl["sumev"], fl["psS"], fl["chain"]))
                        kb.emit("tensor", lambda e, jr=jr, jc=jc, B=B: e.matmul(
                            B[jr:jr + 64, 0:64], sumb[jr:jr + 64, 192:256],
                            states[jr:jr + 64, jc * 64:jc * 64 + 64],
                            start=False, stop=False, tile_position=(jr, jr), skip_group_check=True))
                        kb.emit("tensor", lambda e, jr=jr, B=B: e.matmul(
                            B[jr:jr + 64, 0:64], identb[jr:jr + 64, jr:jr + 64],
                            sumb[jr:jr + 64, 128:192],
                            start=False, stop=True, tile_position=(jr, jr), skip_group_check=True))
                        kb.emit("tensor", lambda e, jr=jr, jc=jc, B=B: e.matmul(
                            B[jr:jr + 64, 512:576], states[jr:jr + 64, jc * 64:jc * 64 + 64],
                            sumb[jr:jr + 64, 64:128],
                            start=True, stop=False, tile_position=(jr, jr), skip_group_check=True),
                            waits=W(fl["psO"]))
                        kb.emit("tensor", lambda e, jr=jr, jc=jc, qbT=qbT, B=B: e.matmul(
                            B[jr:jr + 64, 512:576], states[jr:jr + 64, jc * 64:jc * 64 + 64],
                            qbT,
                            start=False, stop=False, tile_position=(jr, jr), skip_group_check=True))
                        kb.emit("tensor", lambda e, jr=jr, B=B: e.matmul(
                            B[jr:jr + 64, 512:576], identb[jr:jr + 64, jr:jr + 64],
                            sumb[jr:jr + 64, 0:64],
                            start=False, stop=True, tile_position=(jr, jr), skip_group_check=True),
                            inc="T")
                        fl["tch"] = kb.cnt["T"]
                    for l, h in lanes:
                        fl = frl[l]; B = LN[l]["bank"]
                        jr = (h % 2) * 64; jc = h // 2
                        exoff = tb * 4096 + 3 * 1024 + jc * 128 + r0 + 63
                        kb.emit("vector", lambda e, jr=jr, jc=jc, exoff=exoff, B=B: e.tensor_tensor(
                            states[jr:jr + 64, jc * 64:jc * 64 + 64], B[jr:jr + 64, 0:64],
                            fmp[jr:jr + 64, exoff:exoff + 1].broadcast_to((64, 64)),
                            op=ALU.mult), waits=[("T", fl["tch"])], inc="V")
                        fl["psS"] = [("V", kb.cnt["V"])]
                        ocol = jc * 512 + tk4 * 128 + r0
                        kb.emit("vector", lambda e, jr=jr, ocol=ocol, B=B: e.tensor_tensor(
                            osg[jr:jr + 64, ocol:ocol + 64], B[jr:jr + 64, 512:576],
                            ogp[jr:jr + 64, ocol:ocol + 64],
                            op=ALU.mult), waits=W(fr["osg"]), inc="V")
                        fl["psO"] = [("V", kb.cnt["V"])]
                        fl["chain"] = [("V", kb.cnt["V"])]
                for l, h in lanes:
                    frl[l]["bX"] = [("A", frl[l]["axw"])]
            fr["og"] = [("V", kb.cnt["V"])]
            fr["psG"] = [("V", kb.cnt["V"])]
            fr["psH"] = [("V", kb.cnt["V"])]
            fr["vtm"][tb] = [("T", kb.cnt["T"])]
            fr["pack"][tb] = [("T", kb.cnt["T"]), ("DT", kb.cnt["DT"])]
            fr["fmpc"][tb] = [("T", kb.cnt["T"]), ("V", kb.cnt["V"]), ("G", kb.cnt["G"])]
            fr["gkb"] = [("T", kb.cnt["T"])]

        dtn = {0: prologue(0)}
        snap = (list(fr["psG"]), list(fr["psH"]))
        for tk4 in range(TPS):
            if tk4 + 1 < TPS:
                dtn[tk4 + 1] = prologue(tk4 + 1)
            waves(tk4, dtn[tk4], snap)
            snap = (list(fr["psG"]), list(fr["psH"]))
        vlast = kb.cnt["V"]
        for jc in range(8):
            kb.emit("scalar", lambda e, jc=jc, s0=s0: e.dma_start(
                out=io["o_dram"][jc, :, s0:s0 + 512], in_=osg[:, jc * 512:(jc + 1) * 512]),
                waits=([("V", vlast)] if jc == 0 else []), inc="DS")
        fr["osg"] = [("DS", kb.cnt["DS"])]
    while l1_q["u"] < 256:
        l1_tick()
    env["p2_done"] = True


def phase_hidden(kb, io, in_name, out_name, wt_name, sc_name, pstk):
    tag = wt_name
    L = lambda n, sh, dt: kb.sb(n + tag, sh, dt, stack=pstk)
    hblk = L("lh_hblk", [128, 32 * 2048], BF16)
    wbb = L("lh_wbb", [128, 2 * 4096], BF16)
    gout = L("lh_gout", [128, 2 * 512], BF16)
    scc = L("lh_sc", [128, 32], F32)
    psb = [kb.ps("lh_ps%d%s" % (i, tag), stack=pstk) for i in range(2)]
    kb.emit("scalar", lambda e: e.dma_start(out=scc[:], in_=io[sc_name][:]), inc="DS")
    scc_ds = kb.cnt["DS"]

    steps = [(TT, oc) for TT in range(2) for oc in range(32)]
    if LH_LIM is not None:
        steps = steps[:LH_LIM]
    ldq = {}
    st_tail = {0: 0, 1: 0}
    evA_tail = {0: 0, 1: 0}
    mm_tail = {0: 0, 1: 0}
    hld = {}

    def load_for(i):
        if i >= len(steps) or i in ldq:
            return
        TT, oc = steps[i]
        b = i % 2
        sm = "DG" if b == 0 else "DH"
        v = kb.emit("gpsimd", lambda e, oc=oc, b=b: e.dma_start(
            out=wbb[:, b * 4096:(b + 1) * 4096], in_=io[wt_name][oc]),
            waits=[("T", mm_tail[b])], inc=sm)
        ldq[i] = (sm, v)

    load_for(0)
    for i, (TT, oc) in enumerate(steps):
        b = i % 2
        if oc == 0:
            tguard = kb.cnt["T"]
            for dc in range(32):
                kb.emit("scalar", lambda e, dc=dc, TT=TT: e.dma_start(
                    out=hblk[:, dc * 2048:(dc + 1) * 2048],
                    in_=io[in_name][dc, :, TT * 2048:TT * 2048 + 2048]),
                    waits=([("T", tguard)] if dc == 0 else []), inc="DS")
            hld[TT] = kb.cnt["DS"]
        load_for(i + 1)
        for ts in range(4):
            pb = ts % 2
            w0 = ([ldq[i], ("DS", hld[TT])] if ts == 0 else []) + [("A", evA_tail[pb])]
            for dc in range(32):
                kb.emit("tensor", lambda e, dc=dc, b=b, pb=pb, ts=ts, TT=TT, first=(dc == 0): e.matmul(
                    psb[pb][:, :], wbb[:, b * 4096 + dc * 128:b * 4096 + dc * 128 + 128],
                    hblk[:, dc * 2048 + ts * 512:dc * 2048 + ts * 512 + 512],
                    start=first, stop=(dc == 31)),
                    waits=w0 if dc == 0 else [], inc=("T" if dc == 31 else None))
            tmm = kb.cnt["T"]
            kb.emit("scalar", lambda e, pb=pb, oc=oc, tmm=tmm: e.activation(
                gout[:, pb * 512:(pb + 1) * 512], psb[pb][:, :], AF.Gelu,
                scale=scc[:, oc:oc + 1]),
                waits=[("T", tmm), ("DS", st_tail[pb])]
                + ([("DS", scc_ds)] if i == 0 and ts == 0 else []), inc="A")
            evA_tail[pb] = kb.cnt["A"]
            kb.emit("scalar", lambda e, pb=pb, oc=oc, t0=TT * 2048 + ts * 512: e.dma_start(
                out=io[out_name][oc, :, t0:t0 + 512], in_=gout[:, pb * 512:(pb + 1) * 512]),
                waits=[("A", evA_tail[pb])], inc="DS")
            st_tail[pb] = kb.cnt["DS"]
        mm_tail[b] = kb.cnt["T"]


def phase_final(kb, io, pstk):
    L = lambda n, sh, dt: kb.sb(n, sh, dt, stack=pstk)
    w3b = L("f_w3b", [128, 8 * 4096], BF16)
    wob = L("f_wob", [128, 8 * 1024], BF16)
    g2b = L("f_g2b", [128, 2 * 32 * 512], BF16)
    olod = L("f_olod", [128, 8 * 512], BF16)
    olnb = L("f_oln", [128, 8 * 512], BF16)
    osq = L("f_osq", [128, 512], BF16)
    ysb = L("f_ysb", [128, 2 * 512], F32)
    xfb = L("f_xfb", [128, 2 * 512], F32)
    stt = L("f_stats", [2, 6 * 512], F32)
    s3c = L("f_s3", [128, 8], F32)
    ones = kb.tens["c_ones"]
    onesf = kb.tens["c_onesf"]
    nwc = kb.tens["c_nw"]
    pFs = [kb.ps("f_pF%d" % i, stack=pstk) for i in range(2)]
    pWs = [kb.ps("f_pW%d" % i, stack=pstk) for i in range(2)]
    pS = kb.ps("f_pS", (2, 512), stack=pstk)
    pS2 = kb.ps("f_pS2", (2, 512), stack=pstk)
    pR = kb.ps("f_pR", stack=pstk)
    pM = kb.ps("f_pM", stack=pstk)

    kb.emit("scalar", lambda e: e.dma_start(out=s3c[:], in_=io["s3"][:]), inc="DS")
    for oc in range(8):
        kb.emit("gpsimd", lambda e, oc=oc: e.dma_start(
            out=w3b[:, oc * 4096:(oc + 1) * 4096], in_=io["w3t"][oc]), inc="DG")
        kb.emit("gpsimd", lambda e, oc=oc: e.dma_start(
            out=wob[:, oc * 1024:(oc + 1) * 1024], in_=io["wot"][oc]), inc="DG")
    cast_done = kb.cnt["DG"]
    kb.wait("tensor", [("DG", cast_done)])

    MU, S2N, VAR, SQ, RSTD, NMU = (stt[0:1, i * 512:(i + 1) * 512] for i in range(6))
    ev_prev = 0
    yst_tail = {0: 0, 1: 0}
    aw_tail = {0: 0, 1: 0}
    g2_rd = {0: 0, 1: 0}
    for tt in range(8):
        t0 = tt * 512
        gb = (tt % 2)
        for jc in range(8):
            kb.emit("scalar", lambda e, jc=jc, t0=t0: e.dma_start(
                out=olod[:, jc * 512:(jc + 1) * 512], in_=io["o_dram"][jc, :, t0:t0 + 512]),
                waits=([("V", ev_prev)] if jc == 0 else []), inc="DS")
        old = kb.cnt["DS"]
        gsm = "DH" if gb == 0 else "DG"
        for dc in range(32):
            w = [("T", g2_rd[gb])] if dc == 0 else []
            if dc == 0 and gsm == "DG" and tt == 1:
                w = w + [("DG", cast_done)]
            kb.emit("gpsimd", lambda e, dc=dc, t0=t0, gb=gb: e.dma_start(
                out=g2b[:, gb * 16384 + dc * 512:gb * 16384 + (dc + 1) * 512],
                in_=io["h_dram"][dc, :, t0:t0 + 512]),
                waits=w, inc=gsm)
        g2ld = (gsm, kb.cnt[gsm])
        sq_guard = 0
        for jc in range(8):
            kb.emit("scalar", lambda e, jc=jc: e.square(
                osq[:], olod[:, jc * 512:(jc + 1) * 512]),
                waits=([("DS", old)] if jc == 0 else []) + [("T", sq_guard)], inc="A")
            aq = kb.cnt["A"]
            kb.emit("tensor", lambda e, jc=jc, first=(jc == 0): e.matmul(
                pS[0:1, :], ones[:, 0:1], olod[:, jc * 512:(jc + 1) * 512],
                start=first, stop=(jc == 7)), waits=[("DS", old)] if jc == 0 else [])
            kb.emit("tensor", lambda e, jc=jc, first=(jc == 0), aq=aq: e.matmul(
                pS2[0:1, :], ones[:, 0:1], osq[:],
                start=first, stop=(jc == 7)), waits=[("A", aq)], inc="T")
            sq_guard = kb.cnt["T"]
        tsum = kb.cnt["T"]
        kb.emit("vector", lambda e: e.tensor_scalar(MU, pS[0:1, :], 1.0 / D, None, op0=ALU.mult),
                waits=[("T", tsum)], chain=True)
        kb.emit("vector", lambda e: e.tensor_scalar(S2N, pS2[0:1, :], 1.0 / D, None, op0=ALU.mult),
                chain=True)
        kb.emit("vector", lambda e: e.tensor_tensor(VAR, MU, MU, op=ALU.mult), chain=True)
        kb.emit("vector", lambda e: e.tensor_tensor(VAR, S2N, VAR, op=ALU.subtract), chain=True)
        vvar = kb.cnt["V"]
        kb.emit("scalar", lambda e: e.activation(SQ, VAR, AF.Sqrt, bias=kb.tens["c_actc"][0:1, 0:1], scale=1.0),
                waits=[("V", vvar)], inc="A")
        asq2 = kb.cnt["A"]
        kb.emit("vector", lambda e: e.reciprocal(RSTD, SQ), waits=[("A", asq2)], chain=True)
        kb.emit("vector", lambda e: e.tensor_scalar(NMU, MU, -1.0, None, op0=ALU.mult), chain=True)
        vst = kb.cnt["V"]
        kb.emit("tensor", lambda e: e.matmul(pR[:, :], onesf[0:1, :], RSTD, start=True, stop=True),
                waits=[("V", vst), ("V", ev_prev)])
        kb.emit("tensor", lambda e: e.matmul(pM[:, :], onesf[0:1, :], NMU, start=True, stop=True),
                inc="T")
        trep = kb.cnt["T"]
        for jc in range(8):
            kb.emit("vector", lambda e, jc=jc: e.tensor_tensor(
                olnb[:, jc * 512:(jc + 1) * 512], olod[:, jc * 512:(jc + 1) * 512],
                pM[:, :], op=ALU.add), waits=[("T", trep)] if jc == 0 else [], chain=True)
            kb.emit("vector", lambda e, jc=jc: e.scalar_tensor_tensor(
                olnb[:, jc * 512:(jc + 1) * 512], olnb[:, jc * 512:(jc + 1) * 512],
                nwc[:, jc:jc + 1], pR[:, :], op0=ALU.mult, op1=ALU.mult), chain=True)
        ev_prev = kb.cnt["V"]
        voln = kb.cnt["V"]
        for oc in range(8):
            ob = oc % 2
            kb.emit("scalar", lambda e, oc=oc, t0=t0, ob=ob: e.dma_start(
                out=xfb[:, ob * 512:(ob + 1) * 512], in_=io["xT"][oc * 128:(oc + 1) * 128, t0:t0 + 512]),
                waits=[("V", yst_tail[ob])], inc="DS")
            xld = kb.cnt["DS"]
            for dc in range(32):
                kb.emit("tensor", lambda e, dc=dc, oc=oc, gb=gb, ob=ob, first=(dc == 0): e.matmul(
                    pFs[ob][:, :], w3b[:, oc * 4096 + dc * 128:oc * 4096 + dc * 128 + 128],
                    g2b[:, gb * 16384 + dc * 512:gb * 16384 + dc * 512 + 512],
                    start=first, stop=(dc == 31)),
                    waits=[g2ld, ("V", yst_tail[ob]), ("A", aw_tail[ob])] if dc == 0 else [])
            for jc in range(8):
                kb.emit("tensor", lambda e, jc=jc, oc=oc, ob=ob, first=(jc == 0): e.matmul(
                    pWs[ob][:, :], wob[:, oc * 1024 + jc * 128:oc * 1024 + jc * 128 + 128],
                    olnb[:, jc * 512:(jc + 1) * 512],
                    start=first, stop=(jc == 7)),
                    waits=[("V", voln)] if jc == 0 else [], inc=("T" if jc == 7 else None))
            tmm = kb.cnt["T"]
            kb.emit("scalar", lambda e, oc=oc, ob=ob: e.copy(
                ysb[:, ob * 512:(ob + 1) * 512], pWs[ob][:, :]),
                waits=[("T", tmm), ("DS", xld)], inc="A")
            ayw = kb.cnt["A"]
            aw_tail[ob] = ayw
            kb.emit("vector", lambda e, oc=oc, ob=ob: e.scalar_tensor_tensor(
                ysb[:, ob * 512:(ob + 1) * 512], pFs[ob][:, :], s3c[:, oc:oc + 1],
                ysb[:, ob * 512:(ob + 1) * 512],
                op0=ALU.mult, op1=ALU.add), waits=[("A", ayw), ("DS", xld)], chain=True)
            kb.emit("vector", lambda e, oc=oc, ob=ob: e.tensor_tensor(
                ysb[:, ob * 512:(ob + 1) * 512], ysb[:, ob * 512:(ob + 1) * 512],
                xfb[:, ob * 512:(ob + 1) * 512], op=ALU.add), chain=True)
            yv = kb.cnt["V"]
            yst_tail[ob] = yv
            kb.emit("scalar", lambda e, oc=oc, t0=t0, ob=ob, yv=yv: e.dma_start(
                out=io["y"][oc, :, t0:t0 + 512], in_=ysb[:, ob * 512:(ob + 1) * 512]),
                waits=[("V", yv)], inc="DS")
        g2_rd[gb] = kb.cnt["T"]


def build_program(phases=("p0", "p2", "l2", "l3", "fin"), expose=(), p2_nseg=NSEG):
    nc = bass.Bass("TRN2", target_bir_lowering=False)
    io = declare_io(nc, expose)
    stack = ExitStack()
    with stack:
        kb = KB(nc, stack)
        env = None
        xn_stack = ExitStack()
        with ExitStack() as ps0:
            env = phase0_xn(kb, io, ps0, xn_stack=xn_stack)
        if "xn" in expose:
            v = kb.cnt["V"]
            kb.emit("scalar", lambda e: e.dma_start(out=io["xn_dump"][:], in_=kb.tens["xn_fm"][:]),
                    waits=[("V", v)], inc="DS")
        kb.fence()
        if "l1" in phases:
            with ExitStack() as ps:
                phase_l1(kb, io, env, ps)
            kb.fence()
        if "p2" in phases:
            with ExitStack() as ps:
                phase_p2(kb, io, env, ps, nseg=p2_nseg)
            kb.fence()
        xn_stack.close()
        if "l2" in phases:
            with ExitStack() as ps:
                phase_hidden(kb, io, "h_dram", "g_dram", "h1t", "sh1", ps)
            kb.fence()
        if "l3" in phases:
            with ExitStack() as ps:
                phase_hidden(kb, io, "g_dram", "h_dram", "h2t", "sh2", ps)
            kb.fence()
        if "fin" in phases:
            with ExitStack() as ps:
                phase_final(kb, io, ps)
        kb.wait("tensor", [("DS", kb.cnt["DS"]), ("DT", kb.cnt["DT"]), ("DG", kb.cnt["DG"]),
                           ("V", kb.cnt["V"]), ("A", kb.cnt["A"])])
        kb.finalize()
    return nc


def kernel(**inputs):
    cm, xs = host_prepare(inputs)
    nc = build_program()
    in_maps = []
    for b in range(8):
        m = dict(cm)
        m["xT"] = xs[b]
        in_maps.append(m)
    res = run_bass_kernel_spmd(nc, in_maps, core_ids=list(range(8)))
    outs = []
    for b in range(8):
        yfm = np.asarray(res.results[b]["y"], dtype=np.float32)
        outs.append(yfm.reshape(1024, 4096).T)
    return np.ascontiguousarray(np.stack(outs)).astype(np.float32)


# version 15 (final)
# speedup vs baseline: 1.0662x; 1.0178x over previous
import numpy as np
import ml_dtypes
import concourse.bass as bass
import concourse.mybir as mybir
from concourse.bass_utils import run_bass_kernel_spmd
from contextlib import ExitStack

BF16 = mybir.dt.bfloat16
F32 = mybir.dt.float32
AF = mybir.ActivationFunctionType
ALU = mybir.AluOpType
AX = mybir.AxisListType
BND = ml_dtypes.bfloat16

T, D, NH, HD, CH = 4096, 1024, 16, 64, 64
NT = T // 128
NSEG = 8
TPS = 4
DHID = 4096
P2_HEADS = None
P2_FENCE_HEADS = False
P2_DBG = None
P2_SEC = {}
P2_TSEC = 99
LH_LIM = None
L1IL = False

def _tile_ocp(w):
    din, dout = w.shape
    dc, oc = din // 128, dout // 128
    return np.ascontiguousarray(
        w.reshape(dc, 128, oc, 128).transpose(2, 1, 0, 3).reshape(oc, 128, dc * 128))

def host_prepare(inputs):
    g = lambda n: np.ascontiguousarray(np.asarray(inputs[n], dtype=np.float32))
    cm = {}
    cm["w1t"] = _tile_ocp(g("L0_ffn_w1_w").T)
    cm["w2t"] = _tile_ocp(g("L0_ffn_w2_w").T)
    cm["h1t"] = _tile_ocp(g("L0_ffn_h1_w").T)
    cm["h2t"] = _tile_ocp(g("L0_ffn_h2_w").T)
    cm["w3t"] = _tile_ocp(g("L0_ffn_w3_w").T)
    cm["wot"] = _tile_ocp(g("L0_attn_Wo"))
    for nm, k in (("s1", "L0_ffn_w1_s"), ("s2", "L0_ffn_w2_s"),
                  ("sh1", "L0_ffn_h1_s"), ("sh2", "L0_ffn_h2_s"),
                  ("s3", "L0_ffn_w3_s"), ("nw", "L0_attn_norm_w")):
        v = g(k)
        cm[nm] = np.ascontiguousarray(v.reshape(-1, 128).T)
    qkvg = np.concatenate([g("L0_attn_Wq"), g("L0_attn_Wk"), g("L0_attn_Wv"),
                           g("L0_attn_Wgamma")], axis=1)
    cm["qkvg"] = np.ascontiguousarray(qkvg.reshape(8, 128, 3088))
    f1og1 = np.concatenate([g("L0_attn_Wf1"), g("L0_attn_Wog1")], axis=1)
    cm["f1og1"] = np.ascontiguousarray(f1og1.reshape(8, 128, 128))
    cm["wf2"] = g("L0_attn_Wf2")
    wog2 = g("L0_attn_Wog2")
    cm["wog2t"] = np.ascontiguousarray(wog2.reshape(64, 8, 128).transpose(1, 0, 2))
    cm["ident_b"] = np.eye(128).astype(BND)
    cm["ones_b"] = np.ones((128, 128), dtype=BND)
    cm["ones_f"] = np.ones((1, 128), dtype=np.float32)
    cm["actc"] = np.tile(np.array([[1e-5, 1.0]], np.float32), (128, 1))
    su = np.triu(np.ones((CH, CH), np.float32), 1)
    iu = np.triu(np.ones((CH, CH), np.float32), 0)
    sl = np.tril(np.ones((CH, CH), np.float32), -1)
    cm["maskct"] = np.ascontiguousarray(
        np.concatenate([np.concatenate([su, iu], 1)] * 2, 0)).astype(BND)
    cm["maskc"] = np.ascontiguousarray(np.concatenate([sl, sl], 0)).astype(BND)
    cm["lcum"] = np.ascontiguousarray(np.concatenate([iu] * 2, 0)).astype(BND)
    cm["lcum2"] = np.ascontiguousarray(
        np.concatenate([iu + np.eye(CH, dtype=np.float32)] * 2, 0)).astype(BND)
    i64d = np.concatenate([np.eye(64, dtype=np.float32)] * 2, 0)
    cm["i64d"] = i64d.astype(BND)
    xs = [np.ascontiguousarray(np.asarray(inputs["L0_attn_x"][b], np.float32).T)
          for b in range(8)]
    return cm, xs


class KB:
    ENGS = ("tensor", "vector", "scalar", "gpsimd", "sync")

    def __init__(self, nc, stack):
        self.nc = nc
        self.stack = stack
        self.streams = {e: [] for e in self.ENGS}
        self.sems = {}
        self.cnt = {}
        for s in ("T", "V", "A", "G", "F", "DS", "DT", "DG", "DH"):
            self.sems[s] = stack.enter_context(nc.semaphore("sem_" + s))
            self.cnt[s] = 0
        self.tens = {}
        self.waited = {s: set() for s in self.DMA_SEMS}

    def sb(self, name, shape, dtype, stack=None, side=None):
        kw = {"side": side} if side else {}
        t = (stack or self.stack).enter_context(self.nc.sbuf_tensor(name, shape, dtype, **kw))
        self.tens[name] = t
        return t

    def ps(self, name, shape=(128, 512), dtype=F32, stack=None):
        t = (stack or self.stack).enter_context(self.nc.psum_tensor(name, list(shape), dtype))
        self.tens[name] = t
        return t

    def fence(self):
        for eng in ("tensor", "vector", "scalar", "gpsimd"):
            self.emit(eng, lambda e: e.nop(), inc="F")
        snap = dict(self.cnt)
        waits = [(s, snap[s]) for s in ("F", "DS", "DT", "DG", "DH")]
        for eng in self.ENGS:
            self.wait(eng, waits)

    SELF_SEM = {"tensor": "T", "vector": "V", "scalar": "A", "gpsimd": "G"}
    DMA_SEMS = ("DS", "DT", "DG", "DH")

    def emit(self, eng, make, waits=(), inc=None, chain=False):
        v = None
        if eng in ("vector", "scalar", "gpsimd"):
            chain = True
        if chain:
            s = self.SELF_SEM[eng]
            waits = list(waits) + [(s, self.cnt[s])]
            if inc is None:
                inc = s
        if inc is not None:
            k = 16 if inc in self.DMA_SEMS else 1
            self.cnt[inc] += k
            v = self.cnt[inc]
        waits = [(s, w) for (s, w) in waits if w > 0]
        sems = self.sems
        kk = (16 if inc in self.DMA_SEMS else 1) if inc else 0
        def f(e, make=make, inc=inc, k=kk):
            for s, w in f.waits:
                e.wait_ge(sems[s], w)
            ins = make(e)
            if inc is not None:
                ins.then_inc(sems[inc], k)
        f.waits = waits
        f.inc_sem = inc
        f.inc_k = kk
        f.dma_sem = inc if inc in self.DMA_SEMS else None
        f.dma_count = v
        self.streams[eng].append(f)
        return v

    def wait(self, eng, waits):
        waits = [(s, w) for (s, w) in waits if w > 0]
        if not waits:
            return
        sems = self.sems
        def f(e):
            for s, w in f.waits:
                e.wait_ge(sems[s], w)
        f.waits = waits
        f.inc_sem = None
        f.inc_k = 0
        f.dma_sem = None
        f.dma_count = None
        self.streams[eng].append(f)

    def finalize(self):
        waited = {s: set() for s in self.DMA_SEMS}
        succ = {s: {} for s in self.DMA_SEMS}
        for eng in self.ENGS:
            for f in self.streams[eng]:
                for s, w in f.waits:
                    if s in self.DMA_SEMS:
                        waited[s].add(w)
                if f.dma_sem is not None:
                    succ[f.dma_sem][f.dma_count] = f
        for s, vals in waited.items():
            for v in vals:
                nf = succ[s].get(v + 16)
                if nf is not None and not any(
                        ss == s and ww >= v for ss, ww in nf.waits):
                    nf.waits = list(nf.waits) + [(s, v)]
        def run(e, fs):
            for f in fs:
                f(e)
        with self.nc.Block() as block:
            @block.tensor
            def _(e):
                run(e, self.streams["tensor"])
            @block.vector
            def _(e):
                run(e, self.streams["vector"])
            @block.scalar
            def _(e):
                run(e, self.streams["scalar"])
            @block.gpsimd
            def _(e):
                run(e, self.streams["gpsimd"])
            @block.sync
            def _(e):
                run(e, self.streams["sync"])


def declare_io(nc, expose=()):
    d = {}
    def di(name, shape, dtype, kind="ExternalInput"):
        if kind == "Internal" and name in expose:
            kind = "ExternalOutput"
        if kind == "Internal" and ("in:" + name) in expose:
            kind = "ExternalInput"
        d[name] = nc.dram_tensor(name, list(shape), dtype, kind=kind).ap()
    di("xT", (D, T), F32)
    di("qkvg", (8, 128, 3088), F32)
    di("f1og1", (8, 128, 128), F32)
    di("wf2", (64, D), F32)
    di("wog2t", (8, 64, 128), F32)
    di("w1t", (32, 128, 1024), F32)
    di("w2t", (32, 128, 1024), F32)
    di("h1t", (32, 128, 4096), F32)
    di("h2t", (32, 128, 4096), F32)
    di("w3t", (8, 128, 4096), F32)
    di("wot", (8, 128, 1024), F32)
    for nm, n in (("s1", 32), ("s2", 32), ("sh1", 32), ("sh2", 32), ("s3", 8), ("nw", 8)):
        di(nm, (128, n), F32)
    di("ident_b", (128, 128), BF16)
    di("ones_b", (128, 128), BF16)
    di("ones_f", (1, 128), F32)
    di("actc", (128, 2), F32)
    di("maskct", (128, 128), BF16)
    di("maskc", (128, 64), BF16)
    di("lcum", (128, 64), BF16)
    di("lcum2", (128, 64), BF16)
    di("i64d", (128, 64), BF16)
    di("y", (8, 128, T), F32, kind="ExternalOutput")
    di("h_dram", (32, 128, T), BF16, kind="Internal")
    di("g_dram", (32, 128, T), BF16, kind="Internal")
    di("o_dram", (8, 128, T), BF16, kind="Internal")
    if "dbg" in expose:
        di("dbg", (128, 4096), BF16, kind="ExternalOutput")
    if "xn" in expose:
        di("xn_dump", (128, 8 * T), BF16, kind="ExternalOutput")
    return d


def phase0_xn(kb, io, pstk, xn_stack=None):
    nc = kb.nc
    xn = kb.sb("xn_fm", [128, 8 * T], BF16, stack=xn_stack, side="right")
    ones = kb.sb("c_ones", [128, 128], BF16)
    onesf = kb.sb("c_onesf", [1, 128], F32)
    actc = kb.sb("c_actc", [128, 2], F32)
    identb = kb.sb("c_identb", [128, 128], BF16)
    maskct = kb.sb("c_maskct", [128, 128], BF16)
    maskc = kb.sb("c_maskc", [128, 64], BF16)
    lcum = kb.sb("c_lcum", [128, 64], BF16)
    lcum2 = kb.sb("c_lcum2", [128, 64], BF16)
    i64d = kb.sb("c_i64d", [128, 64], BF16)
    nwc = kb.sb("c_nw", [128, 8], F32)
    xf = kb.sb("p0_xf", [128, 2 * 8 * 512], F32, stack=pstk)
    xb = kb.sb("p0_xb", [128, 8 * 512], BF16, stack=pstk)
    x2b = kb.sb("p0_x2b", [128, 8 * 512], BF16, stack=pstk)
    st = kb.sb("p0_stats", [2, 6 * 512], F32, stack=pstk)
    psS = kb.ps("ps0_s", (2, 512), stack=pstk)
    psS2 = kb.ps("ps0_s2", (2, 512), stack=pstk)
    psR = kb.ps("ps0_r", (128, 512), stack=pstk)
    psM = kb.ps("ps0_m", (128, 512), stack=pstk)

    for nm, dst in (("ones_b", ones), ("ones_f", onesf), ("actc", actc), ("ident_b", identb),
                    ("maskct", maskct), ("maskc", maskc), ("lcum", lcum),
                    ("lcum2", lcum2), ("i64d", i64d), ("nw", nwc)):
        kb.emit("scalar", lambda e, dst=dst, nm=nm: e.dma_start(out=dst[:], in_=io[nm][:]), inc="DS")
    c_done = kb.cnt["DS"]

    ev_prev = 0
    tsum_prev = 0
    for tt in range(8):
        t0 = tt * 512
        pb = (tt % 2) * 8 * 512
        for dc in range(8):
            kb.emit("scalar", lambda e, dc=dc, t0=t0, pb=pb: e.dma_start(
                out=xf[:, pb + dc * 512: pb + (dc + 1) * 512],
                in_=io["xT"][dc * 128:(dc + 1) * 128, t0:t0 + 512]), inc="DS")
        ld = kb.cnt["DS"]
        for dc in range(8):
            vv = kb.emit("vector", lambda e, dc=dc, pb=pb: e.tensor_copy(
                xb[:, dc * 512:(dc + 1) * 512], xf[:, pb + dc * 512: pb + dc * 512 + 512]),
                waits=([("DS", ld), ("T", tsum_prev)] if dc == 0 else []), inc="V")
            kb.emit("scalar", lambda e, dc=dc: e.square(
                x2b[:, dc * 512:(dc + 1) * 512], xb[:, dc * 512:(dc + 1) * 512]),
                waits=[("V", vv)] + ([("T", tsum_prev)] if dc == 0 else []), inc="A")
        vcast = kb.cnt["V"]; acast = kb.cnt["A"]
        for dc in range(8):
            kb.emit("tensor", lambda e, dc=dc, first=(dc == 0): e.matmul(
                psS[0:1, :], ones[:, 0:1], xb[:, dc * 512:(dc + 1) * 512],
                start=first, stop=(dc == 7)),
                waits=([("V", vcast)] if dc == 0 else []))
            kb.emit("tensor", lambda e, dc=dc, first=(dc == 0): e.matmul(
                psS2[0:1, :], ones[:, 0:1], x2b[:, dc * 512:(dc + 1) * 512],
                start=first, stop=(dc == 7)),
                waits=([("A", acast)] if dc == 0 else []), inc=("T" if dc == 7 else None))
        tsum = kb.cnt["T"]
        tsum_prev = tsum
        MU, S2N, VAR, SQ, RSTD, NMR = (st[0:1, i * 512:(i + 1) * 512] for i in range(6))
        kb.emit("vector", lambda e: e.tensor_scalar(MU, psS[0:1, :], 1.0 / D, None, op0=ALU.mult),
                waits=[("T", tsum)], chain=True)
        kb.emit("vector", lambda e: e.tensor_scalar(S2N, psS2[0:1, :], 1.0 / D, None, op0=ALU.mult),
                chain=True)
        kb.emit("vector", lambda e: e.tensor_tensor(VAR, MU, MU, op=ALU.mult), chain=True)
        kb.emit("vector", lambda e: e.tensor_tensor(VAR, S2N, VAR, op=ALU.subtract), chain=True)
        vvar = kb.cnt["V"]
        kb.emit("scalar", lambda e: e.activation(SQ, VAR, AF.Sqrt, bias=actc[0:1, 0:1], scale=1.0),
                waits=[("V", vvar)], inc="A")
        asq = kb.cnt["A"]
        kb.emit("vector", lambda e: e.reciprocal(RSTD, SQ), waits=[("A", asq)], chain=True)
        vrstd = kb.cnt["V"]
        kb.emit("vector", lambda e: e.scalar_tensor_tensor(NMR, MU, -1.0, RSTD, op0=ALU.mult, op1=ALU.mult),
                chain=True)
        vnmr = kb.cnt["V"]
        kb.emit("tensor", lambda e: e.matmul(psR[:, :], onesf[0:1, :], RSTD, start=True, stop=True),
                waits=[("V", vrstd), ("V", ev_prev)])
        kb.emit("tensor", lambda e: e.matmul(psM[:, :], onesf[0:1, :], NMR, start=True, stop=True),
                waits=[("V", vnmr)], inc="T")
        trep = kb.cnt["T"]
        for dc in range(8):
            kb.emit("vector", lambda e, dc=dc, pb=pb: e.tensor_tensor(
                xb[:, dc * 512:(dc + 1) * 512],
                xf[:, pb + dc * 512: pb + dc * 512 + 512], psR[:, :], op=ALU.mult),
                waits=[("T", trep)] if dc == 0 else [], chain=True)
            kb.emit("vector", lambda e, dc=dc, t0=t0: e.tensor_tensor(
                xn[:, dc * T + t0: dc * T + t0 + 512],
                xb[:, dc * 512:(dc + 1) * 512], psM[:, :], op=ALU.add), chain=True)
        ev_prev = kb.cnt["V"]
    kb.wait("tensor", [("DS", c_done)])
    return dict(xn=xn, ones=ones, onesf=onesf, identb=identb, maskct=maskct,
                maskc=maskc, lcum=lcum, lcum2=lcum2, i64d=i64d, nwc=nwc,
                xn_ready=kb.cnt["V"])


def phase_l1(kb, io, env, pstk):
    xn = env["xn"]
    s1c = kb.sb("l1_s1", [128, 32], F32, stack=pstk)
    s2c = kb.sb("l1_s2", [128, 32], F32, stack=pstk)
    wbb = kb.sb("l1_wbb", [128, 2 * 2 * 1024], BF16, stack=pstk)
    hsb = kb.sb("l1_h", [128, 2 * 512], BF16, stack=pstk)
    tsl = kb.sb("l1_tsil", [128, 2 * 512], BF16, stack=pstk)
    psA = [kb.ps("psl1_a%d" % i, stack=pstk) for i in range(2)]
    psB = [kb.ps("psl1_b%d" % i, stack=pstk) for i in range(2)]
    kb.emit("scalar", lambda e: e.dma_start(out=s1c[:], in_=io["s1"][:]), inc="DS")
    kb.emit("scalar", lambda e: e.dma_start(out=s2c[:], in_=io["s2"][:]), inc="DS")
    sc_ds = kb.cnt["DS"]

    xn_ready = env["xn_ready"]
    ldq = []
    mm_after_panel = {0: 0, 1: 0}
    def load_panel(oc):
        b = (oc % 2)
        sm = "DG" if b == 0 else "DH"
        kb.emit("gpsimd", lambda e, oc=oc, b=b: e.dma_start(
            out=wbb[:, b * 2048: b * 2048 + 1024], in_=io["w1t"][oc]),
            waits=[("T", mm_after_panel[b])], inc=sm)
        v = kb.emit("gpsimd", lambda e, oc=oc, b=b: e.dma_start(
            out=wbb[:, b * 2048 + 1024: b * 2048 + 2048], in_=io["w2t"][oc]), inc=sm)
        ldq.append((sm, v))

    load_panel(0)
    load_panel(1)
    ev_tail = {0: 0, 1: 0}
    evA_tail = {0: 0, 1: 0}
    st_tail = {0: 0, 1: 0}
    for oc in range(32):
        wb = oc % 2
        for tt in range(8):
            b = tt % 2
            w0 = []
            if tt == 0:
                w0 = [ldq[oc], ("V", xn_ready)]
            for dc in range(8):
                kb.emit("tensor", lambda e, dc=dc, b=b, wb=wb, t0=tt * 512, first=(dc == 0): e.matmul(
                    psA[b][:, :], wbb[:, wb * 2048 + dc * 128: wb * 2048 + dc * 128 + 128],
                    xn[:, dc * T + t0: dc * T + t0 + 512], start=first, stop=(dc == 7)),
                    waits=(w0 if dc == 0 else []) + ([("A", evA_tail[b]), ("V", ev_tail[b])] if dc == 0 else []))
            for dc in range(8):
                kb.emit("tensor", lambda e, dc=dc, b=b, wb=wb, t0=tt * 512, first=(dc == 0): e.matmul(
                    psB[b][:, :], wbb[:, wb * 2048 + 1024 + dc * 128: wb * 2048 + 1024 + dc * 128 + 128],
                    xn[:, dc * T + t0: dc * T + t0 + 512], start=first, stop=(dc == 7)),
                    inc=("T" if dc == 7 else None))
            tmm = kb.cnt["T"]
            kb.emit("scalar", lambda e, b=b, oc=oc: e.activation(
                tsl[:, b * 512:(b + 1) * 512], psA[b][:, :], AF.Silu,
                scale=s1c[:, oc:oc + 1]),
                waits=[("T", tmm), ("V", ev_tail[b])]
                + ([("DS", sc_ds)] if oc == 0 and tt == 0 else []), inc="A")
            evA_tail[b] = kb.cnt["A"]
            kb.emit("vector", lambda e, b=b, oc=oc: e.scalar_tensor_tensor(
                hsb[:, b * 512:(b + 1) * 512], psB[b][:, :], s2c[:, oc:oc + 1],
                tsl[:, b * 512:(b + 1) * 512], op0=ALU.mult, op1=ALU.mult),
                waits=[("A", evA_tail[b]), ("DS", st_tail[b])]
                + ([("DS", sc_ds)] if oc == 0 and tt == 0 else []), inc="V")
            ev_tail[b] = kb.cnt["V"]
            kb.emit("scalar", lambda e, b=b, oc=oc, t0=tt * 512: e.dma_start(
                out=io["h_dram"][oc, :, t0:t0 + 512], in_=hsb[:, b * 512:(b + 1) * 512]),
                waits=[("V", ev_tail[b])], inc="DS")
            st_tail[b] = kb.cnt["DS"]
        mm_after_panel[wb] = kb.cnt["T"]
        if oc + 2 < 32:
            load_panel(oc + 2)
    env["l1_done_ds"] = kb.cnt["DS"]
    env["l1_done_T"] = kb.cnt["T"]


def phase_p2(kb, io, env, pstk, nseg=NSEG):
    xn = env["xn"]; identb = env["identb"]
    maskct = env["maskct"]; maskc = env["maskc"]
    lcum = env["lcum"]; lcum2 = env["lcum2"]; i64d = env["i64d"]
    L = lambda n, sh, dt: kb.sb(n, sh, dt, stack=pstk)
    qkvgw = L("p2_qkvgw", [128, 8 * 3088], BF16)
    f1og1w = L("p2_f1og1", [128, 8 * 128], BF16)
    wf2w = L("p2_wf2", [64, 1024], BF16)
    wog2w = L("p2_wog2", [64, 8 * 128], BF16)
    pack = L("p2_pack", [128, 2 * 4096], BF16)
    fmp = L("p2_fmp", [128, 2 * 4096], BF16)
    vtm2 = L("p2_vtm2", [128, 2 * 2048], BF16)
    ogp = L("p2_ogp", [128, 8 * 512], BF16)
    osg = L("p2_osg", [128, 8 * 512], BF16)
    z1f = L("p2_z1", [64, 512], BF16)
    og1f = L("p2_og1", [64, 512], BF16)
    states = L("p2_S", [128, 512], BF16)
    ctpt = L("p2_ctpt", [128, 128], BF16)
    cun = L("p2_cun", [128, 64], BF16)
    powsb = L("p2_pow", [128, 2 * 128], BF16)
    xw = L("p2_xw", [128, 2 * 128], BF16)
    ctptb = L("p2_ctptb", [128, 128], BF16)
    cunb = L("p2_cunb", [128, 64], BF16)
    powsbb = L("p2_powb", [128, 2 * 128], BF16)
    xwb = L("p2_xwb", [128, 2 * 128], BF16)
    sumb = L("p2_sumb", [128, 256], BF16)
    qtmp = L("p2_qtmp", [128, 1024], BF16)
    ktmp = L("p2_ktmp", [128, 1024], BF16)
    knk = L("p2_knk", [128, 1024], F32)
    kn = L("p2_kn", [128, 64], F32)
    emF = L("p2_emF", [128, 1024], BF16)
    eagF = L("p2_eagF", [128, 1024], BF16)
    gtmp = L("p2_gt", [128, 16], F32)
    gkb = L("p2_gkb", [128, 1024], BF16)
    scr1 = L("p2_scr1", [128, 512], BF16)
    scr2 = L("p2_scr2", [128, 512], BF16)
    pg4 = kb.ps("p2_pg4", (128, 1024), stack=pstk)
    pg4b = kb.ps("p2_pg4b", (128, 1024), stack=pstk)
    px2 = kb.ps("p2_px2", (128, 1024), stack=pstk)
    ps4 = kb.ps("p2_ps4", (128, 1024), stack=pstk)
    l1_s1 = L("p2l1_s1", [128, 32], F32)
    l1_s2 = L("p2l1_s2", [128, 32], F32)
    l1_wbb = L("p2l1_wbb", [128, 2 * 1024], BF16)
    l1_hsb = L("p2l1_h", [128, 512], BF16)
    l1_tsl = L("p2l1_t", [128, 512], BF16)

    for dc in range(8):
        kb.emit("gpsimd", lambda e, dc=dc: e.dma_start(
            out=qkvgw[:, dc * 3088:(dc + 1) * 3088], in_=io["qkvg"][dc]), inc="DG")
    kb.emit("gpsimd", lambda e: e.dma_start(
        out=f1og1w[:].rearrange("p (a b) -> p a b", b=128),
        in_=io["f1og1"].rearrange("a p b -> p a b")), inc="DG")
    kb.emit("gpsimd", lambda e: e.dma_start(out=wf2w[:], in_=io["wf2"][:]), inc="DG")
    kb.emit("gpsimd", lambda e: e.dma_start(
        out=wog2w[:].rearrange("p (a b) -> p a b", b=128),
        in_=io["wog2t"].rearrange("a p b -> p a b")), inc="DG")
    wready = kb.cnt["DG"]
    kb.emit("vector", lambda e: e.memset(states[:], 0.0))
    kb.emit("vector", lambda e: e.memset(ctpt[:], 0.0))
    if P2_HEADS is not None:
        kb.emit("vector", lambda e: e.memset(osg[:], 0.0))
    kb.emit("vector", lambda e: e.memset(cun[:], 0.0), inc="V")
    kb.wait("tensor", [("DG", wready), ("V", kb.cnt["V"])])
    kb.wait("scalar", [("DG", wready)])
    kb.wait("gpsimd", [("V", kb.cnt["V"])])

    fmp4 = fmp.rearrange("p (tb ti jc r) -> p tb ti jc r", tb=2, ti=4, jc=8, r=128)

    fr = dict(psG=[], psH=[], bA=[], bB=[], bC=[], bF=[],
              qtmp=[], ktmp=[], knk=[], gtmp=[], gkb=[], emF=[], eagF=[], scr=[],
              vtm=[[], []], pack=[[], []], fmpc=[[], []], og=[], osg=[], chain=[],
              ct=[], pow=[], xwg=[], sumbg=[])
    LN = [dict(bank=pg4, xoff=0, ct=ctpt, cu=cun, pw=powsb, xw=xw),
          dict(bank=pg4b, xoff=512, ct=ctptb, cu=cunb, pw=powsbb, xw=xwb)]
    frl = [dict(bA=[], bX=[], pow=[], sumev=[], psS=[], psO=[], chain=[],
                vmask=0, vpow=0, axw=0, tg=0, tx=0, tsq=0, taj=0, tch=0, gadd=0)
           for _ in range(2)]

    def W(*gs):
        out = []
        for g in gs:
            out += g
        return out

    kb.emit("scalar", lambda e: e.dma_start(out=l1_s1[:], in_=io["s1"][:]), inc="DS")
    kb.emit("scalar", lambda e: e.dma_start(out=l1_s2[:], in_=io["s2"][:]), inc="DS")
    l1_sc_ds = kb.cnt["DS"]
    l1_ldq = []
    l1_state = {"mm_after": 0, "st": 0, "ts": 0}

    def l1_load_panel(oc):
        sm = "DG" if oc % 2 == 0 else "DH"
        kb.emit("gpsimd", lambda e, oc=oc: e.dma_start(
            out=l1_wbb[:, 0:1024], in_=io["w1t"][oc]),
            waits=[("T", l1_state["mm_after"])], inc=sm)
        v = kb.emit("gpsimd", lambda e, oc=oc: e.dma_start(
            out=l1_wbb[:, 1024:2048], in_=io["w2t"][oc]), inc=sm)
        l1_ldq.append((sm, v))

    l1_load_panel(0)
    l1_q = {"u": 0, "m": 0, "t1": 0}

    def l1_tick():
        if not L1IL:
            return
        u = l1_q["u"]
        if u >= 256:
            return
        m = l1_q["m"]
        oc, tt = u // 8, u % 8
        t0 = tt * 512
        if m < 8:
            dc = m
            w = (([l1_ldq[oc]] if tt == 0 else []) + W(fr["psG"])) if dc == 0 else []
            kb.emit("tensor", lambda e, dc=dc, t0=t0, first=(dc == 0): e.matmul(
                ps4[:, 0:512], l1_wbb[:, dc * 128: dc * 128 + 128],
                xn[:, dc * T + t0: dc * T + t0 + 512], start=first, stop=(dc == 7)),
                waits=w, inc=("T" if dc == 7 else None))
            if dc == 7:
                l1_q["t1"] = kb.cnt["T"]
        else:
            dc = m - 8
            w = W(fr["psH"]) if dc == 0 else []
            kb.emit("tensor", lambda e, dc=dc, t0=t0, first=(dc == 0): e.matmul(
                ps4[:, 512:1024], l1_wbb[:, 1024 + dc * 128: 1024 + dc * 128 + 128],
                xn[:, dc * T + t0: dc * T + t0 + 512], start=first, stop=(dc == 7)),
                waits=w, inc=("T" if dc == 7 else None))
        l1_q["m"] = m + 1
        if l1_q["m"] < 16:
            return
        l1_q["m"] = 0
        l1_q["u"] = u + 1
        t2 = kb.cnt["T"]
        t1 = l1_q["t1"]
        kb.emit("scalar", lambda e, oc=oc: e.activation(
            l1_tsl[:], ps4[:, 0:512], AF.Silu, scale=l1_s1[:, oc:oc + 1]),
            waits=[("T", t1), ("V", l1_state["ts"])]
            + ([("DS", l1_sc_ds)] if u == 0 else []), inc="A")
        a1 = kb.cnt["A"]
        fr["psG"] = [("A", a1)]
        kb.emit("vector", lambda e, oc=oc: e.scalar_tensor_tensor(
            l1_hsb[:], ps4[:, 512:1024], l1_s2[:, oc:oc + 1],
            l1_tsl[:], op0=ALU.mult, op1=ALU.mult),
            waits=[("T", t2), ("A", a1), ("DS", l1_state["st"])]
            + ([("DS", l1_sc_ds)] if u == 0 else []), inc="V")
        v1 = kb.cnt["V"]
        fr["psH"] = [("V", v1)]
        l1_state["ts"] = v1
        kb.emit("scalar", lambda e, oc=oc, t0=t0: e.dma_start(
            out=io["h_dram"][oc, :, t0:t0 + 512], in_=l1_hsb[:]),
            waits=[("V", v1)], inc="DS")
        l1_state["st"] = kb.cnt["DS"]
        if tt == 7:
            l1_state["mm_after"] = kb.cnt["T"]
            if oc + 1 < 32:
                l1_load_panel(oc + 1)

    def projgrp(tk, lo, hi, bank, bguard, waits0=()):
        t0 = tk * 128
        for dc in range(8):
            w = list(waits0) + W(bguard) if dc == 0 else []
            kb.emit("tensor", lambda e, dc=dc, t0=t0, lo=lo, hi=hi, bank=bank,
                    first=(dc == 0): e.matmul(
                bank[:, 0:hi - lo], xn[:, dc * T + t0:dc * T + t0 + 128],
                qkvgw[:, dc * 3088 + lo:dc * 3088 + hi],
                start=first, stop=(dc == 7)),
                waits=w, inc=("T" if dc == 7 else None))
        return kb.cnt["T"]

    bG = lambda: ps4[:, 0:512]
    bH = lambda: ps4[:, 512:1024]
    bA = lambda: pg4[:, 0:512]
    bB = lambda: pg4[:, 512:1024]
    bC = lambda: px2[:, 0:512]
    bF = lambda: px2[:, 512:1024]

    for s in range(nseg):
        s0 = s * 512
        for dc in range(8):
            kb.emit("tensor", lambda e, dc=dc, s0=s0, first=(dc == 0): e.matmul(
                bG()[:, 0:512], f1og1w[:, dc * 128:(dc + 1) * 128],
                xn[:, dc * T + s0:dc * T + s0 + 512], start=first, stop=(dc == 7)),
                waits=W(fr["psG"]) if dc == 0 else [], inc=("T" if dc == 7 else None))
        tz = kb.cnt["T"]
        kb.emit("scalar", lambda e: e.copy(z1f[:], bG()[0:64, 0:512]), waits=[("T", tz)])
        kb.emit("scalar", lambda e: e.copy(og1f[:], bG()[64:128, 0:512]), inc="A")
        fr["psG"] = [("A", kb.cnt["A"])]
        az = kb.cnt["A"]
        for oc in range(8):
            w0 = ([("A", az)] + W(fr["psH"])) if oc == 0 else [("A", kb.cnt["A"])]
            kb.emit("tensor", lambda e, oc=oc: e.matmul(
                bH()[:, 0:512], wog2w[:, oc * 128:(oc + 1) * 128], og1f[:],
                start=True, stop=True), waits=w0, inc="T")
            tog = kb.cnt["T"]
            kb.emit("scalar", lambda e, oc=oc, tog=tog: e.activation(
                ogp[:, oc * 512:(oc + 1) * 512], bH()[:, 0:512], AF.Sigmoid),
                waits=[("T", tog)] + W(fr["og"]), inc="A")
        fr["psH"] = [("A", kb.cnt["A"])]

        def prologue(tk4):
            tk = s * TPS + tk4
            tb = tk % 2
            t0 = tk * 128
            if P2_TSEC < 1:
                return None
            tq = projgrp(tk, 0, 512, bG(), fr["psG"])
            kb.emit("scalar", lambda e, tq=tq: e.activation(
                qtmp[:, 0:512], bG()[:, 0:512], AF.Silu), waits=[("T", tq)] + W(fr["qtmp"]), inc="A")
            fr["psG"] = [("A", kb.cnt["A"])]
            tq = projgrp(tk, 512, 1024, bH(), fr["psH"])
            kb.emit("scalar", lambda e, tq=tq: e.activation(
                qtmp[:, 512:1024], bH()[:, 0:512], AF.Silu), waits=[("T", tq)], inc="A")
            fr["psH"] = [("A", kb.cnt["A"])]
            tq = projgrp(tk, 1024, 1536, bG(), fr["psG"])
            kb.emit("scalar", lambda e, tq=tq: e.activation(
                ktmp[:, 0:512], bG()[:, 0:512], AF.Silu), waits=[("T", tq)] + W(fr["ktmp"]), inc="A")
            fr["psG"] = [("A", kb.cnt["A"])]
            tq = projgrp(tk, 1536, 2048, bH(), fr["psH"])
            kb.emit("scalar", lambda e, tq=tq: e.activation(
                ktmp[:, 512:1024], bH()[:, 0:512], AF.Silu), waits=[("T", tq)], inc="A")
            fr["psH"] = [("A", kb.cnt["A"])]
            kready = kb.cnt["A"]
            for half in range(2):
                tq = projgrp(tk, 2048 + half * 512, 2560 + half * 512,
                             bG() if half == 0 else bH(),
                             fr["psG"] if half == 0 else fr["psH"])
                base = half * 1024
                kb.emit("vector", lambda e, tq=tq, base=base, half=half, tb=tb: e.tensor_copy(
                    vtm2[:, tb * 2048:(tb + 1) * 2048].rearrange(
                        "p (q c) -> p q c", c=64)[:, (base // 64 + 1)::4, :][:, 0:4, :],
                    (bG() if half == 0 else bH())[:, 0:512].rearrange(
                        "p (a b) -> p a b", b=64)[:, 0::2, :]),
                    waits=[("T", tq)] + (W(fr["vtm"][tb]) if half == 0 else []), inc="V")
                kb.emit("vector", lambda e, base=base, half=half, tb=tb: e.tensor_copy(
                    vtm2[:, tb * 2048:(tb + 1) * 2048].rearrange(
                        "p (q c) -> p q c", c=64)[:, (base // 64 + 2)::4, :][:, 0:4, :],
                    (bG() if half == 0 else bH())[:, 0:512].rearrange(
                        "p (a b) -> p a b", b=64)[:, 1::2, :]), inc="V")
                if half == 0:
                    fr["psG"] = [("V", kb.cnt["V"])]
                else:
                    fr["psH"] = [("V", kb.cnt["V"])]
            tq = projgrp(tk, 3072, 3088, bG(), fr["psG"])
            kb.emit("scalar", lambda e, tq=tq: e.activation(
                gtmp[:, 0:16], bG()[:, 0:16], AF.Sigmoid),
                waits=[("T", tq)] + W(fr["gtmp"]), inc="A")
            fr["psG"] = [("A", kb.cnt["A"])]
            if P2_TSEC < 2:
                return None
            for hf in range(2):
                bank, bn = (bH(), "psH") if hf == 0 else (bG(), "psG")
                kb.emit("tensor", lambda e, hf=hf, tk4=tk4, bank=bank: e.matmul(
                    bank[:, 0:512], z1f[:, tk4 * 128:(tk4 + 1) * 128],
                    wf2w[:, hf * 512:(hf + 1) * 512], start=True, stop=True),
                    waits=W(fr[bn], fr["scr"], fr["gkb"]), inc="T")
                tzz = kb.cnt["T"]
                kb.emit("scalar", lambda e, bank=bank: e.activation(scr1[:], bank[:, 0:512], AF.Abs),
                        waits=[("T", tzz)])
                aabs = kb.cnt["A"]
                kb.emit("scalar", lambda e: e.activation(scr1[:], scr1[:], AF.Exp, scale=-1.0))
                kb.emit("scalar", lambda e: e.activation(
                    scr1[:], scr1[:], AF.Ln, bias=kb.tens["c_actc"][:, 1:2]), inc="A")
                aln = kb.cnt["A"]
                kb.emit("vector", lambda e, bank=bank: e.tensor_scalar(
                    scr2[:], bank[:, 0:512], 0.0, None, op0=ALU.min),
                    waits=[("T", tzz), ("A", aabs)])
                kb.emit("vector", lambda e, hf=hf: e.tensor_tensor(
                    gkb[:, hf * 512:(hf + 1) * 512], scr2[:], scr1[:], op=ALU.subtract),
                    waits=[("A", aln)], inc="V")
                fr[bn] = [("V", kb.cnt["V"])]
                fr["scr"] = [("V", kb.cnt["V"])]
            vgk = kb.cnt["V"]
            if P2_TSEC < 3:
                return None
            for hf in range(2):
                for ch in (0, 64):
                    kb.emit("tensor", lambda e, ch=ch, hf=hf: e.matmul(
                        bG()[ch:ch + 64, 0:512], lcum[ch:ch + 64, :],
                        gkb[ch:ch + 64, hf * 512:(hf + 1) * 512],
                        start=True, stop=True, tile_position=(ch, ch), skip_group_check=True),
                        waits=([("V", vgk)] + W(fr["psG"], fr["emF"], fr["pack"][tb])
                               if ch == 0 else []))
                    kb.emit("tensor", lambda e, ch=ch, hf=hf: e.matmul(
                        bH()[ch:ch + 64, 0:512], lcum2[ch:ch + 64, :],
                        gkb[ch:ch + 64, hf * 512:(hf + 1) * 512],
                        start=True, stop=True, tile_position=(ch, ch), skip_group_check=True),
                        waits=(W(fr["psH"], fr["eagF"]) if ch == 0 else []),
                        inc=("T" if ch == 64 else None))
                tcs = kb.cnt["T"]
                kb.emit("scalar", lambda e, hf=hf, tb=tb, tcs=tcs: e.activation(
                    pack[:, tb * 4096 + 3 * 1024 + hf * 512: tb * 4096 + 3 * 1024 + hf * 512 + 512],
                    bG()[:, 0:512], AF.Exp), waits=[("T", tcs)], inc="A")
                kb.emit("scalar", lambda e, hf=hf: e.activation(
                    emF[:, hf * 512:(hf + 1) * 512], bG()[:, 0:512], AF.Exp, scale=-1.0), inc="A")
                fr["psG"] = [("A", kb.cnt["A"])]
                fr["emF"] = [("A", kb.cnt["A"])]
                kb.emit("scalar", lambda e, hf=hf: e.activation(
                    eagF[:, hf * 512:(hf + 1) * 512], bH()[:, 0:512], AF.Exp), inc="A")
                fr["psH"] = [("A", kb.cnt["A"])]
                fr["eagF"] = [("A", kb.cnt["A"])]
            aexp = kb.cnt["A"]
            if P2_TSEC < 4:
                return None
            kb.emit("scalar", lambda e: e.square(knk[:], ktmp[:]),
                    waits=[("A", kready)] + W(fr["knk"]), inc="A")
            fr["ktmp"] = [("A", kb.cnt["A"])]
            asq = kb.cnt["A"]
            kb.emit("vector", lambda e: e.reduce_sum(
                kn[:, 0:16], knk.rearrange("p (h j) -> p h j", h=16)[:, :, :], axis=AX.X),
                waits=[("A", asq)], inc="V")
            vkn = kb.cnt["V"]
            kb.emit("scalar", lambda e: e.activation(kn[:, 16:32], kn[:, 0:16], AF.Sqrt),
                    waits=[("V", vkn)], inc="A")
            akn = kb.cnt["A"]
            kb.emit("vector", lambda e: e.tensor_scalar(
                kn[:, 32:48], kn[:, 16:32], 1e-12, None, op0=ALU.max), waits=[("A", akn)])
            kb.emit("vector", lambda e: e.reciprocal(kn[:, 48:64], kn[:, 32:48]))
            kb.emit("vector", lambda e: e.tensor_tensor(
                knk.rearrange("p (h j) -> p h j", h=16),
                ktmp.rearrange("p (h j) -> p h j", h=16),
                kn[:, 48:64].unsqueeze(-1).broadcast_to((128, 16, 64)), op=ALU.mult), inc="V")
            fr["ktmp"] = fr["ktmp"] + [("V", kb.cnt["V"])]
            kb.emit("vector", lambda e, tb=tb: e.tensor_tensor(
                pack[:, tb * 4096:tb * 4096 + 1024], knk[:], emF[:], op=ALU.mult),
                waits=[("A", aexp)], inc="V")
            fr["emF"] = [("V", kb.cnt["V"])]
            kb.emit("vector", lambda e: e.scalar_tensor_tensor(
                eagF.rearrange("p (h j) -> p h j", h=16),
                eagF.rearrange("p (h j) -> p h j", h=16), -1.0,
                gtmp[:, 0:16].unsqueeze(-1).broadcast_to((128, 16, 64)),
                op0=ALU.mult, op1=ALU.mult))
            fr["gtmp"] = [("V", kb.cnt["V"])]
            kb.emit("vector", lambda e, tb=tb: e.tensor_tensor(
                pack[:, tb * 4096 + 1024:tb * 4096 + 2048], knk[:], eagF[:], op=ALU.mult), inc="V")
            fr["eagF"] = [("V", kb.cnt["V"])]
            fr["knk"] = [("V", kb.cnt["V"])]
            kb.emit("vector", lambda e, tb=tb: e.tensor_copy(
                vtm2[:, tb * 2048:(tb + 1) * 2048].rearrange("p (q c) -> p q c", c=64)[:, 0::4, :],
                pack[:, tb * 4096 + 1024:tb * 4096 + 2048].rearrange(
                    "p (a b) -> p a b", b=64)[:, 0::2, :]), inc="V")
            kb.emit("vector", lambda e, tb=tb: e.tensor_copy(
                vtm2[:, tb * 2048:(tb + 1) * 2048].rearrange("p (q c) -> p q c", c=64)[:, 3::4, :],
                pack[:, tb * 4096 + 1024:tb * 4096 + 2048].rearrange(
                    "p (a b) -> p a b", b=64)[:, 1::2, :]), inc="V")
            kb.emit("vector", lambda e, tb=tb: e.tensor_tensor(
                pack[:, tb * 4096 + 2 * 1024:tb * 4096 + 3 * 1024], qtmp[:],
                pack[:, tb * 4096 + 3 * 1024:tb * 4096 + 4 * 1024], op=ALU.mult), inc="V")
            fr["qtmp"] = [("V", kb.cnt["V"])]
            vpack = kb.cnt["V"]
            apack = kb.cnt["A"]
            if P2_TSEC < 5:
                return None
            kb.emit("sync", lambda e, tb=tb: e.dma_start_transpose(
                fmp[:, tb * 4096:(tb + 1) * 4096].rearrange("p (m r) -> p m r", r=128),
                pack[:, tb * 4096:(tb + 1) * 4096]),
                waits=[("V", vpack), ("A", apack)] + W(fr["fmpc"][tb]), inc="DT")
            return kb.cnt["DT"]

        def waves(tk4, dtf, snap):
            if dtf is None:
                return
            tk = s * TPS + tk4
            tb = tk % 2
            kb.wait("tensor", [("DT", dtf)])
            hl = list(range(16)) if P2_HEADS is None else list(P2_HEADS)
            pairs = [tuple(hl[i:i + 2]) for i in range(0, len(hl), 2)]
            for l in (0, 1):
                frl[l]["psS"] = frl[l]["psS"] + snap[0]
                frl[l]["psO"] = frl[l]["psO"] + snap[1]
            for pi, pair in enumerate(pairs):
                lanes = [(h % 2, h) for h in pair]
                if pi >= 1:
                    l1_tick()
                if pi >= 4:
                    l1_tick()
                    l1_tick()
                for l, h in lanes:
                    fl = frl[l]; B = LN[l]["bank"]
                    jr = (h % 2) * 64; jc = h // 2
                    kt0 = tb * 4096 + jc * 128
                    at0 = tb * 4096 + 1024 + jc * 128
                    for cpar in range(2):
                        r0 = cpar * 64
                        kb.emit("tensor", lambda e, jr=jr, jc=jc, kt0=kt0, r0=r0, tb=tb, B=B: e.matmul(
                            B[r0:r0 + 64, 0:128],
                            fmp[jr:jr + 64, kt0 + r0:kt0 + r0 + 64],
                            fmp4[jr:jr + 64, tb, 1:3, jc, r0:r0 + 64],
                            start=True, stop=True, tile_position=(jr, r0), skip_group_check=True),
                            waits=(W(fl["bA"], fl["psS"], fl["psO"]) if cpar == 0 else []))
                        kb.emit("tensor", lambda e, jr=jr, kt0=kt0, at0=at0, r0=r0, B=B: e.matmul(
                            B[r0:r0 + 64, 512:576],
                            fmp[jr:jr + 64, at0 + r0:at0 + r0 + 64],
                            fmp[jr:jr + 64, kt0 + r0:kt0 + r0 + 64],
                            start=True, stop=True, tile_position=(jr, r0), skip_group_check=True),
                            inc=("T" if cpar == 1 else None))
                    fl["tg"] = kb.cnt["T"]
                if pi >= 1:
                    l1_tick()
                if pi >= 4:
                    l1_tick()
                    l1_tick()
                for l, h in lanes:
                    fl = frl[l]; B = LN[l]["bank"]
                    kb.emit("vector", lambda e, B=B, ct=LN[l]["ct"]: e.tensor_tensor(
                        ct[:], B[:, 0:128], maskct[:], op=ALU.mult),
                        waits=[("T", fl["tg"])], inc="V")
                    kb.emit("vector", lambda e, B=B, cu=LN[l]["cu"]: e.tensor_tensor(
                        cu[:], B[:, 512:576], maskc[:], op=ALU.mult), inc="V")
                    fl["vmask"] = kb.cnt["V"]
                    fl["pow"] = [("V", fl["vmask"])]
                for l, h in lanes:
                    fl = frl[l]; xo = LN[l]["xoff"]
                    for cpar in range(2):
                        r0 = cpar * 64
                        kb.emit("tensor", lambda e, h=h, r0=r0, xo=xo, tb=tb: e.matmul(
                            px2[r0:r0 + 64, xo:xo + 128], identb[r0:r0 + 64, r0:r0 + 64],
                            vtm2[r0:r0 + 64, tb * 2048 + h * 128:tb * 2048 + h * 128 + 128],
                            start=True, stop=True, tile_position=(r0, r0), skip_group_check=True),
                            waits=(W(fl["bX"]) if cpar == 0 else []))
                        kb.emit("tensor", lambda e, h=h, r0=r0, xo=xo, tb=tb, ct=LN[l]["ct"]: e.matmul(
                            px2[r0:r0 + 64, xo:xo + 128], ct[r0:r0 + 64, 0:64],
                            vtm2[r0:r0 + 64, tb * 2048 + h * 128:tb * 2048 + h * 128 + 128],
                            start=False, stop=True, tile_position=(r0, r0), skip_group_check=True),
                            waits=([("V", fl["vmask"])] if cpar == 0 else []),
                            inc=("T" if cpar == 1 else None))
                    fl["tx"] = kb.cnt["T"]
                for l, h in lanes:
                    fl = frl[l]; xo = LN[l]["xoff"]
                    kb.emit("scalar", lambda e, xo=xo, xwl=LN[l]["xw"]: e.copy(
                        xwl[:, 128:256], px2[:, xo:xo + 128]),
                        waits=[("T", fl["tx"])], inc="A")
                    fl["axw"] = kb.cnt["A"]
                for j in range(1, 6):
                    if pi >= 1:
                        l1_tick()
                    for l, h in lanes:
                        fl = frl[l]; B = LN[l]["bank"]; pw = LN[l]["pw"]
                        for cpar in range(2):
                            r0 = cpar * 64
                            if j == 1:
                                ctp = LN[l]["ct"][r0:r0 + 64, 0:64]
                                cpp = LN[l]["cu"][r0:r0 + 64, 0:64]
                            else:
                                pbq = ((j - 1) % 2) * 128
                                ctp = pw[r0:r0 + 64, pbq:pbq + 64]
                                cpp = pw[r0:r0 + 64, pbq + 64:pbq + 128]
                            kb.emit("tensor", lambda e, ctp=ctp, cpp=cpp, r0=r0, B=B: e.matmul(
                                B[r0:r0 + 64, 0:64], cpp, ctp,
                                start=True, stop=True, tile_position=(r0, r0), skip_group_check=True),
                                waits=(W(fl["pow"]) if cpar == 0 else []))
                            kb.emit("tensor", lambda e, ctp=ctp, cpp=cpp, r0=r0, B=B: e.matmul(
                                B[r0:r0 + 64, 512:576], ctp, cpp,
                                start=True, stop=True, tile_position=(r0, r0), skip_group_check=True),
                                inc=("T" if cpar == 1 else None))
                        fl["tsq"] = kb.cnt["T"]
                    for l, h in lanes:
                        fl = frl[l]; B4 = LN[l]["bank"]; pw = LN[l]["pw"]
                        pb2 = (j % 2) * 128
                        kb.emit("vector", lambda e, pb2=pb2, B4=B4, pw=pw, tsq=fl["tsq"]: e.tensor_copy(
                            pw[:, pb2:pb2 + 128].rearrange("p (a b) -> p a b", b=64),
                            B4.rearrange("p (a b) -> p a b", b=64)[:, 0::8, :]),
                            waits=[("T", fl["tsq"])], inc="V")
                        fl["vpow"] = kb.cnt["V"]
                        fl["pow"] = [("V", fl["vpow"])]
                    if pi >= 1:
                        l1_tick()
                    for l, h in lanes:
                        fl = frl[l]; pw = LN[l]["pw"]; xwl = LN[l]["xw"]; xo = LN[l]["xoff"]
                        pb2 = (j % 2) * 128
                        for cpar in range(2):
                            r0 = cpar * 64
                            lh = pw[r0:r0 + 64, pb2:pb2 + 64]
                            rh = xwl[r0:r0 + 64, (j % 2) * 128:(j % 2) * 128 + 128]
                            kb.emit("tensor", lambda e, lh=lh, rh=rh, r0=r0, xo=xo: e.matmul(
                                px2[r0:r0 + 64, xo:xo + 128], lh, rh,
                                start=False, stop=True, tile_position=(r0, r0), skip_group_check=True),
                                waits=([("V", fl["vpow"]), ("A", fl["axw"])] if cpar == 0 else []),
                                inc=("T" if cpar == 1 else None))
                        fl["taj"] = kb.cnt["T"]
                    for l, h in lanes:
                        fl = frl[l]; xwl = LN[l]["xw"]; xo = LN[l]["xoff"]
                        kb.emit("scalar", lambda e, j=j, xo=xo, xwl=xwl, t=fl["taj"]: e.copy(
                            xwl[:, ((j + 1) % 2) * 128:((j + 1) % 2) * 128 + 128],
                            px2[:, xo:xo + 128]),
                            waits=[("T", fl["taj"])], inc="A")
                        fl["axw"] = kb.cnt["A"]
                for cpar in range(2):
                    r0 = cpar * 64
                    if pi >= 1:
                        l1_tick()
                    for l, h in lanes:
                        fl = frl[l]; B = LN[l]["bank"]; xwl = LN[l]["xw"]
                        jr = (h % 2) * 64; jc = h // 2
                        jo = jr ^ 64
                        wa_off = 0 if h % 2 == 0 else 64
                        wv_off = 64 if h % 2 == 0 else 0
                        wv = xwl[r0:r0 + 64, wv_off:wv_off + 64]
                        wa = xwl[r0:r0 + 64, wa_off:wa_off + 64]
                        ktm = pack[r0:r0 + 64, tb * 4096 + h * 64:tb * 4096 + h * 64 + 64]
                        ptc = LN[l]["ct"][r0:r0 + 64, 64:128]
                        w0 = (([("V", fl["vpow"]), ("A", fl["axw"])] if cpar == 0
                               else W(fl["psS"], fl["psO"])) + W(fl["sumev"]))
                        kb.emit("tensor", lambda e, r0=r0, jr=jr, wv=wv, ptc=ptc, B=B: e.matmul(
                            B[jr:jr + 64, 0:64], wv, ptc,
                            start=True, stop=True, tile_position=(r0, jr), skip_group_check=True),
                            waits=w0)
                        kb.emit("tensor", lambda e, r0=r0, jr=jr, wa=wa, ptc=ptc, B=B: e.matmul(
                            B[jr:jr + 64, 512:576], wa, ptc,
                            start=True, stop=True, tile_position=(r0, jr), skip_group_check=True))
                        kb.emit("tensor", lambda e, r0=r0, jo=jo, ktm=ktm, wv=wv, B=B: e.matmul(
                            B[jo:jo + 64, 0:64], ktm, wv,
                            start=True, stop=True, tile_position=(r0, jo), skip_group_check=True))
                        kb.emit("tensor", lambda e, r0=r0, jo=jo, ktm=ktm, wa=wa, B=B: e.matmul(
                            B[jo:jo + 64, 512:576], wa, ktm,
                            start=True, stop=True, tile_position=(r0, jo), skip_group_check=True),
                            inc="T")
                        ts1 = kb.cnt["T"]
                        kb.emit("scalar", lambda e, jr=jr, ts1=ts1, B=B: e.copy(
                            sumb[jr:jr + 64, 0:64], B[jr:jr + 64, 0:64]),
                            waits=[("T", ts1)])
                        kb.emit("scalar", lambda e, jr=jr, B=B: e.copy(
                            sumb[jr:jr + 64, 64:128], B[jr:jr + 64, 512:576]))
                        kb.emit("scalar", lambda e, jr=jr, jo=jo, B=B: e.copy(
                            sumb[jr:jr + 64, 128:192], B[jo:jo + 64, 0:64]))
                        kb.emit("scalar", lambda e, jr=jr, jo=jo, B=B: e.copy(
                            sumb[jr:jr + 64, 192:256], B[jo:jo + 64, 512:576]), inc="A")
                        as2 = kb.cnt["A"]
                        fl["sumev"] = [("A", as2)]
                        fl["bA"] = [("A", as2)]
                    if pi >= 1:
                        l1_tick()
                    for l, h in lanes:
                        fl = frl[l]; B = LN[l]["bank"]
                        jr = (h % 2) * 64; jc = h // 2
                        qbT = fmp[jr:jr + 64, tb * 4096 + 2 * 1024 + jc * 128 + r0:
                                  tb * 4096 + 2 * 1024 + jc * 128 + r0 + 64]
                        kb.emit("tensor", lambda e, jr=jr, jc=jc, B=B: e.matmul(
                            B[jr:jr + 64, 0:64], identb[jr:jr + 64, jr:jr + 64],
                            states[jr:jr + 64, jc * 64:jc * 64 + 64],
                            start=True, stop=False, tile_position=(jr, jr), skip_group_check=True),
                            waits=W(fl["sumev"], fl["psS"], fl["chain"]))
                        kb.emit("tensor", lambda e, jr=jr, jc=jc, B=B: e.matmul(
                            B[jr:jr + 64, 0:64], sumb[jr:jr + 64, 192:256],
                            states[jr:jr + 64, jc * 64:jc * 64 + 64],
                            start=False, stop=False, tile_position=(jr, jr), skip_group_check=True))
                        kb.emit("tensor", lambda e, jr=jr, B=B: e.matmul(
                            B[jr:jr + 64, 0:64], identb[jr:jr + 64, jr:jr + 64],
                            sumb[jr:jr + 64, 128:192],
                            start=False, stop=True, tile_position=(jr, jr), skip_group_check=True))
                        kb.emit("tensor", lambda e, jr=jr, jc=jc, B=B: e.matmul(
                            B[jr:jr + 64, 512:576], states[jr:jr + 64, jc * 64:jc * 64 + 64],
                            sumb[jr:jr + 64, 64:128],
                            start=True, stop=False, tile_position=(jr, jr), skip_group_check=True),
                            waits=W(fl["psO"]))
                        kb.emit("tensor", lambda e, jr=jr, jc=jc, qbT=qbT, B=B: e.matmul(
                            B[jr:jr + 64, 512:576], states[jr:jr + 64, jc * 64:jc * 64 + 64],
                            qbT,
                            start=False, stop=False, tile_position=(jr, jr), skip_group_check=True))
                        kb.emit("tensor", lambda e, jr=jr, B=B: e.matmul(
                            B[jr:jr + 64, 512:576], identb[jr:jr + 64, jr:jr + 64],
                            sumb[jr:jr + 64, 0:64],
                            start=False, stop=True, tile_position=(jr, jr), skip_group_check=True),
                            inc="T")
                        fl["tch"] = kb.cnt["T"]
                    for l, h in lanes:
                        fl = frl[l]; B = LN[l]["bank"]
                        jr = (h % 2) * 64; jc = h // 2
                        exoff = tb * 4096 + 3 * 1024 + jc * 128 + r0 + 63
                        kb.emit("vector", lambda e, jr=jr, jc=jc, exoff=exoff, B=B: e.tensor_tensor(
                            states[jr:jr + 64, jc * 64:jc * 64 + 64], B[jr:jr + 64, 0:64],
                            fmp[jr:jr + 64, exoff:exoff + 1].broadcast_to((64, 64)),
                            op=ALU.mult), waits=[("T", fl["tch"])], inc="V")
                        fl["psS"] = [("V", kb.cnt["V"])]
                        ocol = jc * 512 + tk4 * 128 + r0
                        kb.emit("vector", lambda e, jr=jr, ocol=ocol, B=B: e.tensor_tensor(
                            osg[jr:jr + 64, ocol:ocol + 64], B[jr:jr + 64, 512:576],
                            ogp[jr:jr + 64, ocol:ocol + 64],
                            op=ALU.mult), waits=W(fr["osg"]), inc="V")
                        fl["psO"] = [("V", kb.cnt["V"])]
                        fl["chain"] = [("V", kb.cnt["V"])]
                for l, h in lanes:
                    frl[l]["bX"] = [("A", frl[l]["axw"])]
            fr["og"] = [("V", kb.cnt["V"])]
            fr["psG"] = [("V", kb.cnt["V"])]
            fr["psH"] = [("V", kb.cnt["V"])]
            fr["vtm"][tb] = [("T", kb.cnt["T"])]
            fr["pack"][tb] = [("T", kb.cnt["T"]), ("DT", kb.cnt["DT"])]
            fr["fmpc"][tb] = [("T", kb.cnt["T"]), ("V", kb.cnt["V"]), ("G", kb.cnt["G"])]
            fr["gkb"] = [("T", kb.cnt["T"])]

        dtn = {0: prologue(0)}
        snap = (list(fr["psG"]), list(fr["psH"]))
        for tk4 in range(TPS):
            if tk4 + 1 < TPS:
                dtn[tk4 + 1] = prologue(tk4 + 1)
            waves(tk4, dtn[tk4], snap)
            snap = (list(fr["psG"]), list(fr["psH"]))
        vlast = kb.cnt["V"]
        for jc in range(8):
            kb.emit("scalar", lambda e, jc=jc, s0=s0: e.dma_start(
                out=io["o_dram"][jc, :, s0:s0 + 512], in_=osg[:, jc * 512:(jc + 1) * 512]),
                waits=([("V", vlast)] if jc == 0 else []), inc="DS")
        fr["osg"] = [("DS", kb.cnt["DS"])]
    while L1IL and l1_q["u"] < 256:
        l1_tick()
    env["p2_done"] = True


def phase_hidden(kb, io, in_name, out_name, wt_name, sc_name, pstk):
    tag = wt_name
    L = lambda n, sh, dt: kb.sb(n + tag, sh, dt, stack=pstk)
    hblk = L("lh_hblk", [128, 32 * 2048], BF16)
    wbb = L("lh_wbb", [128, 2 * 4096], BF16)
    gout = L("lh_gout", [128, 2 * 512], BF16)
    scc = L("lh_sc", [128, 32], F32)
    psb = [kb.ps("lh_ps%d%s" % (i, tag), stack=pstk) for i in range(2)]
    kb.emit("scalar", lambda e: e.dma_start(out=scc[:], in_=io[sc_name][:]), inc="DS")
    scc_ds = kb.cnt["DS"]

    steps = [(TT, oc) for TT in range(2) for oc in range(32)]
    if LH_LIM is not None:
        steps = steps[:LH_LIM]
    ldq = {}
    st_tail = {0: 0, 1: 0}
    evA_tail = {0: 0, 1: 0}
    mm_tail = {0: 0, 1: 0}
    hld = {}

    def load_for(i):
        if i >= len(steps) or i in ldq:
            return
        TT, oc = steps[i]
        b = i % 2
        sm = "DG" if b == 0 else "DH"
        v = kb.emit("gpsimd", lambda e, oc=oc, b=b: e.dma_start(
            out=wbb[:, b * 4096:(b + 1) * 4096], in_=io[wt_name][oc]),
            waits=[("T", mm_tail[b])], inc=sm)
        ldq[i] = (sm, v)

    load_for(0)
    for i, (TT, oc) in enumerate(steps):
        b = i % 2
        if oc == 0:
            tguard = kb.cnt["T"]
            for dc in range(32):
                kb.emit("scalar", lambda e, dc=dc, TT=TT: e.dma_start(
                    out=hblk[:, dc * 2048:(dc + 1) * 2048],
                    in_=io[in_name][dc, :, TT * 2048:TT * 2048 + 2048]),
                    waits=([("T", tguard)] if dc == 0 else []), inc="DS")
            hld[TT] = kb.cnt["DS"]
        load_for(i + 1)
        for ts in range(4):
            pb = ts % 2
            w0 = ([ldq[i], ("DS", hld[TT])] if ts == 0 else []) + [("A", evA_tail[pb])]
            for dc in range(32):
                kb.emit("tensor", lambda e, dc=dc, b=b, pb=pb, ts=ts, TT=TT, first=(dc == 0): e.matmul(
                    psb[pb][:, :], wbb[:, b * 4096 + dc * 128:b * 4096 + dc * 128 + 128],
                    hblk[:, dc * 2048 + ts * 512:dc * 2048 + ts * 512 + 512],
                    start=first, stop=(dc == 31)),
                    waits=w0 if dc == 0 else [], inc=("T" if dc == 31 else None))
            tmm = kb.cnt["T"]
            kb.emit("scalar", lambda e, pb=pb, oc=oc, tmm=tmm: e.activation(
                gout[:, pb * 512:(pb + 1) * 512], psb[pb][:, :], AF.Gelu,
                scale=scc[:, oc:oc + 1]),
                waits=[("T", tmm), ("DS", st_tail[pb])]
                + ([("DS", scc_ds)] if i == 0 and ts == 0 else []), inc="A")
            evA_tail[pb] = kb.cnt["A"]
            kb.emit("scalar", lambda e, pb=pb, oc=oc, t0=TT * 2048 + ts * 512: e.dma_start(
                out=io[out_name][oc, :, t0:t0 + 512], in_=gout[:, pb * 512:(pb + 1) * 512]),
                waits=[("A", evA_tail[pb])], inc="DS")
            st_tail[pb] = kb.cnt["DS"]
        mm_tail[b] = kb.cnt["T"]


def phase_final(kb, io, pstk):
    L = lambda n, sh, dt: kb.sb(n, sh, dt, stack=pstk)
    w3b = L("f_w3b", [128, 8 * 4096], BF16)
    wob = L("f_wob", [128, 8 * 1024], BF16)
    g2b = L("f_g2b", [128, 2 * 32 * 512], BF16)
    olod = L("f_olod", [128, 8 * 512], BF16)
    olnb = L("f_oln", [128, 8 * 512], BF16)
    osq = L("f_osq", [128, 512], BF16)
    ysb = L("f_ysb", [128, 2 * 512], F32)
    xfb = L("f_xfb", [128, 2 * 512], F32)
    stt = L("f_stats", [2, 6 * 512], F32)
    s3c = L("f_s3", [128, 8], F32)
    ones = kb.tens["c_ones"]
    onesf = kb.tens["c_onesf"]
    nwc = kb.tens["c_nw"]
    pFs = [kb.ps("f_pF%d" % i, stack=pstk) for i in range(2)]
    pWs = [kb.ps("f_pW%d" % i, stack=pstk) for i in range(2)]
    pS = kb.ps("f_pS", (2, 512), stack=pstk)
    pS2 = kb.ps("f_pS2", (2, 512), stack=pstk)
    pR = kb.ps("f_pR", stack=pstk)
    pM = kb.ps("f_pM", stack=pstk)

    kb.emit("scalar", lambda e: e.dma_start(out=s3c[:], in_=io["s3"][:]), inc="DS")
    for oc in range(8):
        kb.emit("gpsimd", lambda e, oc=oc: e.dma_start(
            out=w3b[:, oc * 4096:(oc + 1) * 4096], in_=io["w3t"][oc]), inc="DG")
        kb.emit("gpsimd", lambda e, oc=oc: e.dma_start(
            out=wob[:, oc * 1024:(oc + 1) * 1024], in_=io["wot"][oc]), inc="DG")
    cast_done = kb.cnt["DG"]
    kb.wait("tensor", [("DG", cast_done)])

    MU, S2N, VAR, SQ, RSTD, NMU = (stt[0:1, i * 512:(i + 1) * 512] for i in range(6))
    ev_prev = 0
    yst_tail = {0: 0, 1: 0}
    aw_tail = {0: 0, 1: 0}
    g2_rd = {0: 0, 1: 0}
    for tt in range(8):
        t0 = tt * 512
        gb = (tt % 2)
        for jc in range(8):
            kb.emit("scalar", lambda e, jc=jc, t0=t0: e.dma_start(
                out=olod[:, jc * 512:(jc + 1) * 512], in_=io["o_dram"][jc, :, t0:t0 + 512]),
                waits=([("V", ev_prev)] if jc == 0 else []), inc="DS")
        old = kb.cnt["DS"]
        gsm = "DH" if gb == 0 else "DG"
        for dc in range(32):
            w = [("T", g2_rd[gb])] if dc == 0 else []
            if dc == 0 and gsm == "DG" and tt == 1:
                w = w + [("DG", cast_done)]
            kb.emit("gpsimd", lambda e, dc=dc, t0=t0, gb=gb: e.dma_start(
                out=g2b[:, gb * 16384 + dc * 512:gb * 16384 + (dc + 1) * 512],
                in_=io["h_dram"][dc, :, t0:t0 + 512]),
                waits=w, inc=gsm)
        g2ld = (gsm, kb.cnt[gsm])
        sq_guard = 0
        for jc in range(8):
            kb.emit("scalar", lambda e, jc=jc: e.square(
                osq[:], olod[:, jc * 512:(jc + 1) * 512]),
                waits=([("DS", old)] if jc == 0 else []) + [("T", sq_guard)], inc="A")
            aq = kb.cnt["A"]
            kb.emit("tensor", lambda e, jc=jc, first=(jc == 0): e.matmul(
                pS[0:1, :], ones[:, 0:1], olod[:, jc * 512:(jc + 1) * 512],
                start=first, stop=(jc == 7)), waits=[("DS", old)] if jc == 0 else [])
            kb.emit("tensor", lambda e, jc=jc, first=(jc == 0), aq=aq: e.matmul(
                pS2[0:1, :], ones[:, 0:1], osq[:],
                start=first, stop=(jc == 7)), waits=[("A", aq)], inc="T")
            sq_guard = kb.cnt["T"]
        tsum = kb.cnt["T"]
        kb.emit("vector", lambda e: e.tensor_scalar(MU, pS[0:1, :], 1.0 / D, None, op0=ALU.mult),
                waits=[("T", tsum)], chain=True)
        kb.emit("vector", lambda e: e.tensor_scalar(S2N, pS2[0:1, :], 1.0 / D, None, op0=ALU.mult),
                chain=True)
        kb.emit("vector", lambda e: e.tensor_tensor(VAR, MU, MU, op=ALU.mult), chain=True)
        kb.emit("vector", lambda e: e.tensor_tensor(VAR, S2N, VAR, op=ALU.subtract), chain=True)
        vvar = kb.cnt["V"]
        kb.emit("scalar", lambda e: e.activation(SQ, VAR, AF.Sqrt, bias=kb.tens["c_actc"][0:1, 0:1], scale=1.0),
                waits=[("V", vvar)], inc="A")
        asq2 = kb.cnt["A"]
        kb.emit("vector", lambda e: e.reciprocal(RSTD, SQ), waits=[("A", asq2)], chain=True)
        kb.emit("vector", lambda e: e.tensor_scalar(NMU, MU, -1.0, None, op0=ALU.mult), chain=True)
        vst = kb.cnt["V"]
        kb.emit("tensor", lambda e: e.matmul(pR[:, :], onesf[0:1, :], RSTD, start=True, stop=True),
                waits=[("V", vst), ("V", ev_prev)])
        kb.emit("tensor", lambda e: e.matmul(pM[:, :], onesf[0:1, :], NMU, start=True, stop=True),
                inc="T")
        trep = kb.cnt["T"]
        for jc in range(8):
            kb.emit("vector", lambda e, jc=jc: e.tensor_tensor(
                olnb[:, jc * 512:(jc + 1) * 512], olod[:, jc * 512:(jc + 1) * 512],
                pM[:, :], op=ALU.add), waits=[("T", trep)] if jc == 0 else [], chain=True)
            kb.emit("vector", lambda e, jc=jc: e.scalar_tensor_tensor(
                olnb[:, jc * 512:(jc + 1) * 512], olnb[:, jc * 512:(jc + 1) * 512],
                nwc[:, jc:jc + 1], pR[:, :], op0=ALU.mult, op1=ALU.mult), chain=True)
        ev_prev = kb.cnt["V"]
        voln = kb.cnt["V"]
        for oc in range(8):
            ob = oc % 2
            kb.emit("scalar", lambda e, oc=oc, t0=t0, ob=ob: e.dma_start(
                out=xfb[:, ob * 512:(ob + 1) * 512], in_=io["xT"][oc * 128:(oc + 1) * 128, t0:t0 + 512]),
                waits=[("V", yst_tail[ob])], inc="DS")
            xld = kb.cnt["DS"]
            for dc in range(32):
                kb.emit("tensor", lambda e, dc=dc, oc=oc, gb=gb, ob=ob, first=(dc == 0): e.matmul(
                    pFs[ob][:, :], w3b[:, oc * 4096 + dc * 128:oc * 4096 + dc * 128 + 128],
                    g2b[:, gb * 16384 + dc * 512:gb * 16384 + dc * 512 + 512],
                    start=first, stop=(dc == 31)),
                    waits=[g2ld, ("V", yst_tail[ob]), ("A", aw_tail[ob])] if dc == 0 else [])
            for jc in range(8):
                kb.emit("tensor", lambda e, jc=jc, oc=oc, ob=ob, first=(jc == 0): e.matmul(
                    pWs[ob][:, :], wob[:, oc * 1024 + jc * 128:oc * 1024 + jc * 128 + 128],
                    olnb[:, jc * 512:(jc + 1) * 512],
                    start=first, stop=(jc == 7)),
                    waits=[("V", voln)] if jc == 0 else [], inc=("T" if jc == 7 else None))
            tmm = kb.cnt["T"]
            kb.emit("scalar", lambda e, oc=oc, ob=ob: e.copy(
                ysb[:, ob * 512:(ob + 1) * 512], pWs[ob][:, :]),
                waits=[("T", tmm), ("DS", xld)], inc="A")
            ayw = kb.cnt["A"]
            aw_tail[ob] = ayw
            kb.emit("vector", lambda e, oc=oc, ob=ob: e.scalar_tensor_tensor(
                ysb[:, ob * 512:(ob + 1) * 512], pFs[ob][:, :], s3c[:, oc:oc + 1],
                ysb[:, ob * 512:(ob + 1) * 512],
                op0=ALU.mult, op1=ALU.add), waits=[("A", ayw), ("DS", xld)], chain=True)
            kb.emit("vector", lambda e, oc=oc, ob=ob: e.tensor_tensor(
                ysb[:, ob * 512:(ob + 1) * 512], ysb[:, ob * 512:(ob + 1) * 512],
                xfb[:, ob * 512:(ob + 1) * 512], op=ALU.add), chain=True)
            yv = kb.cnt["V"]
            yst_tail[ob] = yv
            kb.emit("scalar", lambda e, oc=oc, t0=t0, ob=ob, yv=yv: e.dma_start(
                out=io["y"][oc, :, t0:t0 + 512], in_=ysb[:, ob * 512:(ob + 1) * 512]),
                waits=[("V", yv)], inc="DS")
        g2_rd[gb] = kb.cnt["T"]


def build_program(phases=None, expose=(), p2_nseg=NSEG):
    if phases is None:
        phases = (("p0", "p2", "l2", "l3", "fin") if L1IL
                  else ("p0", "l1", "p2", "l2", "l3", "fin"))
    nc = bass.Bass("TRN2", target_bir_lowering=False)
    io = declare_io(nc, expose)
    stack = ExitStack()
    with stack:
        kb = KB(nc, stack)
        env = None
        xn_stack = ExitStack()
        with ExitStack() as ps0:
            env = phase0_xn(kb, io, ps0, xn_stack=xn_stack)
        if "xn" in expose:
            v = kb.cnt["V"]
            kb.emit("scalar", lambda e: e.dma_start(out=io["xn_dump"][:], in_=kb.tens["xn_fm"][:]),
                    waits=[("V", v)], inc="DS")
        kb.fence()
        if "l1" in phases:
            with ExitStack() as ps:
                phase_l1(kb, io, env, ps)
            kb.fence()
        if "p2" in phases:
            with ExitStack() as ps:
                phase_p2(kb, io, env, ps, nseg=p2_nseg)
            kb.fence()
        xn_stack.close()
        if "l2" in phases:
            with ExitStack() as ps:
                phase_hidden(kb, io, "h_dram", "g_dram", "h1t", "sh1", ps)
            kb.fence()
        if "l3" in phases:
            with ExitStack() as ps:
                phase_hidden(kb, io, "g_dram", "h_dram", "h2t", "sh2", ps)
            kb.fence()
        if "fin" in phases:
            with ExitStack() as ps:
                phase_final(kb, io, ps)
        kb.wait("tensor", [("DS", kb.cnt["DS"]), ("DT", kb.cnt["DT"]), ("DG", kb.cnt["DG"]),
                           ("V", kb.cnt["V"]), ("A", kb.cnt["A"])])
        kb.finalize()
    return nc


def kernel(**inputs):
    cm, xs = host_prepare(inputs)
    nc = build_program()
    in_maps = []
    for b in range(8):
        m = dict(cm)
        m["xT"] = xs[b]
        in_maps.append(m)
    res = run_bass_kernel_spmd(nc, in_maps, core_ids=list(range(8)))
    outs = []
    for b in range(8):
        yfm = np.asarray(res.results[b]["y"], dtype=np.float32)
        outs.append(yfm.reshape(1024, 4096).T)
    return np.ascontiguousarray(np.stack(outs)).astype(np.float32)
